# version 11
# baseline (speedup 1.0000x reference)
import numpy as np
import ml_dtypes
import concourse.bass as bass
import concourse.mybir as mybir
from concourse.bass_utils import run_bass_kernel_spmd

F32 = mybir.dt.float32
BF16 = mybir.dt.bfloat16
I32 = mybir.dt.int32
AL = mybir.AluOpType
AF = mybir.ActivationFunctionType
AX = mybir.AxisListType.X

P = 128
T = 1024
NT = 8
D = 2048
NKD = 16
DI = 4096
H2 = 8192
SB = 4096
NKC = 32
RS128 = float(1.0 / np.sqrt(128.0))
C_UP = float(1.0 / (127.0 * np.sqrt(2048.0)))
C_DN = float(1.0 / (127.0 * np.sqrt(4096.0)))
EPS_U = float(1e-5 * np.sqrt(2048.0))
EPS_Z = float(1e-5 * np.sqrt(4096.0))
TW_NUMEL = [D * DI, DI * D, D * H2, H2 * D]

DEBUG_STOP = None


class St:
    def __init__(self, nc, name):
        self.sem = nc.alloc_semaphore(name)
        self.n = 0


class Pl:

    def __init__(self, mk, n, name):
        self.sts = [mk("%s_%d" % (name, i)) for i in range(n)]
        self.i = 0

    def nxt(self):
        s = self.sts[self.i % len(self.sts)]
        self.i += 1
        return s

    def mark(self):
        return [(s, s.n) for s in self.sts]


def _flat(waits):
    out = []
    for w in waits:
        if isinstance(w, list):
            out.extend(w)
        else:
            out.append(w)
    return out


class Em:

    def __init__(self, nc):
        self.nc = nc
        self.ops = []

    def o(self, eng, fn=None, waits=(), inc=None, k=1, kind="c"):
        post = None
        waits = _flat(list(waits))
        if inc is not None:
            if kind == "d":
                waits.append((inc, inc.n))
            inc.n += k
            post = inc.n
        self.ops.append((eng, fn, tuple(waits), inc, k, kind, post))
        return post

    def emit(self):
        nc = self.nc
        with nc.Block() as blk:
            def mk(name):
                def f(eng):
                    water = {}
                    prev_c = False
                    for (e, fn, waits, inc, k, kind, post) in self.ops:
                        if e != name:
                            continue
                        for (stm, v) in waits:
                            if v > 0 and v > water.get(stm.sem.num, 0):
                                eng.wait_ge(stm.sem, v)
                                water[stm.sem.num] = v
                        if fn is None:
                            ins = (eng.drain() if name in
                                   ("vector", "scalar", "gpsimd")
                                   else eng.nop())
                        else:
                            if kind == "c" and prev_c and name in (
                                    "vector", "scalar", "gpsimd"):
                                eng.drain()
                            ins = fn(eng)
                        prev_c = (kind == "c" and fn is not None)
                        if inc is not None:
                            ins.then_inc(inc.sem, k)
                            if (kind != "d" and name != "gpsimd"
                                    and post > water.get(inc.sem.num, 0)):
                                water[inc.sem.num] = post
                    return None
                return f
            blk.sync(mk("sync"))
            blk.gpsimd(mk("gpsimd"))
            blk.scalar(mk("scalar"))
            blk.vector(mk("vector"))
            blk.tensor(mk("tensor"))


def _bk(ps, b, n=1):
    return ps[:, 512 * b:512 * (b + n)]


def build_program(stop=None):
    nc = bass.Bass("TRN2", target_bir_lowering=False)
    DT = nc.dram_tensor
    SA = nc.alloc_sbuf_tensor

    x_in = DT("x_in", [T, D], F32, kind="ExternalInput")
    wq_d = DT("wq_d", [P, NKD, D], F32, kind="ExternalInput")
    wk_d = DT("wk_d", [P, NKD, D], F32, kind="ExternalInput")
    wv_d = DT("wv_d", [P, NKD, D], F32, kind="ExternalInput")
    wq1_d = DT("wq1_d", [P, NKD, D], F32, kind="ExternalInput")
    woT = DT("woT", [D, D], F32, kind="ExternalInput")
    wkvT = DT("wkvT", [D, 2 * D], F32, kind="ExternalInput")
    wprojT = DT("wprojT", [D, D], F32, kind="ExternalInput")
    h1024 = DT("h1024", [1024, 1024], BF16, kind="ExternalInput")
    bq_in = DT("bq_in", [P, NKD], F32, kind="ExternalInput")
    bk_in = DT("bk_in", [P, NKD], F32, kind="ExternalInput")
    bv_in = DT("bv_in", [P, NKD], F32, kind="ExternalInput")
    bo_row = DT("bo_row", [1, D], F32, kind="ExternalInput")
    bpj_row = DT("bpj_row", [1, D], F32, kind="ExternalInput")
    shup = DT("shup", [D // 8, DI], F32, kind="ExternalInput")
    shdn = DT("shdn", [DI // 8, D], F32, kind="ExternalInput")
    shcf = DT("shcf", [D // 8, H2], F32, kind="ExternalInput")
    shcp = DT("shcp", [H2 // 8, D], F32, kind="ExternalInput")
    y_out = DT("y_out", [T, D], F32, kind="ExternalOutput")

    ws_in = DT("ws_in", [1, 4], F32)
    ws_out = DT("ws_out", [1, 4], F32)
    qup_in = DT("qup_in", [D // 8, DI], BF16)
    qdn_in = DT("qdn_in", [DI // 8, D], BF16)
    qcf_in = DT("qcf_in", [D // 8, H2], BF16)
    qcp_in = DT("qcp_in", [H2 // 8, D], BF16)
    qup = DT("qup", [D, DI], BF16)
    qdn = DT("qdn", [DI, D], BF16)
    qcf = DT("qcf", [D, H2], BF16)
    qcp = DT("qcp", [H2, D], BF16)
    k_in = DT("k_in", [D, T], BF16)
    v_in = DT("v_in", [D, T], BF16)
    n_in = DT("n_in", [1, NKD], BF16)
    k_all = DT("k_all", [4 * D, T], BF16)
    v_all = DT("v_all", [4 * D, T], BF16)
    n_all = DT("n_all", [4, NKD], BF16)
    vk_in = DT("vk_in", [32, 2112], F32)
    vk_out = DT("vk_out", [32, 2112], F32)
    wsum_d = DT("wsum_d", [P, 4], F32)
    xq2_d = DT("xq2_d", [DI, T], BF16)
    khat_d = DT("khat_d", [H2, T], BF16)

    x_sb = SA("x_sb", [P, NT * D], F32)
    big_a = SA("big_a", [P, 16384], BF16)
    big_b = SA("big_b", [P, 16384], BF16)
    wbuf = SA("wbuf", [P, 16384], BF16)
    wt_sb = SA("wt_sb", [P, 4096], BF16)
    lnt = SA("lnt", [P, 2 * D], BF16)
    bc_sb = SA("bc_sb", [P, D], BF16)
    psb = SA("psb", [P, 2 * T], BF16)
    ktile = SA("ktile", [P, 2 * P], BF16)
    vsb = SA("vsb", [P, 288], BF16)
    vaug = SA("vaug", [P, 2 * 2112], BF16)
    st6 = SA("st6", [P, 24], F32)
    agg = SA("agg", [P, 2], F32)
    c_a = SA("c_a", [P, 4], F32)
    qnm = SA("qnm", [1, NKD], F32)
    knm = SA("knm", [1, NKD], F32)
    nrow = SA("nrow", [1, 4 * NKD], BF16)
    n_bf = SA("n_bf", [1, NKD], BF16)
    ksq = SA("ksq", [P, 512], BF16)
    nmx = SA("nmx", [1, 2], F32)
    biasneg = SA("biasneg", [P, NKD], F32)
    bq_sb = SA("bq_sb", [P, NKD], F32)
    bk_sb = SA("bk_sb", [P, NKD], F32)
    bv_sb = SA("bv_sb", [P, NKD], F32)
    wacc = SA("wacc", [P, 4], F32)
    wtmp = SA("wtmp", [P, 4], F32)
    sred = SA("sred", [1, 512], F32)
    wsrow = SA("wsrow", [1, 8], F32)
    negrow = SA("negrow", [1, NKD], F32)
    negrow_bf = SA("negrow_bf", [1, NKD], BF16)
    sc_bc = SA("sc_bc", [P, 4], F32)
    thr_bc = SA("thr_bc", [P, 4], F32)
    nthr_bc = SA("nthr_bc", [P, 4], F32)
    ones_c = SA("ones_c", [P, 1], BF16)
    ones_r = SA("ones_r", [1, P], BF16)
    ones_r32 = SA("ones_r32", [1, P], F32)
    dnv = SA("dnv", [P, 4], F32)
    cfv = SA("cfv", [P, NT], F32)
    cfv2 = SA("cfv2", [P, NT], F32)
    rcd = SA("rcd", [P, 4], F32)
    mmx = SA("mmx", [P, 4], F32)
    vkacc = SA("vkacc", [32, 2112], F32)

    ps = nc.alloc_psum_tensor("ps", [P, 4096], F32)

    def at(base, off_bytes, name, shape, dt):
        a = nc.lookup_mloc(base).addr
        return nc.alloc_sbuf_tensor_at(name, shape, dt, offset=a + off_bytes)

    qtA = at(wbuf, 0, "qtA", [P, 2048], F32)
    qtB = at(wbuf, 8192, "qtB", [P, 2048], F32)
    tq1 = at(psb, 0, "tq1", [P, 2048], BF16)
    tq2 = at(bc_sb, 0, "tq2", [P, 2048], BF16)
    ksl = at(vaug, 0, "ksl", [P, 2 * 512], BF16)
    fmt = at(big_b, 2 * 4096, "fmt", [P, 8192], BF16)
    zq = at(big_b, 2 * 12288, "zq", [P, 4096], BF16)
    qi32 = at(wt_sb, 0, "qi32", [P, 2048], I32)
    y32 = at(vaug, 0, "y32", [P, 2048], F32)
    vkT = at(vaug, 0, "vkT", [P, 2112], BF16)
    ktm2 = at(wt_sb, 0, "ktm2", [P, 4096], BF16)
    rD0 = at(big_b, 0, "rD0", [P, 4096], BF16)
    rD1 = at(big_a, 2 * 12288, "rD1", [P, 4096], BF16)

    def rbufD(tp):
        return rD0 if tp == 0 else rD1

    em = Em(nc)
    o = em.o
    stD = {}

    def st(name):
        stD[name] = St(nc, name)
        return stD[name]

    TP = Pl(st, 5, "tp")
    KL = Pl(st, 4, "kl")
    VL = Pl(st, 4, "vl")
    KS = Pl(st, 4, "ks")

    def tpb(dst, do_, dbs, src, so_, sbs, nblk, waits, grp=8):
        dw, sw = dst.shape[1], src.shape[1]
        for g0 in range(0, nblk, grp):
            g = min(grp, nblk - g0)

            def f(e, g0=g0, g=g):
                ia = bass.AP(src, so_ + sbs * g0,
                             [[sw, P], [sbs, g], [1, 128]])
                oa = bass.AP(dst, do_ + dbs * g0,
                             [[dw, P], [dbs, g], [1, 128]])
                return e.dma_start_transpose(out=oa, in_=ia)
            o("sync", f, waits=waits, inc=TP.nxt(), k=16, kind="d")

    s_x = st("s_x")
    s_bs = st("s_bs")
    s_ini = st("s_ini")
    s_q0 = st("s_q0")
    s_qv = st("s_qv")
    s_c0 = st("s_c0")

    for tt in range(NT):
        o("sync", lambda e, tt=tt: e.dma_start(
            out=x_sb[:, D * tt:D * (tt + 1)],
            in_=x_in[P * tt:P * (tt + 1), :]), inc=s_x, k=16, kind="d")
    XF = s_x.n
    o("sync", lambda e: e.dma_start(out=bq_sb[:, :], in_=bq_in[:, :]),
      inc=s_bs, k=16, kind="d")
    o("sync", lambda e: e.dma_start(out=bk_sb[:, :], in_=bk_in[:, :]),
      inc=s_bs, k=16, kind="d")
    o("sync", lambda e: e.dma_start(out=bv_sb[:, :], in_=bv_in[:, :]),
      inc=s_bs, k=16, kind="d")
    BSF = s_bs.n
    o("gpsimd", lambda e: e.memset(ones_c[:, :], 1.0))
    o("gpsimd", lambda e: e.memset(ones_r[:, :], 1.0))
    o("gpsimd", lambda e: e.memset(vsb[:, 128:129], 1.0))
    o("gpsimd", lambda e: e.memset(vsb[:, 272:273], 1.0))
    INI = o("gpsimd", lambda e: e.memset(ones_r32[:, :], 1.0), inc=s_ini)
    v0 = o("vector", lambda e: e.memset(wacc[:, :], 0.0), inc=s_qv)
    VK0 = o("vector", lambda e: e.memset(vkacc[:, :], 0.0), inc=s_qv)

    qjobs = []
    for r in range(2):
        for cs in range(2):
            qjobs.append((shup, 128 * r, 2048 * cs, 2048, qup_in, 0))
    for r in range(4):
        qjobs.append((shdn, 128 * r, 0, 2048, qdn_in, 1))
    for r in range(2):
        for cs in range(4):
            qjobs.append((shcf, 128 * r, 2048 * cs, 2048, qcf_in, 2))
    for r in range(8):
        qjobs.append((shcp, 128 * r, 0, 2048, qcp_in, 3))
    NQJ = len(qjobs)

    QL = Pl(st, 2, "ql")
    qv_done = []
    for j, (dr, r0, c0, w, _, wi) in enumerate(qjobs):
        buf = qtA if j % 2 == 0 else qtB
        wl = [] if j < 2 else [(s_qv, qv_done[j - 2])]
        stm = QL.nxt()
        ld = (stm, o("sync", lambda e, dr=dr, r0=r0, c0=c0, w=w, buf=buf:
                     e.dma_start(out=buf[:, 0:w],
                                 in_=dr[r0:r0 + P, c0:c0 + w]),
                     waits=wl, inc=stm, k=16, kind="d"))
        o("vector", lambda e, buf=buf, w=w, wi=wi: e.reduce_sum(
            out=wtmp[:, wi:wi + 1], in_=buf[:, 0:w], axis=AX,
            apply_absolute_value=True), waits=[ld])
        qv_done.append(o("vector", lambda e, wi=wi: e.tensor_tensor(
            out=wacc[:, wi:wi + 1], in0=wacc[:, wi:wi + 1],
            in1=wtmp[:, wi:wi + 1], op=AL.add), inc=s_qv))

    c1 = o("gpsimd", lambda e: e.dma_start(out=wsum_d[:, :], in_=wacc[:, :]),
           waits=[(s_qv, qv_done[-1])], inc=s_c0, k=16, kind="d")
    c2 = o("gpsimd", lambda e: e.dma_start(
        out=sred[:, :], in_=bass.AP(wsum_d, 0, [[512, 1], [1, 512]])),
        waits=[(s_c0, c1)], inc=s_c0, k=16, kind="d")
    for wi in range(4):
        o("vector", lambda e, wi=wi: e.reduce_sum(
            out=wsrow[:, wi:wi + 1],
            in_=bass.AP(sred, wi, [[512, 1], [4, 128]]), axis=AX),
          waits=[(s_c0, c2)])
    s_c0v = st("s_c0v")
    s_ar0 = st("s_ar0")
    c3 = o("vector", None, inc=s_c0v)
    c4 = o("gpsimd", lambda e: e.dma_start(out=ws_in[:, :],
                                           in_=wsrow[:, 0:4]),
           waits=[(s_c0v, c3)], inc=s_c0, k=16, kind="d")
    c5 = o("gpsimd", lambda e: e.collective_compute(
        "AllReduce", AL.add, replica_groups=[list(range(8))],
        ins=[ws_in.ap().opt()], outs=[ws_out.ap().opt()]),
        waits=[(s_c0, c4)], inc=s_ar0, kind="d")
    c6 = o("gpsimd", lambda e: e.dma_start(out=wsrow[:, 4:8],
                                           in_=ws_out[:, :]),
           waits=[(s_ar0, c5)], inc=s_c0, k=16, kind="d")
    for wi in range(4):
        o("vector", lambda e, wi=wi: e.tensor_scalar(
            out=wsrow[:, 4 + wi:5 + wi], in0=wsrow[:, 4 + wi:5 + wi],
            scalar1=1.0 / TW_NUMEL[wi], scalar2=1e-5, op0=AL.mult,
            op1=AL.max), waits=[(s_c0, c6)])
    c7 = o("vector", None, inc=s_c0v)
    c8 = o("gpsimd", lambda e: e.dma_start(out=ws_in[:, :],
                                           in_=wsrow[0:1, 4:8]),
           waits=[(s_c0v, c7)], inc=s_c0, k=16, kind="d")
    c9 = o("gpsimd", lambda e: e.dma_start(
        out=sc_bc[:, :], in_=bass.AP(ws_in, 0, [[0, P], [1, 4]])),
        waits=[(s_c0, c8)], inc=s_c0, k=16, kind="d")

    def fscl(e):
        e.tensor_scalar(out=thr_bc[:, :], in0=sc_bc[:, :],
                        scalar1=0.5, scalar2=None, op0=AL.mult)
        e.drain()
        return e.tensor_scalar(out=nthr_bc[:, :], in0=thr_bc[:, :],
                               scalar1=-1.0, scalar2=None, op0=AL.mult)
    SCL = o("vector", fscl, waits=[(s_c0, c9)], inc=s_c0v)

    s_q2 = st("s_q2")
    s_qs = st("s_qs")
    s_qw = st("s_qw")
    s_qst = st("s_qst")
    s_p0 = st("s_p0")
    s_pdn = st("s_pdn")
    s_pcf = st("s_pcf")
    s_pcp = st("s_pcp")

    qst_done = []
    qw_done = []
    for j, (dr, r0, c0, w, qd, wi) in enumerate(qjobs):
        buf = qtA if j % 2 == 0 else qtB
        wl = [(s_c0v, SCL)] if j < 2 else [(s_qw, qw_done[j - 2])]
        stm2 = QL.nxt()
        ld = (stm2, o("sync", lambda e, dr=dr, r0=r0, c0=c0, w=w, buf=buf:
                      e.dma_start(out=buf[:, 0:w],
                                  in_=dr[r0:r0 + P, c0:c0 + w]),
                      waits=wl, inc=stm2, k=16, kind="d"))

        def fq1(e, buf=buf, w=w, wi=wi):
            e.tensor_scalar(out=tq2[:, 0:w], in0=buf[:, 0:w],
                            scalar1=nthr_bc[:, wi:wi + 1], scalar2=None,
                            op0=AL.is_lt)
            e.drain()
            return e.tensor_scalar(out=tq1[:, 0:w], in0=buf[:, 0:w],
                                   scalar1=thr_bc[:, wi:wi + 1],
                                   scalar2=None, op0=AL.is_gt)
        wl_q = [ld]
        if j >= 1:
            wl_q.append((s_qst, qst_done[j - 1]))
        o("vector", fq1, waits=wl_q)
        qw_done.append(o("vector", lambda e, w=w: e.tensor_tensor(
            out=tq1[:, 0:w], in0=tq1[:, 0:w], in1=tq2[:, 0:w],
            op=AL.subtract), inc=s_qw))
        qst_done.append(o("sync", lambda e, qd=qd, r0=r0, c0=c0, w=w:
                          e.dma_start(out=qd[r0:r0 + P, c0:c0 + w],
                                      in_=tq1[:, 0:w]),
                          waits=[(s_qw, qw_done[-1])], inc=s_qst, k=16,
                          kind="d"))
    QST_ALL = qst_done[-1]

    s_lnc = st("s_lnc")

    def ln_chain(tt, slot, pre):
        s_l = s_lnc
        base = s_l.n
        xt = x_sb[:, D * tt:D * (tt + 1)]

        def f1(e):
            for c in range(4):
                e.bn_stats(out=st6[:, 6 * c:6 * (c + 1)],
                           in_=xt[:, 512 * c:512 * (c + 1)])
                e.drain()
            e.bn_aggr(out=agg[:, :], in_=st6[:, :])
            e.drain()
            return e.tensor_scalar(out=c_a[:, 1:2], in0=agg[:, 1:2],
                                   scalar1=1e-5, scalar2=None, op0=AL.add)
        o("vector", f1, waits=pre, inc=s_l)
        o("scalar", lambda e: e.activation(
            out=c_a[:, 1:2], in_=c_a[:, 1:2], func=AF.Sqrt),
          waits=[(s_l, base + 1)], inc=s_l)

        def f2(e):
            e.reciprocal(out=c_a[:, 2:3], in_=c_a[:, 1:2])
            e.drain()
            e.scalar_tensor_tensor(out=c_a[:, 3:4], in0=agg[:, 0:1],
                                   scalar=-1.0, in1=c_a[:, 2:3],
                                   op0=AL.mult, op1=AL.mult)
            e.drain()
            return e.tensor_scalar(
                out=lnt[:, D * slot:D * (slot + 1)], in0=xt,
                scalar1=c_a[:, 2:3], scalar2=c_a[:, 3:4],
                op0=AL.mult, op1=AL.add)
        o("vector", f2, waits=[(s_l, base + 2)], inc=s_l)
        return (s_l, base + 3)

    def wslice_load(dst_half, src, og_col, n_ki, colw, stream, waits,
                    row0=0, cast=True):
        ncols = src.shape[1]

        def f(e):
            ia = bass.AP(src, row0 * ncols + og_col,
                         [[ncols, P], [P * ncols, n_ki], [1, colw]])
            oa = bass.AP(wbuf, 8192 * dst_half,
                         [[16384, P], [colw, n_ki], [1, colw]])
            return e.dma_start(out=oa, in_=ia)
        return o("gpsimd" if cast else "sync", f, waits=waits, inc=stream,
                 k=16, kind="d")

    s_wA = st("s_wA")
    s_peA = st("s_peA")
    s_evA = st("s_evA")
    s_ccA = st("s_ccA")
    s_kamx = st("s_kamx")
    s_sqA = st("s_sqA")
    s_nmm = st("s_nmm")
    s_nmv = st("s_nmv")
    nmv_done = []

    lhsA_done = []
    LHA0 = None
    for tt in range(NT):
        pre = [(s_x, 16 * (tt + 1))]
        if tt >= 2:
            pre.append(lhsA_done[tt - 2])
        s_l, lv = ln_chain(tt, tt % 2, pre)
        tpb(big_a, 128 * tt, 1024, lnt, D * (tt % 2), 128, NKD,
            [(s_l, lv)])
        lhsA_done.append(TP.mark())
        if tt == 3:
            LHA0 = TP.mark()
    LHA = lhsA_done[-1]

    wproj_srcs = [wk_d, wv_d, wq_d]
    wA_done = []
    for pi in range(3):
        for oc in range(NKD):
            j = pi * NKD + oc
            wl = [] if j < 2 else [(s_peA, 2 * (j - 1))]
            wA_done.append(o(
                "gpsimd", lambda e, pi=pi, oc=oc, j=j: e.dma_start(
                    out=wt_sb[:, 2048 * (j % 2):2048 * (j % 2) + 2048],
                    in_=wproj_srcs[pi][:, oc, :]),
                waits=wl, inc=s_wA, k=16, kind="d"))
    peA_done = []
    evA_done = []
    kst_done = []
    nmv_done = []
    KSK = KSV = None
    for pi in range(3):
        bias = (bk_sb, bv_sb, bq_sb)[pi]
        for oc in range(NKD):
            j = pi * NKD + oc
            for g2 in range(2):
                bidx = 2 * (j % 2) + g2

                def fmm(e, j=j, g2=g2, bidx=bidx):
                    last = None
                    for ki in range(NKD):
                        last = e.matmul(
                            _bk(ps, bidx),
                            wt_sb[:, 2048 * (j % 2) + 128 * ki:
                                  2048 * (j % 2) + 128 * (ki + 1)],
                            big_a[:, 1024 * ki + 512 * g2:
                                  1024 * ki + 512 * (g2 + 1)],
                            start=(ki == 0), stop=(ki == NKD - 1))
                    return last
                wl = [LHA0 if g2 == 0 else LHA, (s_wA, wA_done[j]),
                      (s_bs, BSF)]
                if j >= 2:
                    wl.append((s_evA, evA_done[2 * (j - 2) + 1]))
                peA_done.append(o("tensor", fmm, waits=wl, inc=s_peA,
                                  kind="m"))
                if pi == 2:
                    dst = big_b[:, 1024 * oc + 512 * g2:
                                1024 * oc + 512 * (g2 + 1)]
                    evA_done.append(o(
                        "scalar", lambda e, dst=dst, bidx=bidx, oc=oc,
                        bias=bias: e.activation(
                            out=dst, in_=_bk(ps, bidx), func=AF.Identity,
                            bias=bias[:, oc:oc + 1], scale=1.0),
                        waits=[(s_peA, peA_done[-1])], inc=s_evA))
                else:
                    nksl = len(kst_done)
                    half = nksl % 2
                    dst = ksl[:, 512 * half:512 * (half + 1)]
                    wl2 = [(s_peA, peA_done[-1])]
                    if nksl >= 2:
                        wl2.append(kst_done[nksl - 2])
                    evA_done.append(o(
                        "scalar", lambda e, dst=dst, bidx=bidx, oc=oc,
                        bias=bias: e.activation(
                            out=dst, in_=_bk(ps, bidx), func=AF.Identity,
                            bias=bias[:, oc:oc + 1], scale=1.0),
                        waits=wl2, inc=s_evA))
                    dstd = k_in if pi == 0 else v_in
                    kst_done.append((lambda stm: (stm, o(
                        "sync", lambda e, dstd=dstd, oc=oc, g2=g2, dst=dst:
                        e.dma_start(out=dstd[128 * oc:128 * (oc + 1),
                                             512 * g2:512 * (g2 + 1)],
                                    in_=dst),
                        waits=[(s_evA, evA_done[-1])], inc=stm, k=16,
                        kind="d")))(KS.nxt()))
                if pi != 1:
                    nj = len(nmv_done)
                    wl3 = [(s_evA, evA_done[-1])]
                    if nj >= 1:
                        wl3.append((s_nmm, nj))
                    sqv = o("scalar", lambda e, bidx=bidx: e.activation(
                        out=ksq[:, :], in_=_bk(ps, bidx), func=AF.Square),
                        waits=wl3, inc=s_sqA)
                    nb = 4 + (nj % 2)
                    wlm = [(s_sqA, sqv)]
                    if nj >= 2:
                        wlm.append((s_nmv, nmv_done[nj - 2]))
                    mmv = o("tensor", lambda e, nb=nb: e.matmul(
                        ps[0:1, 512 * nb:512 * nb + 512], ones_c[:, :],
                        ksq[:, :], start=True, stop=True),
                        waits=wlm, inc=s_nmm, kind="m")

                    def fnm(e, nb=nb, oc=oc, pi=pi, first=(g2 == 0)):
                        dstn = knm if pi == 0 else qnm
                        e.reduce_max(out=nmx[:, 0:1],
                                     in_=ps[0:1, 512 * nb:512 * nb + 512],
                                     axis=AX)
                        e.drain()
                        if first:
                            return e.tensor_copy(out=dstn[:, oc:oc + 1],
                                                 in_=nmx[:, 0:1])
                        return e.tensor_tensor(out=dstn[:, oc:oc + 1],
                                               in0=dstn[:, oc:oc + 1],
                                               in1=nmx[:, 0:1], op=AL.max)
                    nmv_done.append(o("vector", fnm, waits=[(s_nmm, mmv)],
                                      inc=s_nmv))
        if pi == 0:
            kmx = o("vector", lambda e: e.tensor_scalar(
                out=n_bf[:, :], in0=knm[:, :], scalar1=1.05, scalar2=None,
                op0=AL.mult), waits=[(s_nmv, nmv_done[-1])], inc=s_kamx)
            _nst = (lambda stm: (stm, o(
                "sync", lambda e: e.dma_start(out=n_in[:, :],
                                              in_=n_bf[:, :]),
                waits=[(s_kamx, kmx)], inc=stm, k=16, kind="d")))(KS.nxt())
            KSK = KS.mark()
            grp4 = [[0, 1, 2, 3], [4, 5, 6, 7]]
            ccA1 = o("gpsimd", lambda e: e.collective_compute(
                "AllGather", AL.bypass, replica_groups=grp4,
                ins=[n_in.ap().opt()], outs=[n_all.ap().opt()]),
                waits=KSK, inc=s_ccA, kind="d")
            ccA2 = o("gpsimd", lambda e: e.collective_compute(
                "AllGather", AL.bypass, replica_groups=grp4,
                ins=[k_in.ap().opt()], outs=[k_all.ap().opt()]),
                waits=KSK, inc=s_ccA, kind="d")
        elif pi == 1:
            KSV = KS.mark()
            ccA3 = o("gpsimd", lambda e: e.collective_compute(
                "AllGather", AL.bypass, replica_groups=grp4,
                ins=[v_in.ap().opt()], outs=[v_all.ap().opt()]),
                waits=KSV, inc=s_ccA, kind="d")
            VAUG1 = o("gpsimd", lambda e: e.memset(
                bass.AP(vaug, 32, [[4224, P], [33, 128]]), 1.0),
                waits=KSV, inc=s_ini)

    EVAF = evA_done[-1]

    s_nr = st("s_nr")
    s_peN = st("s_peN")
    s_sc = st("s_sc")
    s_ex = st("s_ex")
    s_at = st("s_at")
    s_evB = st("s_evB")
    s_bo = st("s_bo")

    s_nrv = st("s_nrv")
    ld = o("gpsimd", lambda e: e.dma_start(
        out=nrow[:, :], in_=bass.AP(n_all, 0, [[64, 1], [1, 64]])),
        waits=[(s_ccA, ccA1)], inc=s_nr, k=16, kind="d")

    def fpr(e):
        e.tensor_copy(out=knm[:, :], in_=nrow[0:1, 0:NKD])
        e.drain()
        for r in range(1, 4):
            e.tensor_tensor(out=knm[:, :], in0=knm[:, :],
                            in1=nrow[0:1, NKD * r:NKD * (r + 1)],
                            op=AL.max)
            e.drain()
        return e.drain()
    pr1 = o("vector", fpr, waits=[(s_nr, ld)], inc=s_nrv)

    sq1 = o("scalar", lambda e: e.activation(
        out=qnm[:, :], in_=qnm[:, :], func=AF.Sqrt),
        waits=[(s_nrv, pr1), (s_nmv, nmv_done[-1])], inc=s_nrv)
    sq2 = o("scalar", lambda e: e.activation(
        out=knm[:, :], in_=knm[:, :], func=AF.Sqrt),
        waits=[(s_nrv, sq1)], inc=s_nrv)

    def fng(e):
        e.tensor_tensor(out=negrow[:, :], in0=qnm[:, :], in1=knm[:, :],
                        op=AL.mult)
        e.drain()
        e.tensor_scalar(out=negrow[:, :], in0=negrow[:, :],
                        scalar1=-1.05 * RS128, scalar2=None, op0=AL.mult)
        e.drain()
        return e.tensor_copy(out=negrow_bf[:, :], in_=negrow[:, :])
    ng1 = o("vector", fng, waits=[(s_nrv, sq2), (s_evA, EVAF),
                                  (s_ini, INI), (s_c0v, SCL)], inc=s_nrv)
    pn2 = o("tensor", lambda e: e.matmul(
        ps[:, 3584:3600], ones_r[:, :], negrow_bf[:, :], start=True,
        stop=True), waits=[(s_nrv, ng1)], inc=s_peN, kind="m")
    NRMF = o("vector", lambda e: e.tensor_copy(out=biasneg[:, :],
                                               in_=ps[:, 3584:3600]),
             waits=[(s_peN, pn2)], inc=s_nrv)

    s_bov = st("s_bov")
    b1 = o("gpsimd", lambda e: e.dma_start(
        out=bc_sb[:, :], in_=bass.AP(bo_row, 0, [[0, P], [1, D]])),
        waits=[(s_qst, qst_done[-1]), (s_qw, qw_done[-1])],
        inc=s_bo, k=16, kind="d")
    bo_done = []
    for tt in range(NT):
        bo_done.append(o("vector", lambda e, tt=tt: e.tensor_tensor(
            out=x_sb[:, D * tt:D * (tt + 1)],
            in0=x_sb[:, D * tt:D * (tt + 1)], in1=bc_sb[:, :], op=AL.add),
            waits=[(s_bo, b1), (s_evA, EVAF)], inc=s_bov))

    def at_slot(qc):
        b = 4 + qc // 3
        sl = qc % 3
        return ps[:, 512 * b + 129 * sl:512 * b + 129 * sl + 129]

    sc_done, ex_done, at_done, kld_done, vld_done = [], [], [], [], []
    evB_done = []
    pend_fat = [None]

    def emit_fat():
        if pend_fat[0] is not None:
            f, wl = pend_fat[0]
            at_done.append(o("tensor", f, waits=wl, inc=s_at, kind="m"))
            pend_fat[0] = None
    for h in range(16):
        for kc in range(NKC):
            i = h * NKC + kc
            wl = [(s_ccA, ccA2)]
            if i >= 2:
                wl.append((s_sc, sc_done[i - 2]))
            stm = KL.nxt()
            kld_done.append((stm, o(
                "sync", lambda e, h=h, kc=kc, i=i: e.dma_start(
                    out=ktile[:, 128 * (i % 2):128 * (i % 2) + 128],
                    in_=k_all[2048 * (kc // 8) + 128 * h:
                              2048 * (kc // 8) + 128 * (h + 1),
                              128 * (kc % 8):128 * (kc % 8) + 128]),
                waits=wl, inc=stm, k=16, kind="d")))
            wl = [(s_ccA, ccA3)]
            if i >= 2:
                wl.append((s_at, at_done[i - 2]))
            stm = VL.nxt()
            vld_done.append((stm, o(
                "sync", lambda e, h=h, kc=kc, i=i: e.dma_start_transpose(
                    out=vsb[:, 144 * (i % 2):144 * (i % 2) + 128],
                    in_=v_all[2048 * (kc // 8) + 128 * h:
                              2048 * (kc // 8) + 128 * (h + 1),
                              128 * (kc % 8):128 * (kc % 8) + 128]),
                waits=wl, inc=stm, k=16, kind="d")))

            def fsc(e, h=h, i=i):
                last = None
                for g2 in range(2):
                    last = e.matmul(
                        _bk(ps, 2 * (i % 2) + g2),
                        ktile[:, 128 * (i % 2):128 * (i % 2) + 128],
                        big_b[:, 1024 * h + 512 * g2:
                              1024 * h + 512 * (g2 + 1)],
                        start=True, stop=True)
                return last
            wl = [kld_done[i], (s_evA, EVAF)]
            if i >= 2:
                wl.append((s_ex, ex_done[i - 2]))
            sc_done.append(o("tensor", fsc, waits=wl, inc=s_sc, kind="m"))
            emit_fat()
            wl = [(s_sc, sc_done[i]), (s_nrv, NRMF)]
            if i == 0:
                wl.append((s_qw, qw_done[-1]))
                wl.append((s_qst, QST_ALL))
            if i >= 2:
                wl.append((s_at, at_done[i - 2]))
            ex_done.append(o("scalar", lambda e, i=i, h=h: e.activation(
                out=psb[:, 1024 * (i % 2):1024 * (i % 2) + 1024],
                in_=_bk(ps, 2 * (i % 2), 2), func=AF.Exp,
                bias=biasneg[:, h:h + 1], scale=RS128),
                waits=wl, inc=s_ex))

            def fat(e, kc=kc, i=i):
                last = None
                for qc in range(NT):
                    last = e.matmul(
                        at_slot(qc),
                        psb[:, 1024 * (i % 2) + 128 * qc:
                            1024 * (i % 2) + 128 * (qc + 1)],
                        vsb[:, 144 * (i % 2):144 * (i % 2) + 129],
                        start=(kc == 0 and qc % 3 == 0),
                        stop=(kc == NKC - 1 and (qc % 3 == 2 or qc == 7)))
                return last
            wl = [(s_ex, ex_done[i]), vld_done[i]]
            if kc == 0 and h > 0:
                wl.append((s_evB, evB_done[-1]))
            pend_fat[0] = (fat, wl)

        emit_fat()

        def fev(e, h=h):
            for qc in range(NT):
                e.reciprocal(out=rcd[:, 0:1], in_=at_slot(qc)[:, 128:129])
                e.drain()
                e.tensor_scalar(
                    out=big_a[:, 2048 * qc + 128 * h:
                              2048 * qc + 128 * (h + 1)],
                    in0=at_slot(qc)[:, 0:128], scalar1=rcd[:, 0:1],
                    scalar2=None, op0=AL.mult)
                e.drain()
            return e.drain()
        evB_done.append(o("vector", fev, waits=[(s_at, at_done[-1])],
                          inc=s_evB))
    EVBF = evB_done[-1]
    for (iin, iout, sem) in ((qup_in, qup, s_p0),):
        o("gpsimd", lambda e, iin=iin, iout=iout:
          e.collective_compute(
            "AllGather", AL.bypass, replica_groups=[list(range(8))],
            ins=[iin.ap().opt()], outs=[iout.ap().opt()]),
          waits=[(s_qst, QST_ALL), (s_at, at_done[int(len(at_done) * 0.7)])], inc=sem, kind="d")
    SCF = sc_done[-1]
    EXF = ex_done[-1]

    if stop == "B":
        _finish(nc, em, o, st, x_sb, y_out,
                [(s_evB, EVBF), (s_bov, bo_done[-1])],
                dumps=[("big_a", big_a), ("big_b", big_b),
                       ("biasneg", biasneg), ("qnm", qnm), ("knm", knm),
                       ("bc_sb", bc_sb)])
        em.emit()
        return nc

    s_wC = st("s_wC")
    s_peC = st("s_peC")
    s_evC = st("s_evC")

    for qc in range(NT):
        tpb(big_b, 128 * qc, 1024, big_a, 2048 * qc, 128, NKD,
            [(s_evB, EVBF), (s_sc, SCF)])
    LHC = TP.mark()
    wC_done, peC_done, evC_done = [], [], []
    for og in range(4):
        wl = [(s_qst, qst_done[-1])]
        if og >= 2:
            wl.append((s_peC, peC_done[8 * (og - 1) - 1]))
        wC_done.append(wslice_load(og % 2, woT, 512 * og, NKD, 512, s_wC,
                                   wl))
        for tt in range(NT):
            def fmm(e, og=og, tt=tt):
                last = None
                for ki in range(NKD):
                    last = e.matmul(
                        _bk(ps, tt % 2),
                        big_b[:, 1024 * ki + 128 * tt:
                              1024 * ki + 128 * (tt + 1)],
                        wbuf[:, 8192 * (og % 2) + 512 * ki:
                             8192 * (og % 2) + 512 * (ki + 1)],
                        start=(ki == 0), stop=(ki == NKD - 1))
                return last
            wl = [LHC, (s_wC, wC_done[og]), (s_ex, EXF)]
            n = len(peC_done)
            if n >= 2:
                wl.append((s_evC, evC_done[n - 2]))
            peC_done.append(o("tensor", fmm, waits=wl, inc=s_peC, kind="m"))
            evC_done.append(o(
                "vector", lambda e, og=og, tt=tt: e.scalar_tensor_tensor(
                    out=x_sb[:, D * tt + 512 * og:D * tt + 512 * (og + 1)],
                    in0=_bk(ps, tt % 2), scalar=1.0,
                    in1=x_sb[:, D * tt + 512 * og:
                             D * tt + 512 * (og + 1)],
                    op0=AL.mult, op1=AL.add),
                waits=[(s_peC, peC_done[-1]), (s_bov, bo_done[tt])],
                inc=s_evC))
    EVCF = evC_done[-1]
    PECF = peC_done[-1]

    if stop == "C":
        _finish(nc, em, o, st, x_sb, y_out, [(s_evC, EVCF)])
        em.emit()
        return nc

    for (iin, iout, sem) in ((qdn_in, qdn, s_pdn), (qcf_in, qcf, s_pcf), (qcp_in, qcp, s_pcp)):
        o("gpsimd", lambda e, iin=iin, iout=iout:
          e.collective_compute(
            "AllGather", AL.bypass, replica_groups=[list(range(8))],
            ins=[iin.ap().opt()], outs=[iout.ap().opt()]),
          waits=[(s_qst, QST_ALL)], inc=sem, kind="d")
    s_hld = st("s_hld")
    s_peD = st("s_peD")
    s_aqD = st("s_aqD")
    s_wD = st("s_wD")
    s_peU = st("s_peU")
    s_rD = st("s_rD")
    s_bfD = st("s_bfD")
    s_peH2 = st("s_peH2")
    s_zD = st("s_zD")
    s_aq2 = st("s_aq2")
    s_xqst = st("s_xqst")
    s_wDn = st("s_wDn")
    s_lhDn = st("s_lhDn")
    s_peDn = st("s_peDn")
    s_evD = st("s_evD")

    def fhl(e):
        ia = bass.AP(h1024, 0, [[1024, P], [P * 1024, 8], [1, 1024]])
        oa = bass.AP(big_a, 0, [[16384, P], [1024, 8], [1, 1024]])
        return e.dma_start(out=oa, in_=ia)
    HLD = o("sync", fhl, waits=LHC, inc=s_hld, k=16, kind="d")
    h_sb = big_a

    peD_done = []
    aqD_done = []
    x1T_done = []
    rD_done = []
    bfD_done = []
    zD_done = []
    aq2_done = []
    x2T_done = []
    xqst_done = []
    peU_done = []
    peH2_done = []

    for pr in range(4):
        pair_mark = TP.mark()
        for tp in range(2):
            tt = 2 * pr + tp
            slot = tt % 2
            pre = [(s_evC, EVCF), (s_peC, PECF)]
            if tt >= 2:
                pre += pair_mark
            s_l, lv = ln_chain(tt, slot, pre)

            def fbt(e, slot=slot):
                e.tensor_tensor(
                    out=fmt[:, 4096 * slot:4096 * slot + 1024],
                    in0=lnt[:, D * slot:D * slot + 1024],
                    in1=lnt[:, D * slot + 1024:D * slot + 2048], op=AL.add)
                e.drain()
                return e.tensor_tensor(
                    out=fmt[:, 4096 * slot + 1024:4096 * slot + 2048],
                    in0=lnt[:, D * slot:D * slot + 1024],
                    in1=lnt[:, D * slot + 1024:D * slot + 2048],
                    op=AL.subtract)
            wl = [(s_l, lv)]
            if tt >= 2:
                wl.append((s_peD, peD_done[tt - 2]))
                wl.append((s_xqst, xqst_done[tt - 2]))
            if tt >= 1 and pr >= 1:
                wl.append((s_peU, s_peU.n))
            bt = o("vector", fbt, waits=wl, inc=s_l)
            tpb(fmt, 4096 * slot + 2048, 128, fmt, 4096 * slot, 128, 16,
                [(s_l, bt)])
            lhsD_tile = TP.mark()
            last_pe = None
            for b in range(2):
                for og in range(2):
                    bidx = 4 * (tt % 2) + 2 * b + og

                    def fmm(e, slot=slot, b=b, og=og, bidx=bidx):
                        last = None
                        for ki in range(8):
                            last = e.matmul(
                                _bk(ps, bidx),
                                fmt[:, 4096 * slot + 2048 +
                                    128 * (8 * b + ki):
                                    4096 * slot + 2048 +
                                    128 * (8 * b + ki) + 128],
                                h_sb[:, 1024 * ki + 512 * og:
                                     1024 * ki + 512 * (og + 1)],
                                start=(ki == 0), stop=(ki == 7))
                        return last
                    wl = [lhsD_tile, (s_hld, HLD)]
                    if tt == 0:
                        wl += [(s_ex, EXF), (s_evB, EVBF), (s_nrv, NRMF)]
                    if tt >= 2:
                        wl.append((s_aqD, aqD_done[tt - 2]))
                    last_pe = o("tensor", fmm, waits=wl, inc=s_peD,
                                kind="m")
            peD_done.append(last_pe)
        for tp in range(2):
            tt = 2 * pr + tp

            def faq(e, tt=tt, bidx0=4 * (tt % 2)):
                sp = _bk(ps, bidx0, 4)
                e.reduce_max(out=dnv[:, 0:1], in_=sp, axis=AX,
                             apply_absolute_value=True)
                e.drain()
                e.tensor_scalar(out=dnv[:, 1:2], in0=dnv[:, 0:1],
                                scalar1=EPS_U, scalar2=None, op0=AL.max)
                e.drain()
                e.reciprocal(out=dnv[:, 2:3], in_=dnv[:, 1:2])
                e.drain()
                e.tensor_scalar(out=dnv[:, 2:3], in0=dnv[:, 2:3],
                                scalar1=127.0, scalar2=None, op0=AL.mult)
                e.drain()
                e.scalar_tensor_tensor(out=cfv[:, tt:tt + 1],
                                       in0=dnv[:, 1:2], scalar=C_UP,
                                       in1=sc_bc[:, 0:1], op0=AL.mult,
                                       op1=AL.mult)
                e.drain()
                e.tensor_scalar(out=qi32[:, 0:2048], in0=sp,
                                scalar1=dnv[:, 2:3], scalar2=None,
                                op0=AL.mult)
                e.drain()
                return e.tensor_scalar(
                    out=lnt[:, D * (tt % 2):D * (tt % 2) + 2048],
                    in0=qi32[:, 0:2048], scalar1=0.0, scalar2=None,
                    op0=AL.add)
            wl = [(s_peD, peD_done[tt]), (s_c0v, SCL)]
            if tt >= 2:
                wl.append(x1T_done[tt - 2])
            aqD_done.append(o("vector", faq, waits=wl, inc=s_aqD))
            tpb(big_a, 8192 + 128 * tp, 256, lnt, D * (tt % 2), 128, NKD,
                [(s_aqD, aqD_done[-1]), (s_hld, HLD)])
            x1T_done.append(TP.mark())
        for og in range(8):
            j = 8 * pr + og
            wl = [(s_p0, 1)]
            if j >= 2:
                wl.append((s_peU, peU_done[2 * (j - 2) + 1]))
            wD_j = wslice_load(og % 2, qup, 512 * og, NKD, 512, s_wD,
                               wl, cast=False)
            for tp in range(2):
                tt = 2 * pr + tp

                def fmm(e, og=og, tp=tp):
                    last = None
                    for ki in range(NKD):
                        last = e.matmul(
                            _bk(ps, 2 * (og % 2) + tp),
                            big_a[:, 8192 + 256 * ki + 128 * tp:
                                  8192 + 256 * ki + 128 * (tp + 1)],
                            wbuf[:, 8192 * (og % 2) + 512 * ki:
                                 8192 * (og % 2) + 512 * (ki + 1)],
                            start=(ki == 0), stop=(ki == NKD - 1))
                    return last
                wl = [x1T_done[-1], (s_wD, wD_j)]
                n = len(peU_done)
                if n >= 2:
                    wl.append((s_rD, rD_done[n - 2]))
                if pr >= 1 and og == 0 and tp == 0:
                    wl.append((s_bfD, s_bfD.n))
                peU_done.append(o("tensor", fmm, waits=wl, inc=s_peU,
                                  kind="m"))
                rD_done.append(o(
                    "scalar", lambda e, tp=tp, tt=tt, og=og:
                    e.activation(
                        out=rbufD(tp)[:, 512 * og:512 * (og + 1)],
                        in_=_bk(ps, 2 * (og % 2) + tp), func=AF.Relu,
                        scale=cfv[:, tt:tt + 1]),
                    waits=[(s_peU, peU_done[-1])], inc=s_rD))
        sq_v = {}
        lhsT_t = {}
        peH2_t = {}
        zD_t = {}

        def d3a(tp):
            tt = 2 * pr + tp
            slot = tt % 2
            sq_v[tp] = o("scalar", lambda e, tp=tp: e.activation(
                out=rbufD(tp)[:, :], in_=rbufD(tp)[:, :], func=AF.Square),
                waits=[(s_rD, rD_done[-1])], inc=s_rD)

            def fb2(e, tp=tp, slot=slot):
                rb = rbufD(tp)
                for (do, i0, i1, op2) in ((0, 0, 1024, AL.add),
                                          (1024, 0, 1024, AL.subtract),
                                          (2048, 2048, 3072, AL.add),
                                          (3072, 2048, 3072, AL.subtract)):
                    e.tensor_tensor(
                        out=fmt[:, 4096 * slot + do:
                                4096 * slot + do + 1024],
                        in0=rb[:, i0:i0 + 1024],
                        in1=rb[:, i1:i1 + 1024], op=op2)
                    e.drain()
                for (do, i0, i1, op2) in ((0, 0, 2048, AL.add),
                                          (1024, 1024, 3072, AL.add),
                                          (2048, 0, 2048, AL.subtract),
                                          (3072, 1024, 3072, AL.subtract)):
                    e.tensor_tensor(
                        out=rb[:, do:do + 1024],
                        in0=fmt[:, 4096 * slot + i0:
                                4096 * slot + i0 + 1024],
                        in1=fmt[:, 4096 * slot + i1:
                                4096 * slot + i1 + 1024], op=op2)
                    e.drain()
                return e.nop()
            bfD_done.append(o("vector", fb2,
                              waits=[(s_rD, sq_v[tp]),
                                     (s_peU, peU_done[-1]),
                                     (s_peD, peD_done[tt])], inc=s_bfD))
            tpb(fmt, 4096 * slot, 128, rbufD(tp), 0, 128, 32,
                [(s_bfD, bfD_done[-1])])
            lhsT_t[tp] = TP.mark()

        def d3b(tp):
            slot = (2 * pr + tp) % 2
            for j1 in range(4):
                bidx = 2 * (j1 % 2)

                def fmm(e, slot=slot, j1=j1, bidx=bidx):
                    last = None
                    for ki in range(8):
                        for og in range(2):
                            last = e.matmul(
                                _bk(ps, bidx + og),
                                fmt[:, 4096 * slot +
                                    128 * (8 * j1 + ki):
                                    4096 * slot +
                                    128 * (8 * j1 + ki) + 128],
                                h_sb[:, 1024 * ki + 512 * og:
                                     1024 * ki + 512 * (og + 1)],
                                start=(ki == 0), stop=(ki == 7))
                    return last
                wl = [lhsT_t[tp]]
                n = len(peH2_done)
                if n >= 2:
                    wl.append((s_zD, zD_done[n - 2]))
                peH2_done.append(o("tensor", fmm, waits=wl, inc=s_peH2,
                                   kind="m"))
                wl = [(s_peH2, peH2_done[-1])]
                if j1 == 0 and len(aq2_done) > 0:
                    wl.append((s_aq2, aq2_done[-1]))
                zD_done.append(o("scalar", lambda e, j1=j1, bidx=bidx:
                                 e.activation(
                                   out=zq[:, 1024 * j1:1024 * (j1 + 1)],
                                   in_=_bk(ps, bidx, 2), func=AF.Copy),
                                 waits=wl, inc=s_zD))
            peH2_t[tp] = peH2_done[-1]
            zD_t[tp] = zD_done[-1]

        def d3c(tp):
            tt = 2 * pr + tp
            slot = tt % 2

            def faq2(e, tt=tt, tp=tp):
                e.reduce_max(out=dnv[:, 0:1], in_=zq[:, :], axis=AX,
                             apply_absolute_value=True)
                e.drain()
                e.tensor_scalar(out=dnv[:, 1:2], in0=dnv[:, 0:1],
                                scalar1=EPS_Z, scalar2=None, op0=AL.max)
                e.drain()
                e.reciprocal(out=dnv[:, 2:3], in_=dnv[:, 1:2])
                e.drain()
                e.tensor_scalar(out=dnv[:, 2:3], in0=dnv[:, 2:3],
                                scalar1=127.0, scalar2=None, op0=AL.mult)
                e.drain()
                e.scalar_tensor_tensor(out=cfv2[:, tt:tt + 1],
                                       in0=dnv[:, 1:2], scalar=C_DN,
                                       in1=sc_bc[:, 1:2], op0=AL.mult,
                                       op1=AL.mult)
                e.drain()
                for c in range(2):
                    e.tensor_scalar(out=qi32[:, 0:2048],
                                    in0=zq[:, 2048 * c:2048 * (c + 1)],
                                    scalar1=dnv[:, 2:3], scalar2=None,
                                    op0=AL.mult)
                    e.drain()
                    e.tensor_scalar(
                        out=rbufD(tp)[:, 2048 * c:2048 * (c + 1)],
                        in0=qi32[:, 0:2048], scalar1=0.0, scalar2=None,
                        op0=AL.add)
                    e.drain()
                return e.drain()
            aq2_done.append(o("vector", faq2,
                              waits=[(s_zD, zD_t[tp])], inc=s_aq2))
            tpb(fmt, 4096 * slot, 128, rbufD(tp), 0, 128, 32,
                [(s_aq2, aq2_done[-1]), (s_peH2, peH2_t[tp])])
            x2T_done.append(TP.mark())

            def fst(e, slot=slot, tt=tt):
                ia = bass.AP(fmt, 4096 * slot,
                             [[8192, P], [128, 32], [1, 128]])
                oa = bass.AP(xq2_d, 128 * tt,
                             [[1024, P], [131072, 32], [1, 128]])
                return e.dma_start(out=oa, in_=ia)
            xqst_done.append(o("sync", fst, waits=x2T_done[-1],
                               inc=s_xqst, k=16, kind="d"))

        d3a(0)
        d3a(1)
        d3b(0)
        d3c(0)
        d3b(1)
        d3c(1)
    lhDn_done, wDn_done, peDn_done, evD_done = [], [], [], []
    for kq in range(2):
        def fld(e, kq=kq):
            ia = bass.AP(xq2_d, 2048 * kq * T,
                         [[1024, P], [131072, 16], [1, 1024]])
            oa = bass.AP(big_a, 0, [[16384, P], [1024, 16], [1, 1024]])
            return e.dma_start(out=oa, in_=ia)
        wl = [(s_xqst, xqst_done[-1]), (s_peH2, peH2_done[-1]),
              (s_peU, peU_done[-1]), (s_peD, peD_done[-1])]
        wl += x1T_done[-1]
        if kq == 1:
            wl.append((s_peDn, peDn_done[-1]))
        lhDn_done.append(o("sync", fld, waits=wl, inc=s_lhDn, k=16,
                           kind="d"))
        for og in range(4):
            j = 4 * kq + og
            wl = [(s_peC, PECF), (s_pdn, 1)]
            if j >= 2:
                wl.append((s_peDn, peDn_done[8 * (j - 1) - 1]))
            wDn_done.append(wslice_load(og % 2, qdn, 512 * og, NKD, 512,
                                        s_wDn, wl, row0=2048 * kq,
                                        cast=False))
            for tt in range(NT):
                def fmm(e, og=og, tt=tt):
                    last = None
                    for ki in range(NKD):
                        last = e.matmul(
                            _bk(ps, tt % 2),
                            big_a[:, 1024 * ki + 128 * tt:
                                  1024 * ki + 128 * (tt + 1)],
                            wbuf[:, 8192 * (og % 2) + 512 * ki:
                                 8192 * (og % 2) + 512 * (ki + 1)],
                            start=(ki == 0), stop=(ki == NKD - 1))
                    return last
                wl = [(s_lhDn, lhDn_done[kq]), (s_wDn, wDn_done[j]),
                      (s_aq2, aq2_done[-1])]
                n = len(peDn_done)
                if n >= 2:
                    wl.append((s_evD, evD_done[n - 2]))
                peDn_done.append(o("tensor", fmm, waits=wl, inc=s_peDn,
                                   kind="m"))
                evD_done.append(o(
                    "vector", lambda e, og=og, tt=tt:
                    e.scalar_tensor_tensor(
                        out=x_sb[:, D * tt + 512 * og:
                                 D * tt + 512 * (og + 1)],
                        in0=_bk(ps, tt % 2), scalar=cfv2[:, tt:tt + 1],
                        in1=x_sb[:, D * tt + 512 * og:
                                 D * tt + 512 * (og + 1)],
                        op0=AL.mult, op1=AL.add),
                    waits=[(s_peDn, peDn_done[-1])], inc=s_evD))
    EVDF = evD_done[-1]
    PEDNF = peDn_done[-1]

    if stop == "D":
        _finish(nc, em, o, st, x_sb, y_out, [(s_evD, EVDF)])
        em.emit()
        return nc

    s_wE = st("s_wE")
    s_peE = st("s_peE")
    s_evE = st("s_evE")
    s_vkE = st("s_vkE")
    s_ccE = st("s_ccE")
    s_wQ = st("s_wQ")
    s_peQ = st("s_peQ")
    s_evQ = st("s_evQ")
    s_peO = st("s_peO")
    s_evO = st("s_evO")
    s_wP = st("s_wP")
    s_peP = st("s_peP")
    s_evP = st("s_evP")
    s_bp = st("s_bp")

    lhsE_done = []
    for tt in range(NT):
        pre = [(s_evD, EVDF)]
        if tt >= 2:
            pre.append(lhsE_done[tt - 2])
        s_l, lv = ln_chain(tt, tt % 2, pre)
        tpb(big_a, 128 * tt, 1024, lnt, D * (tt % 2), 128, NKD,
            [(s_l, lv), (s_lhDn, lhDn_done[-1]), (s_peDn, PEDNF)])
        lhsE_done.append(TP.mark())
    LHE = lhsE_done[-1]

    wE_done, peE_done, evE_done, vkE_done, peO_done = [], [], [], [], []
    s_evEv = st("s_evEv")
    for pr in range(4):
        for p2 in range(2):
            for og in range(4):
                j = (pr * 2 + p2) * 4 + og
                wl = [(s_peDn, PEDNF)]
                if j >= 2:
                    wl.append((s_peE, peE_done[2 * (j - 2) + 1]))
                wE_done.append(wslice_load(
                    og % 2, wkvT, 2048 * p2 + 512 * og, NKD, 512, s_wE,
                    wl))
                for tp in range(2):
                    tt = 2 * pr + tp

                    def fmm(e, og=og, tt=tt):
                        last = None
                        for ki in range(NKD):
                            last = e.matmul(
                                _bk(ps, tt % 2),
                                big_a[:, 1024 * ki + 128 * tt:
                                      1024 * ki + 128 * (tt + 1)],
                                wbuf[:, 8192 * (og % 2) + 512 * ki:
                                     8192 * (og % 2) + 512 * (ki + 1)],
                                start=(ki == 0), stop=(ki == NKD - 1))
                        return last
                    wl = [LHE, (s_wE, wE_done[-1]),
                          (s_evD, EVDF)]
                    n = len(peE_done)
                    if n >= 2:
                        wl.append(evE_done[n - 2])
                    if pr >= 1 and p2 == 0 and og == 0:
                        wl.append((s_vkE, vkE_done[-1]))
                    peE_done.append(o("tensor", fmm, waits=wl, inc=s_peE,
                                      kind="m"))
                    if p2 == 0:
                        evE_done.append((s_evE, o(
                            "scalar", lambda e, tp=tp, og=og, tt=tt:
                            e.activation(
                                out=ktm2[:, 2048 * tp + 512 * og:
                                         2048 * tp + 512 * (og + 1)],
                                in_=_bk(ps, tt % 2), func=AF.Relu),
                            waits=[(s_peE, peE_done[-1])], inc=s_evE)))
                    else:
                        def fv(e, tp=tp, og=og, tt=tt):
                            oa = bass.AP(vaug, 2112 * tp + 528 * og,
                                         [[4224, P], [33, 16], [1, 32]])
                            return e.tensor_copy(out=oa,
                                                 in_=_bk(ps, tt % 2))
                        evE_done.append((s_evEv, o(
                            "vector", fv,
                            waits=[(s_peE, peE_done[-1]), (s_ini, VAUG1)],
                            inc=s_evEv)))
        for tp in range(2):
            for rnd in range(5):
                h0 = 15 * rnd
                nh = min(15, 64 - h0)
                bidx = len(peO_done) % 2

                def fvk(e, tp=tp, h0=h0, nh=nh, bidx=bidx):
                    last = None
                    for hl in range(nh):
                        h = h0 + hl
                        last = e.matmul(
                            ps[0:32, 512 * bidx + 33 * hl:
                               512 * bidx + 33 * hl + 33],
                            ktm2[:, 2048 * tp + 32 * h:
                                 2048 * tp + 32 * (h + 1)],
                            vaug[:, 2112 * tp + 33 * h:
                                 2112 * tp + 33 * (h + 1)],
                            start=(hl == 0), stop=(hl == nh - 1))
                    return last
                wl = [evE_done[-1]]
                n = len(peO_done)
                if n >= 2:
                    wl.append((s_vkE, vkE_done[n - 2]))
                peO_done.append(o("tensor", fvk, waits=wl, inc=s_peO,
                                  kind="m"))
                vkE_done.append(o(
                    "vector", lambda e, h0=h0, nh=nh, bidx=bidx:
                    e.tensor_tensor(
                        out=vkacc[0:32, 33 * h0:33 * (h0 + nh)],
                        in0=vkacc[0:32, 33 * h0:33 * (h0 + nh)],
                        in1=ps[0:32, 512 * bidx:512 * bidx + 33 * nh],
                        op=AL.add),
                    waits=[(s_peO, peO_done[-1]), (s_qv, VK0)],
                    inc=s_vkE))
    VKF = vkE_done[-1]

    s_arE = st("s_arE")
    e1 = o("gpsimd", lambda e: e.dma_start(out=vk_in[:, :],
                                           in_=vkacc[:, :]),
           waits=[(s_vkE, VKF)], inc=s_ccE, k=16, kind="d")
    e2 = o("gpsimd", lambda e: e.collective_compute(
        "AllReduce", AL.add, replica_groups=grp4, ins=[vk_in.ap().opt()],
        outs=[vk_out.ap().opt()]), waits=[(s_ccE, e1)], inc=s_arE,
        kind="d")
    ms = o("gpsimd", lambda e: e.memset(vkT[:, :], 0.0),
           waits=[(s_vkE, VKF), (s_arE, e2)], inc=s_arE)
    for j2 in range(4):
        e3 = o("gpsimd", lambda e, j2=j2: e.dma_start(
            out=vkT[32 * j2:32 * (j2 + 1), :].rearrange(
                "p (c e) -> p c e", e=132)[:, :, 33 * j2:33 * (j2 + 1)],
            in_=bass.AP(vk_out, 33 * j2, [[2112, 32], [132, 16], [1, 33]])),
            waits=[(s_arE, ms)], inc=s_ccE, k=16, kind="d")
    CCEF = s_ccE.n

    wQ_done, peQ_done, evQ_done = [], [], []
    for oc in range(NKD):
        wl = [(s_vkE, VKF)]
        if oc >= 2:
            wl.append((s_peQ, peQ_done[2 * (oc - 2) + 1]))
        wQ_done.append(o("gpsimd", lambda e, oc=oc: e.dma_start(
            out=wt_sb[:, 2048 * (oc % 2):2048 * (oc % 2) + 2048],
            in_=wq1_d[:, oc, :]), waits=wl, inc=s_wQ, k=16, kind="d"))
        for g2 in range(2):
            bidx = 4 + 2 * (oc % 2) + g2

            def fmm(e, oc=oc, g2=g2, bidx=bidx):
                last = None
                for ki in range(NKD):
                    last = e.matmul(
                        _bk(ps, bidx),
                        wt_sb[:, 2048 * (oc % 2) + 128 * ki:
                              2048 * (oc % 2) + 128 * (ki + 1)],
                        big_a[:, 1024 * ki + 512 * g2:
                              1024 * ki + 512 * (g2 + 1)],
                        start=(ki == 0), stop=(ki == NKD - 1))
                return last
            wl = [(s_wQ, wQ_done[oc]), (s_aqD, aq2_done[-1] if False
                                        else s_aqD.n)]
            if oc == 0 and g2 == 0:
                wl.append((s_aq2, aq2_done[-1]))
            if oc >= 2:
                wl.append((s_evQ, evQ_done[2 * (oc - 2) + 1]))
            peQ_done.append(o("tensor", fmm, waits=wl, inc=s_peQ,
                              kind="m"))
            evQ_done.append(o("scalar", lambda e, oc=oc, g2=g2, bidx=bidx:
                              e.activation(
                                out=big_b[:, 1024 * oc + 512 * g2:
                                          1024 * oc + 512 * (g2 + 1)],
                                in_=_bk(ps, bidx), func=AF.Relu),
                              waits=[(s_peQ, peQ_done[-1])], inc=s_evQ))
    EVQF = evQ_done[-1]

    peO2_done, evO_done, lhsP_done = [], [], []
    for tt in range(NT):
        for c in range(NKD):
            i2 = tt * NKD + c
            bidx = i2 % 2

            def fo(e, tt=tt, c=c, bidx=bidx):
                return e.matmul(
                    ps[:, 512 * bidx:512 * bidx + 132],
                    big_b[:, 1024 * c + 128 * tt:1024 * c + 128 * (tt + 1)],
                    vkT[:, 132 * c:132 * (c + 1)],
                    start=True, stop=True)
            wl = [(s_evQ, EVQF), (s_ccE, CCEF), (s_vkE, VKF)]
            if i2 >= 2:
                wl.append((s_evO, evO_done[i2 - 2]))
            peO2_done.append(o("tensor", fo, waits=wl, inc=s_peO,
                               kind="m"))

            def fdiv(e, tt=tt, c=c, bidx=bidx):
                e.tensor_scalar(
                    out=rcd[:, 0:4],
                    in0=bass.AP(ps, 512 * bidx + 32,
                                [[4096, P], [33, 4]]),
                    scalar1=1e-15, scalar2=None, op0=AL.add)
                e.drain()
                e.reciprocal(out=rcd[:, 0:4], in_=rcd[:, 0:4])
                e.drain()
                e.tensor_tensor(
                    out=lnt[:, D * (tt % 2) + 128 * c:
                            D * (tt % 2) + 128 * (c + 1)],
                    in0=bass.AP(ps, 512 * bidx,
                                [[4096, P], [33, 4], [1, 32]]),
                    in1=bass.AP(rcd, 0, [[4, P], [1, 4], [0, 32]]),
                    op=AL.mult)
                e.drain()
                return e.drain()
            wl = [(s_peO, peO2_done[-1])]
            if tt >= 2 and c == 0:
                wl.append(lhsP_done[tt - 2])
            evO_done.append(o("vector", fdiv, waits=wl, inc=s_evO))
        tpb(big_a, 128 * tt, 1024, lnt, D * (tt % 2), 128, NKD,
            [(s_evO, evO_done[-1]), (s_peQ, peQ_done[-1])] + LHE)
        lhsP_done.append(TP.mark())
    LHP = lhsP_done[-1]

    s_bpv = st("s_bpv")
    p1 = o("gpsimd", lambda e: e.dma_start(
        out=bc_sb[:, :], in_=bass.AP(bpj_row, 0, [[0, P], [1, D]])),
        waits=[(s_bov, bo_done[-1]), (s_evC, EVCF)], inc=s_bp, k=16,
        kind="d")
    bp_done = []
    for tt in range(NT):
        bp_done.append(o("vector", lambda e, tt=tt: e.tensor_tensor(
            out=x_sb[:, D * tt:D * (tt + 1)],
            in0=x_sb[:, D * tt:D * (tt + 1)], in1=bc_sb[:, :], op=AL.add),
            waits=[(s_bp, p1), (s_evD, EVDF)] + lhsE_done[tt],
            inc=s_bpv))

    wP_done, peP_done, evP_done = [], [], []
    for og in range(4):
        wl = [LHP]
        if og >= 2:
            wl.append((s_peP, peP_done[8 * (og - 1) - 1]))
        wP_done.append(wslice_load(og % 2, wprojT, 512 * og, NKD, 512,
                                   s_wP, wl))
        for tt in range(NT):
            def fmm(e, og=og, tt=tt):
                last = None
                for ki in range(NKD):
                    last = e.matmul(
                        _bk(ps, tt % 2),
                        big_a[:, 1024 * ki + 128 * tt:
                              1024 * ki + 128 * (tt + 1)],
                        wbuf[:, 8192 * (og % 2) + 512 * ki:
                             8192 * (og % 2) + 512 * (ki + 1)],
                        start=(ki == 0), stop=(ki == NKD - 1))
                return last
            wl = [(s_wP, wP_done[og]), (s_peO, peO2_done[-1])] + LHP
            n = len(peP_done)
            if n >= 2:
                wl.append((s_evP, evP_done[n - 2]))
            peP_done.append(o("tensor", fmm, waits=wl, inc=s_peP,
                              kind="m"))
            evP_done.append(o(
                "vector", lambda e, og=og, tt=tt: e.scalar_tensor_tensor(
                    out=x_sb[:, D * tt + 512 * og:
                             D * tt + 512 * (og + 1)],
                    in0=_bk(ps, tt % 2), scalar=1.0,
                    in1=x_sb[:, D * tt + 512 * og:
                             D * tt + 512 * (og + 1)],
                    op0=AL.mult, op1=AL.add),
                waits=[(s_peP, peP_done[-1]), (s_bpv, bp_done[tt])],
                inc=s_evP))
    EVPF = evP_done[-1]
    PEPF = peP_done[-1]

    if stop == "E":
        _finish(nc, em, o, st, x_sb, y_out, [(s_evP, EVPF)])
        em.emit()
        return nc

    s_wF = st("s_wF")
    s_peF = st("s_peF")
    s_rF = st("s_rF")
    s_qF = st("s_qF")
    s_sqF = st("s_sqF")
    s_khst = st("s_khst")
    s_lhG = st("s_lhG")
    s_wG = st("s_wG")
    s_peG = st("s_peG")
    s_evG = st("s_evG")

    rbufF = big_a
    lnF_done, wF_done, peF_done, rF_done = [], [], [], []
    qF_done, khst_done = [], []
    for pr in range(4):
        for tp in range(2):
            tt = 2 * pr + tp
            pre = [(s_evP, EVPF)]
            if tt >= 2:
                pre.append(lnF_done[tt - 2])
            s_l, lv = ln_chain(tt, tt % 2, pre)
            wl = [(s_l, lv), (s_peP, PEPF)]
            if pr >= 1:
                wl.append((s_peF, peF_done[-1]))
            tpb(big_b, 128 * tp, 256, lnt, D * (tt % 2), 128, NKD, wl)
            lnF_done.append(TP.mark())
        for og in range(16):
            j = 16 * pr + og
            wl = [(s_pcf, 1), (s_peP, PEPF)]
            if j >= 2:
                wl.append((s_peF, peF_done[2 * (j - 2) + 1]))
            wF_done.append(wslice_load(og % 2, qcf, 512 * og, NKD,
                                       512, s_wF, wl, cast=False))
            for tp in range(2):
                tt = 2 * pr + tp

                def fmm(e, og=og, tp=tp):
                    last = None
                    for ki in range(NKD):
                        last = e.matmul(
                            _bk(ps, 2 * (og % 2) + tp),
                            big_b[:, 256 * ki + 128 * tp:
                                  256 * ki + 128 * (tp + 1)],
                            wbuf[:, 8192 * (og % 2) + 512 * ki:
                                 8192 * (og % 2) + 512 * (ki + 1)],
                            start=(ki == 0), stop=(ki == NKD - 1))
                    return last
                wl = [lnF_done[-1], (s_wF, wF_done[-1])]
                n = len(peF_done)
                if n >= 2:
                    wl.append((s_rF, rF_done[n - 2]))
                if pr >= 1 and og == 0 and tp == 0:
                    wl.append((s_khst, khst_done[-1]))
                peF_done.append(o("tensor", fmm, waits=wl, inc=s_peF,
                                  kind="m"))
                rF_done.append(o(
                    "scalar", lambda e, tp=tp, og=og: e.activation(
                        out=rbufF[:, 8192 * tp + 512 * og:
                                  8192 * tp + 512 * (og + 1)],
                        in_=_bk(ps, 2 * (og % 2) + tp), func=AF.Relu,
                        scale=sc_bc[:, 2:3]),
                    waits=[(s_peF, peF_done[-1]), (s_c0v, SCL)],
                    inc=s_rF))
        for tp in range(2):
            tt = 2 * pr + tp

            def fq(e, tp=tp, tt=tt):
                e.reduce_max(out=dnv[:, 0:1],
                             in_=rbufF[:, 8192 * tp:8192 * (tp + 1)],
                             axis=AX, apply_absolute_value=True)
                e.drain()
                e.tensor_tensor(out=dnv[:, 0:1], in0=dnv[:, 0:1],
                                in1=dnv[:, 0:1], op=AL.mult)
                e.drain()
                e.tensor_scalar(out=dnv[:, 1:2], in0=dnv[:, 0:1],
                                scalar1=1e-5, scalar2=None, op0=AL.max)
                e.drain()
                e.reciprocal(out=dnv[:, 2:3], in_=dnv[:, 1:2])
                e.drain()
                e.tensor_scalar(out=dnv[:, 2:3], in0=dnv[:, 2:3],
                                scalar1=255.0, scalar2=None, op0=AL.mult)
                e.drain()
                e.scalar_tensor_tensor(out=cfv[:, tt:tt + 1],
                                       in0=dnv[:, 1:2],
                                       scalar=1.0 / 255.0,
                                       in1=sc_bc[:, 3:4], op0=AL.mult,
                                       op1=AL.mult)
                e.drain()
                return e.drain()
            qF_done.append(o("vector", fq, waits=[(s_rF, rF_done[-1])],
                             inc=s_qF))
            base_q = qF_done[-1]
            sq_done = []
            for c in range(8):
                hb = 1024 * (c % 2)
                wl = [(s_qF, base_q)] if c < 2 else [(s_qF, qF_done[-1])]
                sq_done.append(o("scalar", lambda e, tp=tp, c=c, hb=hb:
                                 e.activation(
                                   out=y32[:, hb:hb + 1024],
                                   in_=rbufF[:, 8192 * tp + 1024 * c:
                                             8192 * tp + 1024 * (c + 1)],
                                   func=AF.Square),
                                 waits=wl, inc=s_sqF))

                def fr(e, tp=tp, c=c, hb=hb):
                    e.tensor_scalar(out=qi32[:, hb:hb + 1024],
                                    in0=y32[:, hb:hb + 1024],
                                    scalar1=dnv[:, 2:3], scalar2=None,
                                    op0=AL.mult)
                    e.drain()
                    return e.tensor_scalar(
                        out=rbufF[:, 8192 * tp + 1024 * c:
                                  8192 * tp + 1024 * (c + 1)],
                        in0=qi32[:, hb:hb + 1024], scalar1=0.0,
                        scalar2=None, op0=AL.add)
                qF_done.append(o("vector", fr,
                                 waits=[(s_sqF, sq_done[-1])], inc=s_qF))
            for wv in range(2):
                wl = [(s_qF, qF_done[-1])]
                if len(khst_done) >= 1:
                    wl.append((s_khst, khst_done[-1]))
                tpb(zq, 0, 128, rbufF, 8192 * tp + 4096 * wv, 128, 32, wl)

                def fst(e, tt=tt, wv=wv):
                    ia = bass.AP(zq, 0, [[4096, P], [128, 32], [1, 128]])
                    oa = bass.AP(khat_d, 4096 * wv * T + 128 * tt,
                                 [[1024, P], [131072, 32], [1, 128]])
                    return e.dma_start(out=oa, in_=ia)
                khst_done.append(o("sync", fst, waits=TP.mark(),
                                   inc=s_khst, k=16, kind="d"))
    lhG_done, wG_done, peG_done, evG_done = [], [], [], []
    for kq in range(4):
        def fld(e, kq=kq):
            ia = bass.AP(khat_d, 2048 * kq * T,
                         [[1024, P], [131072, 16], [1, 1024]])
            oa = bass.AP(big_a, 0, [[16384, P], [1024, 16], [1, 1024]])
            return e.dma_start(out=oa, in_=ia)
        wl = [(s_khst, khst_done[-1]), (s_peF, peF_done[-1]),
              (s_qF, qF_done[-1]), (s_rF, rF_done[-1])]
        if kq >= 1:
            wl.append((s_peG, peG_done[-1]))
        lhG_done.append(o("sync", fld, waits=wl, inc=s_lhG, k=16,
                          kind="d"))
        for og in range(4):
            j = 4 * kq + og
            wl = [(s_peP, PEPF), (s_pcp, 1)]
            if j >= 2:
                wl.append((s_peG, peG_done[8 * (j - 1) - 1]))
            wG_done.append(wslice_load(og % 2, qcp, 512 * og, NKD, 512,
                                       s_wG, wl, row0=2048 * kq,
                                       cast=False))
            for tt in range(NT):
                def fmm(e, og=og, tt=tt):
                    last = None
                    for ki in range(NKD):
                        last = e.matmul(
                            _bk(ps, tt % 2),
                            big_a[:, 1024 * ki + 128 * tt:
                                  1024 * ki + 128 * (tt + 1)],
                            wbuf[:, 8192 * (og % 2) + 512 * ki:
                                 8192 * (og % 2) + 512 * (ki + 1)],
                            start=(ki == 0), stop=(ki == NKD - 1))
                    return last
                wl = [(s_lhG, lhG_done[kq]), (s_wG, wG_done[j]),
                      (s_evP, EVPF)]
                n = len(peG_done)
                if n >= 2:
                    wl.append((s_evG, evG_done[n - 2]))
                peG_done.append(o("tensor", fmm, waits=wl, inc=s_peG,
                                  kind="m"))
                evG_done.append(o(
                    "vector", lambda e, og=og, tt=tt:
                    e.scalar_tensor_tensor(
                        out=x_sb[:, D * tt + 512 * og:
                                 D * tt + 512 * (og + 1)],
                        in0=_bk(ps, tt % 2), scalar=cfv[:, tt:tt + 1],
                        in1=x_sb[:, D * tt + 512 * og:
                                 D * tt + 512 * (og + 1)],
                        op0=AL.mult, op1=AL.add),
                    waits=[(s_peG, peG_done[-1])], inc=s_evG))

    _finish(nc, em, o, st, x_sb, y_out, [(s_evG, evG_done[-1])])
    em.emit()
    return nc


def _finish(nc, em, o, st, x_sb, y_out, waits, dumps=()):
    s_out = st("s_out")
    vals = []
    for tt in range(NT):
        vals.append(o("sync", lambda e, tt=tt: e.dma_start(
            out=y_out[P * tt:P * (tt + 1), :],
            in_=x_sb[:, D * tt:D * (tt + 1)]), waits=waits, inc=s_out,
            k=16, kind="d"))
    for (name, sb) in dumps:
        dshape = [sb.shape[0], sb.shape[1]]
        dten = nc.dram_tensor("dbg_" + name, dshape, sb.dtype,
                              kind="ExternalOutput")
        vals.append(o("sync", lambda e, dten=dten, sb=sb: e.dma_start(
            out=dten[:, :], in_=sb[:, :]), waits=waits, inc=s_out,
            k=16, kind="d"))
    o("sync", None, waits=[(s_out, vals[-1])])


def _hadamard(n):
    h = np.array([[1.0]], dtype=np.float32)
    while h.shape[0] < n:
        h = np.block([[h, h], [h, -h]])
    return h


def _dev_wstat(wT):
    K, O = wT.shape
    nki, noc = K // P, O // P
    w = wT.reshape(nki, P, noc, P).transpose(1, 2, 0, 3).reshape(
        P, noc, nki * P)
    return np.ascontiguousarray(w)


_NC_CACHE = {}


def kernel(L0_attn_x, L0_attn_Wq, L0_attn_bq, L0_attn_Wk, L0_attn_bk,
           L0_attn_Wv, L0_attn_bv, L0_attn_Wout, L0_attn_bout,
           L0_ffn_w_up, L0_ffn_w_down, L1_attn_w_qkv, L1_attn_w_proj,
           L1_attn_b_proj, L1_ffn_c_fc, L1_ffn_c_proj):
    f32 = np.float32
    x = np.ascontiguousarray(np.asarray(L0_attn_x, f32).reshape(2 * SB, D))
    wqT = np.asarray(L0_attn_Wq, f32).T
    wkT = np.asarray(L0_attn_Wk, f32).T
    wvT = np.asarray(L0_attn_Wv, f32).T
    woT = np.ascontiguousarray(np.asarray(L0_attn_Wout, f32).T)
    wqkvT = np.asarray(L1_attn_w_qkv, f32).T
    wq1T = wqkvT[:, 0:D]
    wkvT = np.ascontiguousarray(wqkvT[:, D:3 * D])
    wprojT = np.ascontiguousarray(np.asarray(L1_attn_w_proj, f32).T)
    w_upT = np.ascontiguousarray(np.asarray(L0_ffn_w_up, f32).T)
    w_dnT = np.ascontiguousarray(np.asarray(L0_ffn_w_down, f32).T)
    cfc = np.ascontiguousarray(np.asarray(L1_ffn_c_fc, f32))
    cpjT = np.ascontiguousarray(np.asarray(L1_ffn_c_proj, f32).T)

    wq_d = _dev_wstat(wqT)
    wk_d = _dev_wstat(wkT)
    wv_d = _dev_wstat(wvT)
    wq1_d = _dev_wstat(np.ascontiguousarray(wq1T))

    def fmb(b):
        return np.ascontiguousarray(np.asarray(b, f32).reshape(NKD, P).T)

    h1024 = _hadamard(1024).astype(ml_dtypes.bfloat16)

    key = DEBUG_STOP
    if key not in _NC_CACHE:
        _NC_CACHE[key] = build_program(stop=DEBUG_STOP)
    nc = _NC_CACHE[key]

    in_maps = []
    for c in range(8):
        in_maps.append({
            "x_in": np.ascontiguousarray(x[T * c:T * (c + 1)]),
            "wq_d": wq_d, "wk_d": wk_d, "wv_d": wv_d, "wq1_d": wq1_d,
            "woT": woT, "wkvT": wkvT, "wprojT": wprojT, "h1024": h1024,
            "bq_in": fmb(L0_attn_bq), "bk_in": fmb(L0_attn_bk),
            "bv_in": fmb(L0_attn_bv),
            "bo_row": np.ascontiguousarray(
                np.asarray(L0_attn_bout, f32)[None, :]),
            "bpj_row": np.ascontiguousarray(
                np.asarray(L1_attn_b_proj, f32)[None, :]),
            "shup": np.ascontiguousarray(w_upT[256 * c:256 * (c + 1)]),
            "shdn": np.ascontiguousarray(w_dnT[512 * c:512 * (c + 1)]),
            "shcf": np.ascontiguousarray(cfc[256 * c:256 * (c + 1)]),
            "shcp": np.ascontiguousarray(cpjT[1024 * c:1024 * (c + 1)]),
        })
    res = run_bass_kernel_spmd(nc, in_maps, core_ids=list(range(8)))
    out = np.concatenate([res.results[c]["y_out"] for c in range(8)],
                         axis=0)
    return out.reshape(2, SB, D)


# version 12
# speedup vs baseline: 1.0138x; 1.0138x over previous
import numpy as np
import ml_dtypes
import concourse.bass as bass
import concourse.mybir as mybir
from concourse.bass_utils import run_bass_kernel_spmd

F32 = mybir.dt.float32
BF16 = mybir.dt.bfloat16
I32 = mybir.dt.int32
AL = mybir.AluOpType
AF = mybir.ActivationFunctionType
AX = mybir.AxisListType.X

P = 128
T = 1024
NT = 8
D = 2048
NKD = 16
DI = 4096
H2 = 8192
SB = 4096
NKC = 32
RS128 = float(1.0 / np.sqrt(128.0))
C_UP = float(1.0 / (127.0 * np.sqrt(2048.0)))
C_DN = float(1.0 / (127.0 * np.sqrt(4096.0)))
EPS_U = float(1e-5 * np.sqrt(2048.0))
EPS_Z = float(1e-5 * np.sqrt(4096.0))
TW_NUMEL = [D * DI, DI * D, D * H2, H2 * D]

DEBUG_STOP = None


class St:
    def __init__(self, nc, name):
        self.sem = nc.alloc_semaphore(name)
        self.n = 0


class Pl:

    def __init__(self, mk, n, name):
        self.sts = [mk("%s_%d" % (name, i)) for i in range(n)]
        self.i = 0

    def nxt(self):
        s = self.sts[self.i % len(self.sts)]
        self.i += 1
        return s

    def mark(self):
        return [(s, s.n) for s in self.sts]


def _flat(waits):
    out = []
    for w in waits:
        if isinstance(w, list):
            out.extend(w)
        else:
            out.append(w)
    return out


class Em:

    def __init__(self, nc):
        self.nc = nc
        self.ops = []

    def o(self, eng, fn=None, waits=(), inc=None, k=1, kind="c"):
        post = None
        waits = _flat(list(waits))
        if inc is not None:
            if kind == "d":
                waits.append((inc, inc.n))
            inc.n += k
            post = inc.n
        self.ops.append((eng, fn, tuple(waits), inc, k, kind, post))
        return post

    def emit(self):
        nc = self.nc
        with nc.Block() as blk:
            def mk(name):
                def f(eng):
                    water = {}
                    prev_c = False
                    for (e, fn, waits, inc, k, kind, post) in self.ops:
                        if e != name:
                            continue
                        for (stm, v) in waits:
                            if v > 0 and v > water.get(stm.sem.num, 0):
                                eng.wait_ge(stm.sem, v)
                                water[stm.sem.num] = v
                        if fn is None:
                            ins = (eng.drain() if name in
                                   ("vector", "scalar", "gpsimd")
                                   else eng.nop())
                        else:
                            if kind == "c" and prev_c and name in (
                                    "vector", "scalar", "gpsimd"):
                                eng.drain()
                            ins = fn(eng)
                        prev_c = (kind == "c" and fn is not None)
                        if inc is not None:
                            ins.then_inc(inc.sem, k)
                            if (kind != "d" and name != "gpsimd"
                                    and post > water.get(inc.sem.num, 0)):
                                water[inc.sem.num] = post
                    return None
                return f
            blk.sync(mk("sync"))
            blk.gpsimd(mk("gpsimd"))
            blk.scalar(mk("scalar"))
            blk.vector(mk("vector"))
            blk.tensor(mk("tensor"))


def _bk(ps, b, n=1):
    return ps[:, 512 * b:512 * (b + n)]


def build_program(stop=None):
    nc = bass.Bass("TRN2", target_bir_lowering=False)
    DT = nc.dram_tensor
    SA = nc.alloc_sbuf_tensor

    x_in = DT("x_in", [T, D], F32, kind="ExternalInput")
    wq_d = DT("wq_d", [P, NKD, D], F32, kind="ExternalInput")
    wk_d = DT("wk_d", [P, NKD, D], F32, kind="ExternalInput")
    wv_d = DT("wv_d", [P, NKD, D], F32, kind="ExternalInput")
    wq1_d = DT("wq1_d", [P, NKD, D], F32, kind="ExternalInput")
    woT = DT("woT", [D, D], F32, kind="ExternalInput")
    wkvT = DT("wkvT", [D, 2 * D], F32, kind="ExternalInput")
    wprojT = DT("wprojT", [D, D], F32, kind="ExternalInput")
    h1024 = DT("h1024", [1024, 1024], BF16, kind="ExternalInput")
    bq_in = DT("bq_in", [P, NKD], F32, kind="ExternalInput")
    bk_in = DT("bk_in", [P, NKD], F32, kind="ExternalInput")
    bv_in = DT("bv_in", [P, NKD], F32, kind="ExternalInput")
    bo_row = DT("bo_row", [1, D], F32, kind="ExternalInput")
    bpj_row = DT("bpj_row", [1, D], F32, kind="ExternalInput")
    shup = DT("shup", [D // 8, DI], F32, kind="ExternalInput")
    shdn = DT("shdn", [DI // 8, D], F32, kind="ExternalInput")
    shcf = DT("shcf", [D // 8, H2], F32, kind="ExternalInput")
    shcp = DT("shcp", [H2 // 8, D], F32, kind="ExternalInput")
    y_out = DT("y_out", [T, D], F32, kind="ExternalOutput")

    ws_in = DT("ws_in", [1, 4], F32)
    ws_out = DT("ws_out", [1, 4], F32)
    qup_in = DT("qup_in", [D // 8, DI], BF16)
    qdn_in = DT("qdn_in", [DI // 8, D], BF16)
    qcf_in = DT("qcf_in", [D // 8, H2], BF16)
    qcp_in = DT("qcp_in", [H2 // 8, D], BF16)
    qup = DT("qup", [D, DI], BF16)
    qdn = DT("qdn", [DI, D], BF16)
    qcf = DT("qcf", [D, H2], BF16)
    qcp = DT("qcp", [H2, D], BF16)
    k_in = DT("k_in", [D, T], BF16)
    v_in = DT("v_in", [D, T], BF16)
    n_in = DT("n_in", [1, NKD], BF16)
    k_all = DT("k_all", [4 * D, T], BF16)
    v_all = DT("v_all", [4 * D, T], BF16)
    n_all = DT("n_all", [4, NKD], BF16)
    vk_in = DT("vk_in", [32, 2112], F32)
    vk_out = DT("vk_out", [32, 2112], F32)
    wsum_d = DT("wsum_d", [P, 4], F32)
    xq2_d = DT("xq2_d", [DI, T], BF16)
    khat_d = DT("khat_d", [H2, T], BF16)

    x_sb = SA("x_sb", [P, NT * D], F32)
    big_a = SA("big_a", [P, 16384], BF16)
    big_b = SA("big_b", [P, 16384], BF16)
    wbuf = SA("wbuf", [P, 16384], BF16)
    wt_sb = SA("wt_sb", [P, 4096], BF16)
    lnt = SA("lnt", [P, 2 * D], BF16)
    bc_sb = SA("bc_sb", [P, D], BF16)
    psb = SA("psb", [P, 2 * T], BF16)
    ktile = SA("ktile", [P, 2 * P], BF16)
    vsb = SA("vsb", [P, 288], BF16)
    vaug = SA("vaug", [P, 2 * 2112], BF16)
    st6 = SA("st6", [P, 24], F32)
    agg = SA("agg", [P, 2], F32)
    c_a = SA("c_a", [P, 4], F32)
    qnm = SA("qnm", [1, NKD], F32)
    knm = SA("knm", [1, NKD], F32)
    nrow = SA("nrow", [1, 4 * NKD], BF16)
    n_bf = SA("n_bf", [1, NKD], BF16)
    ksq = SA("ksq", [P, 512], BF16)
    nmx = SA("nmx", [1, 2], F32)
    biasneg = SA("biasneg", [P, NKD], F32)
    bq_sb = SA("bq_sb", [P, NKD], F32)
    bk_sb = SA("bk_sb", [P, NKD], F32)
    bv_sb = SA("bv_sb", [P, NKD], F32)
    wacc = SA("wacc", [P, 4], F32)
    wtmp = SA("wtmp", [P, 4], F32)
    sred = SA("sred", [1, 512], F32)
    wsrow = SA("wsrow", [1, 8], F32)
    negrow = SA("negrow", [1, NKD], F32)
    negrow_bf = SA("negrow_bf", [1, NKD], BF16)
    sc_bc = SA("sc_bc", [P, 4], F32)
    thr_bc = SA("thr_bc", [P, 4], F32)
    nthr_bc = SA("nthr_bc", [P, 4], F32)
    ones_c = SA("ones_c", [P, 1], BF16)
    ones_r = SA("ones_r", [1, P], BF16)
    ones_r32 = SA("ones_r32", [1, P], F32)
    dnv = SA("dnv", [P, 4], F32)
    cfv = SA("cfv", [P, NT], F32)
    cfv2 = SA("cfv2", [P, NT], F32)
    rcd = SA("rcd", [P, 4], F32)
    mmx = SA("mmx", [P, 4], F32)
    vkacc = SA("vkacc", [32, 2112], F32)

    ps = nc.alloc_psum_tensor("ps", [P, 4096], F32)

    def at(base, off_bytes, name, shape, dt):
        a = nc.lookup_mloc(base).addr
        return nc.alloc_sbuf_tensor_at(name, shape, dt, offset=a + off_bytes)

    qtA = at(wbuf, 0, "qtA", [P, 2048], F32)
    qtB = at(wbuf, 8192, "qtB", [P, 2048], F32)
    tq1 = at(psb, 0, "tq1", [P, 2048], BF16)
    tq2 = at(bc_sb, 0, "tq2", [P, 2048], BF16)
    ksl = at(vaug, 0, "ksl", [P, 2 * 512], BF16)
    fmt = at(big_b, 2 * 4096, "fmt", [P, 8192], BF16)
    zq = at(big_b, 2 * 12288, "zq", [P, 4096], BF16)
    qi32 = at(wt_sb, 0, "qi32", [P, 2048], I32)
    y32 = at(vaug, 0, "y32", [P, 2048], F32)
    vkT = at(vaug, 0, "vkT", [P, 2112], BF16)
    ktm2 = at(wt_sb, 0, "ktm2", [P, 4096], BF16)
    rD0 = at(big_b, 0, "rD0", [P, 4096], BF16)
    rD1 = at(big_a, 2 * 12288, "rD1", [P, 4096], BF16)

    def rbufD(tp):
        return rD0 if tp == 0 else rD1

    em = Em(nc)
    o = em.o
    stD = {}

    def st(name):
        stD[name] = St(nc, name)
        return stD[name]

    TP = Pl(st, 5, "tp")
    KL = Pl(st, 4, "kl")
    VL = Pl(st, 4, "vl")
    KS = Pl(st, 4, "ks")

    def tpb(dst, do_, dbs, src, so_, sbs, nblk, waits, grp=8):
        dw, sw = dst.shape[1], src.shape[1]
        for g0 in range(0, nblk, grp):
            g = min(grp, nblk - g0)

            def f(e, g0=g0, g=g):
                ia = bass.AP(src, so_ + sbs * g0,
                             [[sw, P], [sbs, g], [1, 128]])
                oa = bass.AP(dst, do_ + dbs * g0,
                             [[dw, P], [dbs, g], [1, 128]])
                return e.dma_start_transpose(out=oa, in_=ia)
            o("sync", f, waits=waits, inc=TP.nxt(), k=16, kind="d")

    s_x = st("s_x")
    s_bs = st("s_bs")
    s_ini = st("s_ini")
    s_q0 = st("s_q0")
    s_qv = st("s_qv")
    s_c0 = st("s_c0")

    for tt in range(NT):
        o("sync", lambda e, tt=tt: e.dma_start(
            out=x_sb[:, D * tt:D * (tt + 1)],
            in_=x_in[P * tt:P * (tt + 1), :]), inc=s_x, k=16, kind="d")
    XF = s_x.n
    o("sync", lambda e: e.dma_start(out=bq_sb[:, :], in_=bq_in[:, :]),
      inc=s_bs, k=16, kind="d")
    o("sync", lambda e: e.dma_start(out=bk_sb[:, :], in_=bk_in[:, :]),
      inc=s_bs, k=16, kind="d")
    o("sync", lambda e: e.dma_start(out=bv_sb[:, :], in_=bv_in[:, :]),
      inc=s_bs, k=16, kind="d")
    BSF = s_bs.n
    o("gpsimd", lambda e: e.memset(ones_c[:, :], 1.0))
    o("gpsimd", lambda e: e.memset(ones_r[:, :], 1.0))
    o("gpsimd", lambda e: e.memset(vsb[:, 128:129], 1.0))
    o("gpsimd", lambda e: e.memset(vsb[:, 272:273], 1.0))
    INI = o("gpsimd", lambda e: e.memset(ones_r32[:, :], 1.0), inc=s_ini)
    v0 = o("vector", lambda e: e.memset(wacc[:, :], 0.0), inc=s_qv)
    VK0 = o("vector", lambda e: e.memset(vkacc[:, :], 0.0), inc=s_qv)

    qjobs = []
    for r in range(2):
        for cs in range(2):
            qjobs.append((shup, 128 * r, 2048 * cs, 2048, qup_in, 0))
    for r in range(4):
        qjobs.append((shdn, 128 * r, 0, 2048, qdn_in, 1))
    for r in range(2):
        for cs in range(4):
            qjobs.append((shcf, 128 * r, 2048 * cs, 2048, qcf_in, 2))
    for r in range(8):
        qjobs.append((shcp, 128 * r, 0, 2048, qcp_in, 3))
    NQJ = len(qjobs)

    QL = Pl(st, 2, "ql")
    qv_done = []
    for j, (dr, r0, c0, w, _, wi) in enumerate(qjobs):
        buf = qtA if j % 2 == 0 else qtB
        wl = [] if j < 2 else [(s_qv, qv_done[j - 2])]
        stm = QL.nxt()
        ld = (stm, o("sync", lambda e, dr=dr, r0=r0, c0=c0, w=w, buf=buf:
                     e.dma_start(out=buf[:, 0:w],
                                 in_=dr[r0:r0 + P, c0:c0 + w]),
                     waits=wl, inc=stm, k=16, kind="d"))
        o("vector", lambda e, buf=buf, w=w, wi=wi: e.reduce_sum(
            out=wtmp[:, wi:wi + 1], in_=buf[:, 0:w], axis=AX,
            apply_absolute_value=True), waits=[ld])
        qv_done.append(o("vector", lambda e, wi=wi: e.tensor_tensor(
            out=wacc[:, wi:wi + 1], in0=wacc[:, wi:wi + 1],
            in1=wtmp[:, wi:wi + 1], op=AL.add), inc=s_qv))

    c1 = o("gpsimd", lambda e: e.dma_start(out=wsum_d[:, :], in_=wacc[:, :]),
           waits=[(s_qv, qv_done[-1])], inc=s_c0, k=16, kind="d")
    c2 = o("gpsimd", lambda e: e.dma_start(
        out=sred[:, :], in_=bass.AP(wsum_d, 0, [[512, 1], [1, 512]])),
        waits=[(s_c0, c1)], inc=s_c0, k=16, kind="d")
    for wi in range(4):
        o("vector", lambda e, wi=wi: e.reduce_sum(
            out=wsrow[:, wi:wi + 1],
            in_=bass.AP(sred, wi, [[512, 1], [4, 128]]), axis=AX),
          waits=[(s_c0, c2)])
    s_c0v = st("s_c0v")
    s_ar0 = st("s_ar0")
    c3 = o("vector", None, inc=s_c0v)
    c4 = o("gpsimd", lambda e: e.dma_start(out=ws_in[:, :],
                                           in_=wsrow[:, 0:4]),
           waits=[(s_c0v, c3)], inc=s_c0, k=16, kind="d")
    c5 = o("gpsimd", lambda e: e.collective_compute(
        "AllReduce", AL.add, replica_groups=[list(range(8))],
        ins=[ws_in.ap().opt()], outs=[ws_out.ap().opt()]),
        waits=[(s_c0, c4)], inc=s_ar0, kind="d")
    c6 = o("gpsimd", lambda e: e.dma_start(out=wsrow[:, 4:8],
                                           in_=ws_out[:, :]),
           waits=[(s_ar0, c5)], inc=s_c0, k=16, kind="d")
    for wi in range(4):
        o("vector", lambda e, wi=wi: e.tensor_scalar(
            out=wsrow[:, 4 + wi:5 + wi], in0=wsrow[:, 4 + wi:5 + wi],
            scalar1=1.0 / TW_NUMEL[wi], scalar2=1e-5, op0=AL.mult,
            op1=AL.max), waits=[(s_c0, c6)])
    c7 = o("vector", None, inc=s_c0v)
    c8 = o("gpsimd", lambda e: e.dma_start(out=ws_in[:, :],
                                           in_=wsrow[0:1, 4:8]),
           waits=[(s_c0v, c7)], inc=s_c0, k=16, kind="d")
    c9 = o("gpsimd", lambda e: e.dma_start(
        out=sc_bc[:, :], in_=bass.AP(ws_in, 0, [[0, P], [1, 4]])),
        waits=[(s_c0, c8)], inc=s_c0, k=16, kind="d")

    def fscl(e):
        e.tensor_scalar(out=thr_bc[:, :], in0=sc_bc[:, :],
                        scalar1=0.5, scalar2=None, op0=AL.mult)
        e.drain()
        return e.tensor_scalar(out=nthr_bc[:, :], in0=thr_bc[:, :],
                               scalar1=-1.0, scalar2=None, op0=AL.mult)
    SCL = o("vector", fscl, waits=[(s_c0, c9)], inc=s_c0v)

    s_q2 = st("s_q2")
    s_qs = st("s_qs")
    s_qw = st("s_qw")
    s_qst = st("s_qst")
    s_p0 = st("s_p0")
    s_pdn = st("s_pdn")
    s_pcf = st("s_pcf")
    s_pcp = st("s_pcp")

    qst_done = []
    qw_done = []
    for j, (dr, r0, c0, w, qd, wi) in enumerate(qjobs):
        buf = qtA if j % 2 == 0 else qtB
        wl = [(s_c0v, SCL)] if j < 2 else [(s_qw, qw_done[j - 2])]
        stm2 = QL.nxt()
        ld = (stm2, o("sync", lambda e, dr=dr, r0=r0, c0=c0, w=w, buf=buf:
                      e.dma_start(out=buf[:, 0:w],
                                  in_=dr[r0:r0 + P, c0:c0 + w]),
                      waits=wl, inc=stm2, k=16, kind="d"))

        def fq1(e, buf=buf, w=w, wi=wi):
            e.tensor_scalar(out=tq2[:, 0:w], in0=buf[:, 0:w],
                            scalar1=nthr_bc[:, wi:wi + 1], scalar2=None,
                            op0=AL.is_lt)
            e.drain()
            return e.tensor_scalar(out=tq1[:, 0:w], in0=buf[:, 0:w],
                                   scalar1=thr_bc[:, wi:wi + 1],
                                   scalar2=None, op0=AL.is_gt)
        wl_q = [ld]
        if j >= 1:
            wl_q.append((s_qst, qst_done[j - 1]))
        o("vector", fq1, waits=wl_q)
        qw_done.append(o("vector", lambda e, w=w: e.tensor_tensor(
            out=tq1[:, 0:w], in0=tq1[:, 0:w], in1=tq2[:, 0:w],
            op=AL.subtract), inc=s_qw))
        qst_done.append(o("sync", lambda e, qd=qd, r0=r0, c0=c0, w=w:
                          e.dma_start(out=qd[r0:r0 + P, c0:c0 + w],
                                      in_=tq1[:, 0:w]),
                          waits=[(s_qw, qw_done[-1])], inc=s_qst, k=16,
                          kind="d"))
    QST_ALL = qst_done[-1]

    s_lnc = st("s_lnc")

    def ln_chain(tt, slot, pre):
        s_l = s_lnc
        base = s_l.n
        xt = x_sb[:, D * tt:D * (tt + 1)]

        def f1(e):
            for c in range(4):
                e.bn_stats(out=st6[:, 6 * c:6 * (c + 1)],
                           in_=xt[:, 512 * c:512 * (c + 1)])
                e.drain()
            e.bn_aggr(out=agg[:, :], in_=st6[:, :])
            e.drain()
            return e.tensor_scalar(out=c_a[:, 1:2], in0=agg[:, 1:2],
                                   scalar1=1e-5, scalar2=None, op0=AL.add)
        o("vector", f1, waits=pre, inc=s_l)
        o("scalar", lambda e: e.activation(
            out=c_a[:, 1:2], in_=c_a[:, 1:2], func=AF.Sqrt),
          waits=[(s_l, base + 1)], inc=s_l)

        def f2(e):
            e.reciprocal(out=c_a[:, 2:3], in_=c_a[:, 1:2])
            e.drain()
            e.scalar_tensor_tensor(out=c_a[:, 3:4], in0=agg[:, 0:1],
                                   scalar=-1.0, in1=c_a[:, 2:3],
                                   op0=AL.mult, op1=AL.mult)
            e.drain()
            return e.tensor_scalar(
                out=lnt[:, D * slot:D * (slot + 1)], in0=xt,
                scalar1=c_a[:, 2:3], scalar2=c_a[:, 3:4],
                op0=AL.mult, op1=AL.add)
        o("vector", f2, waits=[(s_l, base + 2)], inc=s_l)
        return (s_l, base + 3)

    def wslice_load(dst_half, src, og_col, n_ki, colw, stream, waits,
                    row0=0, cast=True):
        ncols = src.shape[1]

        def f(e):
            ia = bass.AP(src, row0 * ncols + og_col,
                         [[ncols, P], [P * ncols, n_ki], [1, colw]])
            oa = bass.AP(wbuf, 8192 * dst_half,
                         [[16384, P], [colw, n_ki], [1, colw]])
            return e.dma_start(out=oa, in_=ia)
        return o("gpsimd" if cast else "sync", f, waits=waits, inc=stream,
                 k=16, kind="d")

    s_wA = st("s_wA")
    s_peA = st("s_peA")
    s_evA = st("s_evA")
    s_ccA = st("s_ccA")
    s_kamx = st("s_kamx")
    s_sqA = st("s_sqA")
    s_nmm = st("s_nmm")
    s_nmv = st("s_nmv")
    nmv_done = []

    lhsA_done = []
    LHA0 = None
    for tt in range(NT):
        pre = [(s_x, 16 * (tt + 1))]
        if tt >= 2:
            pre.append(lhsA_done[tt - 2])
        s_l, lv = ln_chain(tt, tt % 2, pre)
        tpb(big_a, 128 * tt, 1024, lnt, D * (tt % 2), 128, NKD,
            [(s_l, lv)])
        lhsA_done.append(TP.mark())
        if tt == 3:
            LHA0 = TP.mark()
    LHA = lhsA_done[-1]

    wproj_srcs = [wk_d, wv_d, wq_d]
    wA_done = []
    for pi in range(3):
        for oc in range(NKD):
            j = pi * NKD + oc
            wl = [] if j < 2 else [(s_peA, 2 * (j - 1))]
            wA_done.append(o(
                "gpsimd", lambda e, pi=pi, oc=oc, j=j: e.dma_start(
                    out=wt_sb[:, 2048 * (j % 2):2048 * (j % 2) + 2048],
                    in_=wproj_srcs[pi][:, oc, :]),
                waits=wl, inc=s_wA, k=16, kind="d"))
    peA_done = []
    evA_done = []
    kst_done = []
    nmv_done = []
    KSK = KSV = None
    for pi in range(3):
        bias = (bk_sb, bv_sb, bq_sb)[pi]
        for oc in range(NKD):
            j = pi * NKD + oc
            for g2 in range(2):
                bidx = 2 * (j % 2) + g2

                def fmm(e, j=j, g2=g2, bidx=bidx):
                    last = None
                    for ki in range(NKD):
                        last = e.matmul(
                            _bk(ps, bidx),
                            wt_sb[:, 2048 * (j % 2) + 128 * ki:
                                  2048 * (j % 2) + 128 * (ki + 1)],
                            big_a[:, 1024 * ki + 512 * g2:
                                  1024 * ki + 512 * (g2 + 1)],
                            start=(ki == 0), stop=(ki == NKD - 1))
                    return last
                wl = [LHA0 if g2 == 0 else LHA, (s_wA, wA_done[j]),
                      (s_bs, BSF)]
                if j >= 2:
                    wl.append((s_evA, evA_done[2 * (j - 2) + 1]))
                peA_done.append(o("tensor", fmm, waits=wl, inc=s_peA,
                                  kind="m"))
                if pi == 2:
                    dst = big_b[:, 1024 * oc + 512 * g2:
                                1024 * oc + 512 * (g2 + 1)]
                    evA_done.append(o(
                        "scalar", lambda e, dst=dst, bidx=bidx, oc=oc,
                        bias=bias: e.activation(
                            out=dst, in_=_bk(ps, bidx), func=AF.Identity,
                            bias=bias[:, oc:oc + 1], scale=1.0),
                        waits=[(s_peA, peA_done[-1])], inc=s_evA))
                else:
                    nksl = len(kst_done)
                    half = nksl % 2
                    dst = ksl[:, 512 * half:512 * (half + 1)]
                    wl2 = [(s_peA, peA_done[-1])]
                    if nksl >= 2:
                        wl2.append(kst_done[nksl - 2])
                    evA_done.append(o(
                        "scalar", lambda e, dst=dst, bidx=bidx, oc=oc,
                        bias=bias: e.activation(
                            out=dst, in_=_bk(ps, bidx), func=AF.Identity,
                            bias=bias[:, oc:oc + 1], scale=1.0),
                        waits=wl2, inc=s_evA))
                    dstd = k_in if pi == 0 else v_in
                    kst_done.append((lambda stm: (stm, o(
                        "sync", lambda e, dstd=dstd, oc=oc, g2=g2, dst=dst:
                        e.dma_start(out=dstd[128 * oc:128 * (oc + 1),
                                             512 * g2:512 * (g2 + 1)],
                                    in_=dst),
                        waits=[(s_evA, evA_done[-1])], inc=stm, k=16,
                        kind="d")))(KS.nxt()))
                if pi != 1:
                    nj = len(nmv_done)
                    wl3 = [(s_evA, evA_done[-1])]
                    if nj >= 1:
                        wl3.append((s_nmm, nj))
                    sqv = o("scalar", lambda e, bidx=bidx: e.activation(
                        out=ksq[:, :], in_=_bk(ps, bidx), func=AF.Square),
                        waits=wl3, inc=s_sqA)
                    nb = 4 + (nj % 2)
                    wlm = [(s_sqA, sqv)]
                    if nj >= 2:
                        wlm.append((s_nmv, nmv_done[nj - 2]))
                    mmv = o("tensor", lambda e, nb=nb: e.matmul(
                        ps[0:1, 512 * nb:512 * nb + 512], ones_c[:, :],
                        ksq[:, :], start=True, stop=True),
                        waits=wlm, inc=s_nmm, kind="m")

                    def fnm(e, nb=nb, oc=oc, pi=pi, first=(g2 == 0)):
                        dstn = knm if pi == 0 else qnm
                        e.reduce_max(out=nmx[:, 0:1],
                                     in_=ps[0:1, 512 * nb:512 * nb + 512],
                                     axis=AX)
                        e.drain()
                        if first:
                            return e.tensor_copy(out=dstn[:, oc:oc + 1],
                                                 in_=nmx[:, 0:1])
                        return e.tensor_tensor(out=dstn[:, oc:oc + 1],
                                               in0=dstn[:, oc:oc + 1],
                                               in1=nmx[:, 0:1], op=AL.max)
                    nmv_done.append(o("vector", fnm, waits=[(s_nmm, mmv)],
                                      inc=s_nmv))
        if pi == 0:
            kmx = o("vector", lambda e: e.tensor_scalar(
                out=n_bf[:, :], in0=knm[:, :], scalar1=1.05, scalar2=None,
                op0=AL.mult), waits=[(s_nmv, nmv_done[-1])], inc=s_kamx)
            _nst = (lambda stm: (stm, o(
                "sync", lambda e: e.dma_start(out=n_in[:, :],
                                              in_=n_bf[:, :]),
                waits=[(s_kamx, kmx)], inc=stm, k=16, kind="d")))(KS.nxt())
            KSK = KS.mark()
            grp4 = [[0, 1, 2, 3], [4, 5, 6, 7]]
            ccA1 = o("gpsimd", lambda e: e.collective_compute(
                "AllGather", AL.bypass, replica_groups=grp4,
                ins=[n_in.ap().opt()], outs=[n_all.ap().opt()]),
                waits=KSK, inc=s_ccA, kind="d")
            ccA2 = o("gpsimd", lambda e: e.collective_compute(
                "AllGather", AL.bypass, replica_groups=grp4,
                ins=[k_in.ap().opt()], outs=[k_all.ap().opt()]),
                waits=KSK, inc=s_ccA, kind="d")
        elif pi == 1:
            KSV = KS.mark()
            ccA3 = o("gpsimd", lambda e: e.collective_compute(
                "AllGather", AL.bypass, replica_groups=grp4,
                ins=[v_in.ap().opt()], outs=[v_all.ap().opt()]),
                waits=KSV, inc=s_ccA, kind="d")
            VAUG1 = o("gpsimd", lambda e: e.memset(
                bass.AP(vaug, 32, [[4224, P], [33, 128]]), 1.0),
                waits=KSV, inc=s_ini)

    EVAF = evA_done[-1]

    s_nr = st("s_nr")
    s_peN = st("s_peN")
    s_sc = st("s_sc")
    s_ex = st("s_ex")
    s_at = st("s_at")
    s_evB = st("s_evB")
    s_bo = st("s_bo")

    s_nrv = st("s_nrv")
    ld = o("gpsimd", lambda e: e.dma_start(
        out=nrow[:, :], in_=bass.AP(n_all, 0, [[64, 1], [1, 64]])),
        waits=[(s_ccA, ccA1)], inc=s_nr, k=16, kind="d")

    def fpr(e):
        e.tensor_copy(out=knm[:, :], in_=nrow[0:1, 0:NKD])
        e.drain()
        for r in range(1, 4):
            e.tensor_tensor(out=knm[:, :], in0=knm[:, :],
                            in1=nrow[0:1, NKD * r:NKD * (r + 1)],
                            op=AL.max)
            e.drain()
        return e.drain()
    pr1 = o("vector", fpr, waits=[(s_nr, ld)], inc=s_nrv)

    sq1 = o("scalar", lambda e: e.activation(
        out=qnm[:, :], in_=qnm[:, :], func=AF.Sqrt),
        waits=[(s_nrv, pr1), (s_nmv, nmv_done[-1])], inc=s_nrv)
    sq2 = o("scalar", lambda e: e.activation(
        out=knm[:, :], in_=knm[:, :], func=AF.Sqrt),
        waits=[(s_nrv, sq1)], inc=s_nrv)

    def fng(e):
        e.tensor_tensor(out=negrow[:, :], in0=qnm[:, :], in1=knm[:, :],
                        op=AL.mult)
        e.drain()
        e.tensor_scalar(out=negrow[:, :], in0=negrow[:, :],
                        scalar1=-1.05 * RS128, scalar2=None, op0=AL.mult)
        e.drain()
        return e.tensor_copy(out=negrow_bf[:, :], in_=negrow[:, :])
    ng1 = o("vector", fng, waits=[(s_nrv, sq2), (s_evA, EVAF),
                                  (s_ini, INI), (s_c0v, SCL)], inc=s_nrv)
    pn2 = o("tensor", lambda e: e.matmul(
        ps[:, 3584:3600], ones_r[:, :], negrow_bf[:, :], start=True,
        stop=True), waits=[(s_nrv, ng1)], inc=s_peN, kind="m")
    NRMF = o("vector", lambda e: e.tensor_copy(out=biasneg[:, :],
                                               in_=ps[:, 3584:3600]),
             waits=[(s_peN, pn2)], inc=s_nrv)

    s_bov = st("s_bov")
    b1 = o("gpsimd", lambda e: e.dma_start(
        out=bc_sb[:, :], in_=bass.AP(bo_row, 0, [[0, P], [1, D]])),
        waits=[(s_qst, qst_done[-1]), (s_qw, qw_done[-1])],
        inc=s_bo, k=16, kind="d")
    bo_done = []
    for tt in range(NT):
        bo_done.append(o("vector", lambda e, tt=tt: e.tensor_tensor(
            out=x_sb[:, D * tt:D * (tt + 1)],
            in0=x_sb[:, D * tt:D * (tt + 1)], in1=bc_sb[:, :], op=AL.add),
            waits=[(s_bo, b1), (s_evA, EVAF)], inc=s_bov))

    def at_slot(qc):
        b = 4 + qc // 3
        sl = qc % 3
        return ps[:, 512 * b + 129 * sl:512 * b + 129 * sl + 129]

    sc_done, ex_done, at_done, kld_done, vld_done = [], [], [], [], []
    evB_done = []
    pend_fat = [None]

    def emit_fat():
        if pend_fat[0] is not None:
            f, wl = pend_fat[0]
            at_done.append(o("tensor", f, waits=wl, inc=s_at, kind="m"))
            pend_fat[0] = None
    for h in range(16):
        for kc in range(NKC):
            i = h * NKC + kc
            wl = [(s_ccA, ccA2)]
            if i >= 2:
                wl.append((s_sc, sc_done[i - 2]))
            stm = KL.nxt()
            kld_done.append((stm, o(
                "sync", lambda e, h=h, kc=kc, i=i: e.dma_start(
                    out=ktile[:, 128 * (i % 2):128 * (i % 2) + 128],
                    in_=k_all[2048 * (kc // 8) + 128 * h:
                              2048 * (kc // 8) + 128 * (h + 1),
                              128 * (kc % 8):128 * (kc % 8) + 128]),
                waits=wl, inc=stm, k=16, kind="d")))
            wl = [(s_ccA, ccA3)]
            if i >= 2:
                wl.append((s_at, at_done[i - 2]))
            stm = VL.nxt()
            vld_done.append((stm, o(
                "sync", lambda e, h=h, kc=kc, i=i: e.dma_start_transpose(
                    out=vsb[:, 144 * (i % 2):144 * (i % 2) + 128],
                    in_=v_all[2048 * (kc // 8) + 128 * h:
                              2048 * (kc // 8) + 128 * (h + 1),
                              128 * (kc % 8):128 * (kc % 8) + 128]),
                waits=wl, inc=stm, k=16, kind="d")))

            def fsc(e, h=h, i=i):
                last = None
                for g2 in range(2):
                    last = e.matmul(
                        _bk(ps, 2 * (i % 2) + g2),
                        ktile[:, 128 * (i % 2):128 * (i % 2) + 128],
                        big_b[:, 1024 * h + 512 * g2:
                              1024 * h + 512 * (g2 + 1)],
                        start=True, stop=True)
                return last
            wl = [kld_done[i], (s_evA, EVAF)]
            if i >= 2:
                wl.append((s_ex, ex_done[i - 2]))
            sc_done.append(o("tensor", fsc, waits=wl, inc=s_sc, kind="m"))
            emit_fat()
            wl = [(s_sc, sc_done[i]), (s_nrv, NRMF)]
            if i == 0:
                wl.append((s_qw, qw_done[-1]))
                wl.append((s_qst, QST_ALL))
            if i >= 2:
                wl.append((s_at, at_done[i - 2]))
            ex_done.append(o("scalar", lambda e, i=i, h=h: e.activation(
                out=psb[:, 1024 * (i % 2):1024 * (i % 2) + 1024],
                in_=_bk(ps, 2 * (i % 2), 2), func=AF.Exp,
                bias=biasneg[:, h:h + 1], scale=RS128),
                waits=wl, inc=s_ex))

            def fat(e, kc=kc, i=i):
                last = None
                for qc in range(NT):
                    last = e.matmul(
                        at_slot(qc),
                        psb[:, 1024 * (i % 2) + 128 * qc:
                            1024 * (i % 2) + 128 * (qc + 1)],
                        vsb[:, 144 * (i % 2):144 * (i % 2) + 129],
                        start=(kc == 0 and qc % 3 == 0),
                        stop=(kc == NKC - 1 and (qc % 3 == 2 or qc == 7)))
                return last
            wl = [(s_ex, ex_done[i]), vld_done[i]]
            if kc == 0 and h > 0:
                wl.append((s_evB, evB_done[-1]))
            pend_fat[0] = (fat, wl)

        emit_fat()

        def fev(e, h=h):
            for qc in range(NT):
                e.reciprocal(out=rcd[:, 0:1], in_=at_slot(qc)[:, 128:129])
                e.drain()
                e.tensor_scalar(
                    out=big_a[:, 2048 * qc + 128 * h:
                              2048 * qc + 128 * (h + 1)],
                    in0=at_slot(qc)[:, 0:128], scalar1=rcd[:, 0:1],
                    scalar2=None, op0=AL.mult)
                e.drain()
            return e.drain()
        evB_done.append(o("vector", fev, waits=[(s_at, at_done[-1])],
                          inc=s_evB))
    EVBF = evB_done[-1]
    for (iin, iout, sem) in ((qup_in, qup, s_p0),):
        o("gpsimd", lambda e, iin=iin, iout=iout:
          e.collective_compute(
            "AllGather", AL.bypass, replica_groups=[list(range(8))],
            ins=[iin.ap().opt()], outs=[iout.ap().opt()]),
          waits=[(s_qst, QST_ALL), (s_at, at_done[int(len(at_done) * 0.7)])], inc=sem, kind="d")
    SCF = sc_done[-1]
    EXF = ex_done[-1]

    if stop == "B":
        _finish(nc, em, o, st, x_sb, y_out,
                [(s_evB, EVBF), (s_bov, bo_done[-1])],
                dumps=[("big_a", big_a), ("big_b", big_b),
                       ("biasneg", biasneg), ("qnm", qnm), ("knm", knm),
                       ("bc_sb", bc_sb)])
        em.emit()
        return nc

    s_wC = st("s_wC")
    s_peC = st("s_peC")
    s_evC = st("s_evC")

    for qc in range(NT):
        tpb(big_b, 128 * qc, 1024, big_a, 2048 * qc, 128, NKD,
            [(s_evB, EVBF), (s_sc, SCF)])
    LHC = TP.mark()
    wC_done, peC_done, evC_done = [], [], []
    for og in range(4):
        wl = [(s_qst, qst_done[-1])]
        if og >= 2:
            wl.append((s_peC, peC_done[8 * (og - 1) - 1]))
        wC_done.append(wslice_load(og % 2, woT, 512 * og, NKD, 512, s_wC,
                                   wl))
        for tt in range(NT):
            def fmm(e, og=og, tt=tt):
                last = None
                for ki in range(NKD):
                    last = e.matmul(
                        _bk(ps, tt % 2),
                        big_b[:, 1024 * ki + 128 * tt:
                              1024 * ki + 128 * (tt + 1)],
                        wbuf[:, 8192 * (og % 2) + 512 * ki:
                             8192 * (og % 2) + 512 * (ki + 1)],
                        start=(ki == 0), stop=(ki == NKD - 1))
                return last
            wl = [LHC, (s_wC, wC_done[og]), (s_ex, EXF)]
            n = len(peC_done)
            if n >= 2:
                wl.append((s_evC, evC_done[n - 2]))
            peC_done.append(o("tensor", fmm, waits=wl, inc=s_peC, kind="m"))
            evC_done.append(o(
                "vector", lambda e, og=og, tt=tt: e.scalar_tensor_tensor(
                    out=x_sb[:, D * tt + 512 * og:D * tt + 512 * (og + 1)],
                    in0=_bk(ps, tt % 2), scalar=1.0,
                    in1=x_sb[:, D * tt + 512 * og:
                             D * tt + 512 * (og + 1)],
                    op0=AL.mult, op1=AL.add),
                waits=[(s_peC, peC_done[-1]), (s_bov, bo_done[tt])],
                inc=s_evC))
    EVCF = evC_done[-1]
    PECF = peC_done[-1]

    if stop == "C":
        _finish(nc, em, o, st, x_sb, y_out, [(s_evC, EVCF)])
        em.emit()
        return nc

    for (iin, iout, sem) in ((qdn_in, qdn, s_pdn), (qcf_in, qcf, s_pcf), (qcp_in, qcp, s_pcp)):
        o("gpsimd", lambda e, iin=iin, iout=iout:
          e.collective_compute(
            "AllGather", AL.bypass, replica_groups=[list(range(8))],
            ins=[iin.ap().opt()], outs=[iout.ap().opt()]),
          waits=[(s_qst, QST_ALL)], inc=sem, kind="d")
    s_hld = st("s_hld")
    s_peD = st("s_peD")
    s_aqD = st("s_aqD")
    s_wD = st("s_wD")
    s_peU = st("s_peU")
    s_rD = st("s_rD")
    s_bfD = st("s_bfD")
    s_peH2 = st("s_peH2")
    s_zD = st("s_zD")
    s_aq2 = st("s_aq2")
    s_xqst = st("s_xqst")
    s_wDn = st("s_wDn")
    s_lhDn = st("s_lhDn")
    s_peDn = st("s_peDn")
    s_evD = st("s_evD")

    def fhl(e):
        ia = bass.AP(h1024, 0, [[1024, P], [P * 1024, 8], [1, 1024]])
        oa = bass.AP(big_a, 0, [[16384, P], [1024, 8], [1, 1024]])
        return e.dma_start(out=oa, in_=ia)
    HLD = o("sync", fhl, waits=LHC, inc=s_hld, k=16, kind="d")
    h_sb = big_a

    peD_done = []
    aqD_done = []
    x1T_done = []
    rD_done = []
    bfD_done = []
    zD_done = []
    aq2_done = []
    x2T_done = []
    xqst_done = []
    peU_done = []
    peH2_done = []

    for pr in range(4):
        pair_mark = TP.mark()
        for tp in range(2):
            tt = 2 * pr + tp
            slot = tt % 2
            pre = [(s_evC, EVCF), (s_peC, PECF)]
            if tt >= 2:
                pre += pair_mark
            s_l, lv = ln_chain(tt, slot, pre)

            def fbt(e, slot=slot):
                e.tensor_tensor(
                    out=fmt[:, 4096 * slot:4096 * slot + 1024],
                    in0=lnt[:, D * slot:D * slot + 1024],
                    in1=lnt[:, D * slot + 1024:D * slot + 2048], op=AL.add)
                e.drain()
                return e.tensor_tensor(
                    out=fmt[:, 4096 * slot + 1024:4096 * slot + 2048],
                    in0=lnt[:, D * slot:D * slot + 1024],
                    in1=lnt[:, D * slot + 1024:D * slot + 2048],
                    op=AL.subtract)
            wl = [(s_l, lv)]
            if tt >= 2:
                wl.append((s_peD, peD_done[tt - 2]))
                wl.append((s_xqst, xqst_done[tt - 2]))
            if tt >= 1 and pr >= 1:
                wl.append((s_peU, s_peU.n))
            bt = o("vector", fbt, waits=wl, inc=s_l)
            tpb(fmt, 4096 * slot + 2048, 128, fmt, 4096 * slot, 128, 16,
                [(s_l, bt)])
            lhsD_tile = TP.mark()
            last_pe = None
            for b in range(2):
                for og in range(2):
                    bidx = 4 * (tt % 2) + 2 * b + og

                    def fmm(e, slot=slot, b=b, og=og, bidx=bidx):
                        last = None
                        for ki in range(8):
                            last = e.matmul(
                                _bk(ps, bidx),
                                fmt[:, 4096 * slot + 2048 +
                                    128 * (8 * b + ki):
                                    4096 * slot + 2048 +
                                    128 * (8 * b + ki) + 128],
                                h_sb[:, 1024 * ki + 512 * og:
                                     1024 * ki + 512 * (og + 1)],
                                start=(ki == 0), stop=(ki == 7))
                        return last
                    wl = [lhsD_tile, (s_hld, HLD)]
                    if tt == 0:
                        wl += [(s_ex, EXF), (s_evB, EVBF), (s_nrv, NRMF)]
                    if tt >= 2:
                        wl.append((s_aqD, aqD_done[tt - 2]))
                    last_pe = o("tensor", fmm, waits=wl, inc=s_peD,
                                kind="m")
            peD_done.append(last_pe)
        for tp in range(2):
            tt = 2 * pr + tp

            def faq(e, tt=tt, bidx0=4 * (tt % 2)):
                sp = _bk(ps, bidx0, 4)
                e.reduce_max(out=dnv[:, 0:1], in_=sp, axis=AX,
                             apply_absolute_value=True)
                e.drain()
                e.tensor_scalar(out=dnv[:, 1:2], in0=dnv[:, 0:1],
                                scalar1=EPS_U, scalar2=None, op0=AL.max)
                e.drain()
                e.reciprocal(out=dnv[:, 2:3], in_=dnv[:, 1:2])
                e.drain()
                e.tensor_scalar(out=dnv[:, 2:3], in0=dnv[:, 2:3],
                                scalar1=127.0, scalar2=None, op0=AL.mult)
                e.drain()
                e.scalar_tensor_tensor(out=cfv[:, tt:tt + 1],
                                       in0=dnv[:, 1:2], scalar=C_UP,
                                       in1=sc_bc[:, 0:1], op0=AL.mult,
                                       op1=AL.mult)
                e.drain()
                e.tensor_scalar(out=qi32[:, 0:2048], in0=sp,
                                scalar1=dnv[:, 2:3], scalar2=None,
                                op0=AL.mult)
                e.drain()
                return e.tensor_scalar(
                    out=lnt[:, D * (tt % 2):D * (tt % 2) + 2048],
                    in0=qi32[:, 0:2048], scalar1=0.0, scalar2=None,
                    op0=AL.add)
            wl = [(s_peD, peD_done[tt]), (s_c0v, SCL)]
            if tt >= 2:
                wl.append(x1T_done[tt - 2])
            aqD_done.append(o("vector", faq, waits=wl, inc=s_aqD))
            tpb(big_a, 8192 + 128 * tp, 256, lnt, D * (tt % 2), 128, NKD,
                [(s_aqD, aqD_done[-1]), (s_hld, HLD)])
            x1T_done.append(TP.mark())
        for og in range(8):
            j = 8 * pr + og
            wl = [(s_p0, 1)]
            if j >= 2:
                wl.append((s_peU, peU_done[2 * (j - 2) + 1]))
            wD_j = wslice_load(og % 2, qup, 512 * og, NKD, 512, s_wD,
                               wl, cast=False)
            for tp in range(2):
                tt = 2 * pr + tp

                def fmm(e, og=og, tp=tp):
                    last = None
                    for ki in range(NKD):
                        last = e.matmul(
                            _bk(ps, 2 * (og % 2) + tp),
                            big_a[:, 8192 + 256 * ki + 128 * tp:
                                  8192 + 256 * ki + 128 * (tp + 1)],
                            wbuf[:, 8192 * (og % 2) + 512 * ki:
                                 8192 * (og % 2) + 512 * (ki + 1)],
                            start=(ki == 0), stop=(ki == NKD - 1))
                    return last
                wl = [x1T_done[-1], (s_wD, wD_j)]
                n = len(peU_done)
                if n >= 2:
                    wl.append((s_rD, rD_done[n - 2]))
                if pr >= 1 and og == 0 and tp == 0:
                    wl.append((s_bfD, s_bfD.n))
                peU_done.append(o("tensor", fmm, waits=wl, inc=s_peU,
                                  kind="m"))
                rD_done.append(o(
                    "scalar", lambda e, tp=tp, tt=tt, og=og:
                    e.activation(
                        out=rbufD(tp)[:, 512 * og:512 * (og + 1)],
                        in_=_bk(ps, 2 * (og % 2) + tp), func=AF.Relu,
                        scale=cfv[:, tt:tt + 1]),
                    waits=[(s_peU, peU_done[-1])], inc=s_rD))
        sq_v = {}
        lhsT_t = {}
        peH2_t = {}
        zD_t = {}

        def d3a(tp):
            tt = 2 * pr + tp
            slot = tt % 2
            sq_v[tp] = o("scalar", lambda e, tp=tp: e.activation(
                out=rbufD(tp)[:, :], in_=rbufD(tp)[:, :], func=AF.Square),
                waits=[(s_rD, rD_done[-1])], inc=s_rD)

            def fb2(e, tp=tp, slot=slot):
                rb = rbufD(tp)
                for (do, i0, i1, op2) in ((0, 0, 1024, AL.add),
                                          (1024, 0, 1024, AL.subtract),
                                          (2048, 2048, 3072, AL.add),
                                          (3072, 2048, 3072, AL.subtract)):
                    e.tensor_tensor(
                        out=fmt[:, 4096 * slot + do:
                                4096 * slot + do + 1024],
                        in0=rb[:, i0:i0 + 1024],
                        in1=rb[:, i1:i1 + 1024], op=op2)
                    e.drain()
                for (do, i0, i1, op2) in ((0, 0, 2048, AL.add),
                                          (1024, 1024, 3072, AL.add),
                                          (2048, 0, 2048, AL.subtract),
                                          (3072, 1024, 3072, AL.subtract)):
                    e.tensor_tensor(
                        out=rb[:, do:do + 1024],
                        in0=fmt[:, 4096 * slot + i0:
                                4096 * slot + i0 + 1024],
                        in1=fmt[:, 4096 * slot + i1:
                                4096 * slot + i1 + 1024], op=op2)
                    e.drain()
                return e.nop()
            bfD_done.append(o("vector", fb2,
                              waits=[(s_rD, sq_v[tp]),
                                     (s_peU, peU_done[-1]),
                                     (s_peD, peD_done[tt])], inc=s_bfD))
            tpb(fmt, 4096 * slot, 128, rbufD(tp), 0, 128, 32,
                [(s_bfD, bfD_done[-1])])
            lhsT_t[tp] = TP.mark()

        def d3b(tp):
            slot = (2 * pr + tp) % 2
            for j1 in range(4):
                bidx = 2 * (j1 % 2)

                def fmm(e, slot=slot, j1=j1, bidx=bidx):
                    last = None
                    for ki in range(8):
                        for og in range(2):
                            last = e.matmul(
                                _bk(ps, bidx + og),
                                fmt[:, 4096 * slot +
                                    128 * (8 * j1 + ki):
                                    4096 * slot +
                                    128 * (8 * j1 + ki) + 128],
                                h_sb[:, 1024 * ki + 512 * og:
                                     1024 * ki + 512 * (og + 1)],
                                start=(ki == 0), stop=(ki == 7))
                    return last
                wl = [lhsT_t[tp]]
                n = len(peH2_done)
                if n >= 2:
                    wl.append((s_zD, zD_done[n - 2]))
                peH2_done.append(o("tensor", fmm, waits=wl, inc=s_peH2,
                                   kind="m"))
                wl = [(s_peH2, peH2_done[-1])]
                if j1 == 0 and len(aq2_done) > 0:
                    wl.append((s_aq2, aq2_done[-1]))
                zD_done.append(o("scalar", lambda e, j1=j1, bidx=bidx:
                                 e.activation(
                                   out=zq[:, 1024 * j1:1024 * (j1 + 1)],
                                   in_=_bk(ps, bidx, 2), func=AF.Copy),
                                 waits=wl, inc=s_zD))
            peH2_t[tp] = peH2_done[-1]
            zD_t[tp] = zD_done[-1]

        def d3c(tp):
            tt = 2 * pr + tp
            slot = tt % 2

            def faq2(e, tt=tt, tp=tp):
                e.reduce_max(out=dnv[:, 0:1], in_=zq[:, :], axis=AX,
                             apply_absolute_value=True)
                e.drain()
                e.tensor_scalar(out=dnv[:, 1:2], in0=dnv[:, 0:1],
                                scalar1=EPS_Z, scalar2=None, op0=AL.max)
                e.drain()
                e.reciprocal(out=dnv[:, 2:3], in_=dnv[:, 1:2])
                e.drain()
                e.tensor_scalar(out=dnv[:, 2:3], in0=dnv[:, 2:3],
                                scalar1=127.0, scalar2=None, op0=AL.mult)
                e.drain()
                e.scalar_tensor_tensor(out=cfv2[:, tt:tt + 1],
                                       in0=dnv[:, 1:2], scalar=C_DN,
                                       in1=sc_bc[:, 1:2], op0=AL.mult,
                                       op1=AL.mult)
                e.drain()
                for c in range(2):
                    e.tensor_scalar(out=qi32[:, 0:2048],
                                    in0=zq[:, 2048 * c:2048 * (c + 1)],
                                    scalar1=dnv[:, 2:3], scalar2=None,
                                    op0=AL.mult)
                    e.drain()
                    e.tensor_scalar(
                        out=rbufD(tp)[:, 2048 * c:2048 * (c + 1)],
                        in0=qi32[:, 0:2048], scalar1=0.0, scalar2=None,
                        op0=AL.add)
                    e.drain()
                return e.drain()
            aq2_done.append(o("vector", faq2,
                              waits=[(s_zD, zD_t[tp])], inc=s_aq2))
            tpb(fmt, 4096 * slot, 128, rbufD(tp), 0, 128, 32,
                [(s_aq2, aq2_done[-1]), (s_peH2, peH2_t[tp])])
            x2T_done.append(TP.mark())

            def fst(e, slot=slot, tt=tt):
                ia = bass.AP(fmt, 4096 * slot,
                             [[8192, P], [128, 32], [1, 128]])
                oa = bass.AP(xq2_d, 128 * tt,
                             [[1024, P], [131072, 32], [1, 128]])
                return e.dma_start(out=oa, in_=ia)
            xqst_done.append(o("sync", fst, waits=x2T_done[-1],
                               inc=s_xqst, k=16, kind="d"))

        d3a(0)
        d3a(1)
        d3b(0)
        d3c(0)
        d3b(1)
        d3c(1)
    lhDn_done, wDn_done, peDn_done, evD_done = [], [], [], []
    for kq in range(2):
        def fld(e, kq=kq):
            ia = bass.AP(xq2_d, 2048 * kq * T,
                         [[1024, P], [131072, 16], [1, 1024]])
            oa = bass.AP(big_a, 0, [[16384, P], [1024, 16], [1, 1024]])
            return e.dma_start(out=oa, in_=ia)
        wl = [(s_xqst, xqst_done[-1]), (s_peH2, peH2_done[-1]),
              (s_peU, peU_done[-1]), (s_peD, peD_done[-1])]
        wl += x1T_done[-1]
        if kq == 1:
            wl.append((s_peDn, peDn_done[-1]))
        lhDn_done.append(o("sync", fld, waits=wl, inc=s_lhDn, k=16,
                           kind="d"))
        for og in range(4):
            j = 4 * kq + og
            wl = [(s_peC, PECF), (s_pdn, 1)]
            if j >= 2:
                wl.append((s_peDn, peDn_done[8 * (j - 1) - 1]))
            wDn_done.append(wslice_load(og % 2, qdn, 512 * og, NKD, 512,
                                        s_wDn, wl, row0=2048 * kq,
                                        cast=False))
            for tt in range(NT):
                def fmm(e, og=og, tt=tt):
                    last = None
                    for ki in range(NKD):
                        last = e.matmul(
                            _bk(ps, tt % 2),
                            big_a[:, 1024 * ki + 128 * tt:
                                  1024 * ki + 128 * (tt + 1)],
                            wbuf[:, 8192 * (og % 2) + 512 * ki:
                                 8192 * (og % 2) + 512 * (ki + 1)],
                            start=(ki == 0), stop=(ki == NKD - 1))
                    return last
                wl = [(s_lhDn, lhDn_done[kq]), (s_wDn, wDn_done[j]),
                      (s_aq2, aq2_done[-1])]
                n = len(peDn_done)
                if n >= 2:
                    wl.append((s_evD, evD_done[n - 2]))
                peDn_done.append(o("tensor", fmm, waits=wl, inc=s_peDn,
                                   kind="m"))
                evD_done.append(o(
                    "vector", lambda e, og=og, tt=tt:
                    e.scalar_tensor_tensor(
                        out=x_sb[:, D * tt + 512 * og:
                                 D * tt + 512 * (og + 1)],
                        in0=_bk(ps, tt % 2), scalar=cfv2[:, tt:tt + 1],
                        in1=x_sb[:, D * tt + 512 * og:
                                 D * tt + 512 * (og + 1)],
                        op0=AL.mult, op1=AL.add),
                    waits=[(s_peDn, peDn_done[-1])], inc=s_evD))
    EVDF = evD_done[-1]
    PEDNF = peDn_done[-1]

    if stop == "D":
        _finish(nc, em, o, st, x_sb, y_out, [(s_evD, EVDF)])
        em.emit()
        return nc

    s_wE = st("s_wE")
    s_peE = st("s_peE")
    s_evE = st("s_evE")
    s_vkE = st("s_vkE")
    s_ccE = st("s_ccE")
    s_wQ = st("s_wQ")
    s_peQ = st("s_peQ")
    s_evQ = st("s_evQ")
    s_peO = st("s_peO")
    s_evO = st("s_evO")
    s_wP = st("s_wP")
    s_peP = st("s_peP")
    s_evP = st("s_evP")
    s_bp = st("s_bp")

    lhsE_done = []
    for tt in range(NT):
        pre = [(s_evD, EVDF)]
        if tt >= 2:
            pre.append(lhsE_done[tt - 2])
        s_l, lv = ln_chain(tt, tt % 2, pre)
        tpb(big_a, 128 * tt, 1024, lnt, D * (tt % 2), 128, NKD,
            [(s_l, lv), (s_lhDn, lhDn_done[-1]), (s_peDn, PEDNF)])
        lhsE_done.append(TP.mark())
    LHE = lhsE_done[-1]

    wE_done, peE_done, evE_done, vkE_done, peO_done = [], [], [], [], []
    s_evEv = st("s_evEv")
    for pr in range(4):
        for p2 in range(2):
            for og in range(4):
                j = (pr * 2 + p2) * 4 + og
                wl = [(s_peDn, PEDNF)]
                if j >= 2:
                    wl.append((s_peE, peE_done[2 * (j - 2) + 1]))
                wE_done.append(wslice_load(
                    og % 2, wkvT, 2048 * p2 + 512 * og, NKD, 512, s_wE,
                    wl))
                for tp in range(2):
                    tt = 2 * pr + tp

                    def fmm(e, og=og, tt=tt):
                        last = None
                        for ki in range(NKD):
                            last = e.matmul(
                                _bk(ps, tt % 2),
                                big_a[:, 1024 * ki + 128 * tt:
                                      1024 * ki + 128 * (tt + 1)],
                                wbuf[:, 8192 * (og % 2) + 512 * ki:
                                     8192 * (og % 2) + 512 * (ki + 1)],
                                start=(ki == 0), stop=(ki == NKD - 1))
                        return last
                    wl = [LHE, (s_wE, wE_done[-1]),
                          (s_evD, EVDF)]
                    n = len(peE_done)
                    if n >= 2:
                        wl.append(evE_done[n - 2])
                    if pr >= 1 and p2 == 0 and og == 0:
                        wl.append((s_vkE, vkE_done[-1]))
                    peE_done.append(o("tensor", fmm, waits=wl, inc=s_peE,
                                      kind="m"))
                    if p2 == 0:
                        evE_done.append((s_evE, o(
                            "scalar", lambda e, tp=tp, og=og, tt=tt:
                            e.activation(
                                out=ktm2[:, 2048 * tp + 512 * og:
                                         2048 * tp + 512 * (og + 1)],
                                in_=_bk(ps, tt % 2), func=AF.Relu),
                            waits=[(s_peE, peE_done[-1])], inc=s_evE)))
                    else:
                        def fv(e, tp=tp, og=og, tt=tt):
                            oa = bass.AP(vaug, 2112 * tp + 528 * og,
                                         [[4224, P], [33, 16], [1, 32]])
                            return e.tensor_copy(out=oa,
                                                 in_=_bk(ps, tt % 2))
                        evE_done.append((s_evEv, o(
                            "vector", fv,
                            waits=[(s_peE, peE_done[-1]), (s_ini, VAUG1)],
                            inc=s_evEv)))
        for tp in range(2):
            for rnd in range(5):
                h0 = 15 * rnd
                nh = min(15, 64 - h0)
                bidx = len(peO_done) % 2

                def fvk(e, tp=tp, h0=h0, nh=nh, bidx=bidx):
                    last = None
                    for hl in range(nh):
                        h = h0 + hl
                        last = e.matmul(
                            ps[0:32, 512 * bidx + 33 * hl:
                               512 * bidx + 33 * hl + 33],
                            ktm2[:, 2048 * tp + 32 * h:
                                 2048 * tp + 32 * (h + 1)],
                            vaug[:, 2112 * tp + 33 * h:
                                 2112 * tp + 33 * (h + 1)],
                            start=(hl == 0), stop=(hl == nh - 1))
                    return last
                wl = [evE_done[-1]]
                n = len(peO_done)
                if n >= 2:
                    wl.append((s_vkE, vkE_done[n - 2]))
                peO_done.append(o("tensor", fvk, waits=wl, inc=s_peO,
                                  kind="m"))
                vkE_done.append(o(
                    "vector", lambda e, h0=h0, nh=nh, bidx=bidx:
                    e.tensor_tensor(
                        out=vkacc[0:32, 33 * h0:33 * (h0 + nh)],
                        in0=vkacc[0:32, 33 * h0:33 * (h0 + nh)],
                        in1=ps[0:32, 512 * bidx:512 * bidx + 33 * nh],
                        op=AL.add),
                    waits=[(s_peO, peO_done[-1]), (s_qv, VK0)],
                    inc=s_vkE))
    VKF = vkE_done[-1]

    s_arE = st("s_arE")
    e1 = o("gpsimd", lambda e: e.dma_start(out=vk_in[:, :],
                                           in_=vkacc[:, :]),
           waits=[(s_vkE, VKF)], inc=s_ccE, k=16, kind="d")
    e2 = o("gpsimd", lambda e: e.collective_compute(
        "AllReduce", AL.add, replica_groups=grp4, ins=[vk_in.ap().opt()],
        outs=[vk_out.ap().opt()]), waits=[(s_ccE, e1)], inc=s_arE,
        kind="d")
    ms = o("gpsimd", lambda e: e.memset(vkT[:, :], 0.0),
           waits=[(s_vkE, VKF), (s_arE, e2)], inc=s_arE)
    for j2 in range(4):
        e3 = o("gpsimd", lambda e, j2=j2: e.dma_start(
            out=vkT[32 * j2:32 * (j2 + 1), :].rearrange(
                "p (c e) -> p c e", e=132)[:, :, 33 * j2:33 * (j2 + 1)],
            in_=bass.AP(vk_out, 33 * j2, [[2112, 32], [132, 16], [1, 33]])),
            waits=[(s_arE, ms)], inc=s_ccE, k=16, kind="d")
    CCEF = s_ccE.n

    wQ_done, peQ_done, evQ_done = [], [], []
    for oc in range(NKD):
        wl = [(s_vkE, VKF)]
        if oc >= 2:
            wl.append((s_peQ, peQ_done[2 * (oc - 2) + 1]))
        wQ_done.append(o("gpsimd", lambda e, oc=oc: e.dma_start(
            out=wt_sb[:, 2048 * (oc % 2):2048 * (oc % 2) + 2048],
            in_=wq1_d[:, oc, :]), waits=wl, inc=s_wQ, k=16, kind="d"))
        for g2 in range(2):
            bidx = 4 + 2 * (oc % 2) + g2

            def fmm(e, oc=oc, g2=g2, bidx=bidx):
                last = None
                for ki in range(NKD):
                    last = e.matmul(
                        _bk(ps, bidx),
                        wt_sb[:, 2048 * (oc % 2) + 128 * ki:
                              2048 * (oc % 2) + 128 * (ki + 1)],
                        big_a[:, 1024 * ki + 512 * g2:
                              1024 * ki + 512 * (g2 + 1)],
                        start=(ki == 0), stop=(ki == NKD - 1))
                return last
            wl = [(s_wQ, wQ_done[oc]), (s_aqD, aq2_done[-1] if False
                                        else s_aqD.n)]
            if oc == 0 and g2 == 0:
                wl.append((s_aq2, aq2_done[-1]))
            if oc >= 2:
                wl.append((s_evQ, evQ_done[2 * (oc - 2) + 1]))
            peQ_done.append(o("tensor", fmm, waits=wl, inc=s_peQ,
                              kind="m"))
            evQ_done.append(o("scalar", lambda e, oc=oc, g2=g2, bidx=bidx:
                              e.activation(
                                out=big_b[:, 1024 * oc + 512 * g2:
                                          1024 * oc + 512 * (g2 + 1)],
                                in_=_bk(ps, bidx), func=AF.Relu),
                              waits=[(s_peQ, peQ_done[-1])], inc=s_evQ))
    EVQF = evQ_done[-1]

    peO2_done, evO_done, lhsP_done = [], [], []
    for tt in range(NT):
        for c in range(NKD):
            i2 = tt * NKD + c
            bidx = i2 % 2

            def fo(e, tt=tt, c=c, bidx=bidx):
                return e.matmul(
                    ps[:, 512 * bidx:512 * bidx + 132],
                    big_b[:, 1024 * c + 128 * tt:1024 * c + 128 * (tt + 1)],
                    vkT[:, 132 * c:132 * (c + 1)],
                    start=True, stop=True)
            wl = [(s_evQ, EVQF), (s_ccE, CCEF), (s_vkE, VKF)]
            if i2 >= 2:
                wl.append((s_evO, evO_done[i2 - 2]))
            peO2_done.append(o("tensor", fo, waits=wl, inc=s_peO,
                               kind="m"))

            def fdiv(e, tt=tt, c=c, bidx=bidx):
                e.tensor_scalar(
                    out=rcd[:, 0:4],
                    in0=bass.AP(ps, 512 * bidx + 32,
                                [[4096, P], [33, 4]]),
                    scalar1=1e-15, scalar2=None, op0=AL.add)
                e.drain()
                e.reciprocal(out=rcd[:, 0:4], in_=rcd[:, 0:4])
                e.drain()
                e.tensor_tensor(
                    out=lnt[:, D * (tt % 2) + 128 * c:
                            D * (tt % 2) + 128 * (c + 1)],
                    in0=bass.AP(ps, 512 * bidx,
                                [[4096, P], [33, 4], [1, 32]]),
                    in1=bass.AP(rcd, 0, [[4, P], [1, 4], [0, 32]]),
                    op=AL.mult)
                e.drain()
                return e.drain()
            wl = [(s_peO, peO2_done[-1])]
            if tt >= 2 and c == 0:
                wl.append(lhsP_done[tt - 2])
            evO_done.append(o("vector", fdiv, waits=wl, inc=s_evO))
        tpb(big_a, 128 * tt, 1024, lnt, D * (tt % 2), 128, NKD,
            [(s_evO, evO_done[-1]), (s_peQ, peQ_done[-1])] + LHE)
        lhsP_done.append(TP.mark())
    LHP = lhsP_done[-1]

    s_bpv = st("s_bpv")
    p1 = o("gpsimd", lambda e: e.dma_start(
        out=bc_sb[:, :], in_=bass.AP(bpj_row, 0, [[0, P], [1, D]])),
        waits=[(s_bov, bo_done[-1]), (s_evC, EVCF)], inc=s_bp, k=16,
        kind="d")
    bp_done = []
    for tt in range(NT):
        bp_done.append(o("vector", lambda e, tt=tt: e.tensor_tensor(
            out=x_sb[:, D * tt:D * (tt + 1)],
            in0=x_sb[:, D * tt:D * (tt + 1)], in1=bc_sb[:, :], op=AL.add),
            waits=[(s_bp, p1), (s_evD, EVDF)] + lhsE_done[tt],
            inc=s_bpv))

    wP_done, peP_done, evP_done = [], [], []
    for og in range(4):
        wl = [LHP]
        if og >= 2:
            wl.append((s_peP, peP_done[8 * (og - 1) - 1]))
        wP_done.append(wslice_load(og % 2, wprojT, 512 * og, NKD, 512,
                                   s_wP, wl))
        for tt in range(NT):
            def fmm(e, og=og, tt=tt):
                last = None
                for ki in range(NKD):
                    last = e.matmul(
                        _bk(ps, tt % 2),
                        big_a[:, 1024 * ki + 128 * tt:
                              1024 * ki + 128 * (tt + 1)],
                        wbuf[:, 8192 * (og % 2) + 512 * ki:
                             8192 * (og % 2) + 512 * (ki + 1)],
                        start=(ki == 0), stop=(ki == NKD - 1))
                return last
            wl = [(s_wP, wP_done[og]), (s_peO, peO2_done[-1])] + LHP
            n = len(peP_done)
            if n >= 2:
                wl.append((s_evP, evP_done[n - 2]))
            peP_done.append(o("tensor", fmm, waits=wl, inc=s_peP,
                              kind="m"))
            evP_done.append(o(
                "vector", lambda e, og=og, tt=tt: e.scalar_tensor_tensor(
                    out=x_sb[:, D * tt + 512 * og:
                             D * tt + 512 * (og + 1)],
                    in0=_bk(ps, tt % 2), scalar=1.0,
                    in1=x_sb[:, D * tt + 512 * og:
                             D * tt + 512 * (og + 1)],
                    op0=AL.mult, op1=AL.add),
                waits=[(s_peP, peP_done[-1]), (s_bpv, bp_done[tt])],
                inc=s_evP))
    EVPF = evP_done[-1]
    PEPF = peP_done[-1]

    if stop == "E":
        _finish(nc, em, o, st, x_sb, y_out, [(s_evP, EVPF)])
        em.emit()
        return nc

    s_wF = st("s_wF")
    s_peF = st("s_peF")
    s_rF = st("s_rF")
    s_qF = st("s_qF")
    s_sqF = st("s_sqF")
    s_khst = st("s_khst")
    s_lhG = st("s_lhG")
    s_wG = st("s_wG")
    s_peG = st("s_peG")
    s_evG = st("s_evG")

    rbufF = big_a
    lnF_done, wF_done, peF_done, rF_done = [], [], [], []
    qF_done, khst_done = [], []
    for pr in range(4):
        for tp in range(2):
            tt = 2 * pr + tp
            pre = [(s_evP, EVPF)]
            if tt >= 2:
                pre.append(lnF_done[tt - 2])
            s_l, lv = ln_chain(tt, tt % 2, pre)
            wl = [(s_l, lv), (s_peP, PEPF)]
            if pr >= 1:
                wl.append((s_peF, peF_done[-2] if tp == 0
                           else peF_done[-1]))
            tpb(big_b, 128 * tp, 256, lnt, D * (tt % 2), 128, NKD, wl)
            lnF_done.append(TP.mark())
        for og in range(16):
            j = 16 * pr + og
            wl = [(s_pcf, 1), (s_peP, PEPF)]
            if j >= 2:
                wl.append((s_peF, peF_done[2 * (j - 2) + 1]))
            wF_done.append(wslice_load(og % 2, qcf, 512 * og, NKD,
                                       512, s_wF, wl, cast=False))
            for tp in range(2):
                tt = 2 * pr + tp

                def fmm(e, og=og, tp=tp):
                    last = None
                    for ki in range(NKD):
                        last = e.matmul(
                            _bk(ps, 2 * (og % 2) + tp),
                            big_b[:, 256 * ki + 128 * tp:
                                  256 * ki + 128 * (tp + 1)],
                            wbuf[:, 8192 * (og % 2) + 512 * ki:
                                 8192 * (og % 2) + 512 * (ki + 1)],
                            start=(ki == 0), stop=(ki == NKD - 1))
                    return last
                wl = [lnF_done[-1], (s_wF, wF_done[-1])]
                n = len(peF_done)
                if n >= 2:
                    wl.append((s_rF, rF_done[n - 2]))
                if pr >= 1 and og == 0:
                    wl.append((s_khst, khst_done[-3] if tp == 0
                               else khst_done[-1]))
                peF_done.append(o("tensor", fmm, waits=wl, inc=s_peF,
                                  kind="m"))
                rF_done.append(o(
                    "scalar", lambda e, tp=tp, og=og: e.activation(
                        out=rbufF[:, 8192 * tp + 512 * og:
                                  8192 * tp + 512 * (og + 1)],
                        in_=_bk(ps, 2 * (og % 2) + tp), func=AF.Relu,
                        scale=sc_bc[:, 2:3]),
                    waits=[(s_peF, peF_done[-1]), (s_c0v, SCL)],
                    inc=s_rF))
        for tp in range(2):
            tt = 2 * pr + tp

            def fq(e, tp=tp, tt=tt):
                e.reduce_max(out=dnv[:, 0:1],
                             in_=rbufF[:, 8192 * tp:8192 * (tp + 1)],
                             axis=AX, apply_absolute_value=True)
                e.drain()
                e.tensor_tensor(out=dnv[:, 0:1], in0=dnv[:, 0:1],
                                in1=dnv[:, 0:1], op=AL.mult)
                e.drain()
                e.tensor_scalar(out=dnv[:, 1:2], in0=dnv[:, 0:1],
                                scalar1=1e-5, scalar2=None, op0=AL.max)
                e.drain()
                e.reciprocal(out=dnv[:, 2:3], in_=dnv[:, 1:2])
                e.drain()
                e.tensor_scalar(out=dnv[:, 2:3], in0=dnv[:, 2:3],
                                scalar1=255.0, scalar2=None, op0=AL.mult)
                e.drain()
                e.scalar_tensor_tensor(out=cfv[:, tt:tt + 1],
                                       in0=dnv[:, 1:2],
                                       scalar=1.0 / 255.0,
                                       in1=sc_bc[:, 3:4], op0=AL.mult,
                                       op1=AL.mult)
                e.drain()
                return e.drain()
            qF_done.append(o("vector", fq, waits=[(s_rF, rF_done[-1])],
                             inc=s_qF))
            base_q = qF_done[-1]
            sq_done = []
            for c in range(8):
                hb = 1024 * (c % 2)
                wl = [(s_qF, base_q)] if c < 2 else [(s_qF, qF_done[-1])]
                sq_done.append(o("scalar", lambda e, tp=tp, c=c, hb=hb:
                                 e.activation(
                                   out=y32[:, hb:hb + 1024],
                                   in_=rbufF[:, 8192 * tp + 1024 * c:
                                             8192 * tp + 1024 * (c + 1)],
                                   func=AF.Square),
                                 waits=wl, inc=s_sqF))

                def fr(e, tp=tp, c=c, hb=hb):
                    e.tensor_scalar(out=qi32[:, hb:hb + 1024],
                                    in0=y32[:, hb:hb + 1024],
                                    scalar1=dnv[:, 2:3], scalar2=None,
                                    op0=AL.mult)
                    e.drain()
                    return e.tensor_scalar(
                        out=rbufF[:, 8192 * tp + 1024 * c:
                                  8192 * tp + 1024 * (c + 1)],
                        in0=qi32[:, hb:hb + 1024], scalar1=0.0,
                        scalar2=None, op0=AL.add)
                qF_done.append(o("vector", fr,
                                 waits=[(s_sqF, sq_done[-1])], inc=s_qF))
            for wv in range(2):
                wl = [(s_qF, qF_done[-1] if wv == 1 else qF_done[-5])]
                if len(khst_done) >= 1:
                    wl.append((s_khst, khst_done[-1]))
                tpb(zq, 0, 128, rbufF, 8192 * tp + 4096 * wv, 128, 32, wl)

                def fst(e, tt=tt, wv=wv):
                    ia = bass.AP(zq, 0, [[4096, P], [128, 32], [1, 128]])
                    oa = bass.AP(khat_d, 4096 * wv * T + 128 * tt,
                                 [[1024, P], [131072, 32], [1, 128]])
                    return e.dma_start(out=oa, in_=ia)
                khst_done.append(o("sync", fst, waits=TP.mark(),
                                   inc=s_khst, k=16, kind="d"))
    lhG_done, wG_done, peG_done, evG_done = [], [], [], []
    for kq in range(4):
        def fld(e, kq=kq):
            ia = bass.AP(khat_d, 2048 * kq * T,
                         [[1024, P], [131072, 16], [1, 1024]])
            oa = bass.AP(big_a, 0, [[16384, P], [1024, 16], [1, 1024]])
            return e.dma_start(out=oa, in_=ia)
        wl = [(s_khst, khst_done[-1]), (s_peF, peF_done[-1]),
              (s_qF, qF_done[-1]), (s_rF, rF_done[-1])]
        if kq >= 1:
            wl.append((s_peG, peG_done[-1]))
        lhG_done.append(o("sync", fld, waits=wl, inc=s_lhG, k=16,
                          kind="d"))
        for og in range(4):
            j = 4 * kq + og
            wl = [(s_peP, PEPF), (s_pcp, 1)]
            if j >= 2:
                wl.append((s_peG, peG_done[8 * (j - 1) - 1]))
            wG_done.append(wslice_load(og % 2, qcp, 512 * og, NKD, 512,
                                       s_wG, wl, row0=2048 * kq,
                                       cast=False))
            for tt in range(NT):
                def fmm(e, og=og, tt=tt):
                    last = None
                    for ki in range(NKD):
                        last = e.matmul(
                            _bk(ps, tt % 2),
                            big_a[:, 1024 * ki + 128 * tt:
                                  1024 * ki + 128 * (tt + 1)],
                            wbuf[:, 8192 * (og % 2) + 512 * ki:
                                 8192 * (og % 2) + 512 * (ki + 1)],
                            start=(ki == 0), stop=(ki == NKD - 1))
                    return last
                wl = [(s_lhG, lhG_done[kq]), (s_wG, wG_done[j]),
                      (s_evP, EVPF)]
                n = len(peG_done)
                if n >= 2:
                    wl.append((s_evG, evG_done[n - 2]))
                peG_done.append(o("tensor", fmm, waits=wl, inc=s_peG,
                                  kind="m"))
                evG_done.append(o(
                    "vector", lambda e, og=og, tt=tt:
                    e.scalar_tensor_tensor(
                        out=x_sb[:, D * tt + 512 * og:
                                 D * tt + 512 * (og + 1)],
                        in0=_bk(ps, tt % 2), scalar=cfv[:, tt:tt + 1],
                        in1=x_sb[:, D * tt + 512 * og:
                                 D * tt + 512 * (og + 1)],
                        op0=AL.mult, op1=AL.add),
                    waits=[(s_peG, peG_done[-1])], inc=s_evG))

    _finish(nc, em, o, st, x_sb, y_out, [(s_evG, evG_done[-1])])
    em.emit()
    return nc


def _finish(nc, em, o, st, x_sb, y_out, waits, dumps=()):
    s_out = st("s_out")
    vals = []
    for tt in range(NT):
        vals.append(o("sync", lambda e, tt=tt: e.dma_start(
            out=y_out[P * tt:P * (tt + 1), :],
            in_=x_sb[:, D * tt:D * (tt + 1)]), waits=waits, inc=s_out,
            k=16, kind="d"))
    for (name, sb) in dumps:
        dshape = [sb.shape[0], sb.shape[1]]
        dten = nc.dram_tensor("dbg_" + name, dshape, sb.dtype,
                              kind="ExternalOutput")
        vals.append(o("sync", lambda e, dten=dten, sb=sb: e.dma_start(
            out=dten[:, :], in_=sb[:, :]), waits=waits, inc=s_out,
            k=16, kind="d"))
    o("sync", None, waits=[(s_out, vals[-1])])


def _hadamard(n):
    h = np.array([[1.0]], dtype=np.float32)
    while h.shape[0] < n:
        h = np.block([[h, h], [h, -h]])
    return h


def _dev_wstat(wT):
    K, O = wT.shape
    nki, noc = K // P, O // P
    w = wT.reshape(nki, P, noc, P).transpose(1, 2, 0, 3).reshape(
        P, noc, nki * P)
    return np.ascontiguousarray(w)


_NC_CACHE = {}


def kernel(L0_attn_x, L0_attn_Wq, L0_attn_bq, L0_attn_Wk, L0_attn_bk,
           L0_attn_Wv, L0_attn_bv, L0_attn_Wout, L0_attn_bout,
           L0_ffn_w_up, L0_ffn_w_down, L1_attn_w_qkv, L1_attn_w_proj,
           L1_attn_b_proj, L1_ffn_c_fc, L1_ffn_c_proj):
    f32 = np.float32
    x = np.ascontiguousarray(np.asarray(L0_attn_x, f32).reshape(2 * SB, D))
    wqT = np.asarray(L0_attn_Wq, f32).T
    wkT = np.asarray(L0_attn_Wk, f32).T
    wvT = np.asarray(L0_attn_Wv, f32).T
    woT = np.ascontiguousarray(np.asarray(L0_attn_Wout, f32).T)
    wqkvT = np.asarray(L1_attn_w_qkv, f32).T
    wq1T = wqkvT[:, 0:D]
    wkvT = np.ascontiguousarray(wqkvT[:, D:3 * D])
    wprojT = np.ascontiguousarray(np.asarray(L1_attn_w_proj, f32).T)
    w_upT = np.ascontiguousarray(np.asarray(L0_ffn_w_up, f32).T)
    w_dnT = np.ascontiguousarray(np.asarray(L0_ffn_w_down, f32).T)
    cfc = np.ascontiguousarray(np.asarray(L1_ffn_c_fc, f32))
    cpjT = np.ascontiguousarray(np.asarray(L1_ffn_c_proj, f32).T)

    wq_d = _dev_wstat(wqT)
    wk_d = _dev_wstat(wkT)
    wv_d = _dev_wstat(wvT)
    wq1_d = _dev_wstat(np.ascontiguousarray(wq1T))

    def fmb(b):
        return np.ascontiguousarray(np.asarray(b, f32).reshape(NKD, P).T)

    h1024 = _hadamard(1024).astype(ml_dtypes.bfloat16)

    key = DEBUG_STOP
    if key not in _NC_CACHE:
        _NC_CACHE[key] = build_program(stop=DEBUG_STOP)
    nc = _NC_CACHE[key]

    in_maps = []
    for c in range(8):
        in_maps.append({
            "x_in": np.ascontiguousarray(x[T * c:T * (c + 1)]),
            "wq_d": wq_d, "wk_d": wk_d, "wv_d": wv_d, "wq1_d": wq1_d,
            "woT": woT, "wkvT": wkvT, "wprojT": wprojT, "h1024": h1024,
            "bq_in": fmb(L0_attn_bq), "bk_in": fmb(L0_attn_bk),
            "bv_in": fmb(L0_attn_bv),
            "bo_row": np.ascontiguousarray(
                np.asarray(L0_attn_bout, f32)[None, :]),
            "bpj_row": np.ascontiguousarray(
                np.asarray(L1_attn_b_proj, f32)[None, :]),
            "shup": np.ascontiguousarray(w_upT[256 * c:256 * (c + 1)]),
            "shdn": np.ascontiguousarray(w_dnT[512 * c:512 * (c + 1)]),
            "shcf": np.ascontiguousarray(cfc[256 * c:256 * (c + 1)]),
            "shcp": np.ascontiguousarray(cpjT[1024 * c:1024 * (c + 1)]),
        })
    res = run_bass_kernel_spmd(nc, in_maps, core_ids=list(range(8)))
    out = np.concatenate([res.results[c]["y_out"] for c in range(8)],
                         axis=0)
    return out.reshape(2, SB, D)


# version 13
# speedup vs baseline: 1.0216x; 1.0077x over previous
import numpy as np
import ml_dtypes
import concourse.bass as bass
import concourse.mybir as mybir
from concourse.bass_utils import run_bass_kernel_spmd

F32 = mybir.dt.float32
BF16 = mybir.dt.bfloat16
I32 = mybir.dt.int32
AL = mybir.AluOpType
AF = mybir.ActivationFunctionType
AX = mybir.AxisListType.X

P = 128
T = 1024
NT = 8
D = 2048
NKD = 16
DI = 4096
H2 = 8192
SB = 4096
NKC = 32
RS128 = float(1.0 / np.sqrt(128.0))
C_UP = float(1.0 / (127.0 * np.sqrt(2048.0)))
C_DN = float(1.0 / (127.0 * np.sqrt(4096.0)))
EPS_U = float(1e-5 * np.sqrt(2048.0))
EPS_Z = float(1e-5 * np.sqrt(4096.0))
TW_NUMEL = [D * DI, DI * D, D * H2, H2 * D]

DEBUG_STOP = None


class St:
    def __init__(self, nc, name):
        self.sem = nc.alloc_semaphore(name)
        self.n = 0


class Pl:

    def __init__(self, mk, n, name):
        self.sts = [mk("%s_%d" % (name, i)) for i in range(n)]
        self.i = 0

    def nxt(self):
        s = self.sts[self.i % len(self.sts)]
        self.i += 1
        return s

    def mark(self):
        return [(s, s.n) for s in self.sts]


def _flat(waits):
    out = []
    for w in waits:
        if isinstance(w, list):
            out.extend(w)
        else:
            out.append(w)
    return out


class Em:

    def __init__(self, nc):
        self.nc = nc
        self.ops = []

    def o(self, eng, fn=None, waits=(), inc=None, k=1, kind="c"):
        post = None
        waits = _flat(list(waits))
        if inc is not None:
            if kind == "d":
                waits.append((inc, inc.n))
            inc.n += k
            post = inc.n
        self.ops.append((eng, fn, tuple(waits), inc, k, kind, post))
        return post

    def emit(self):
        nc = self.nc
        with nc.Block() as blk:
            def mk(name):
                def f(eng):
                    water = {}
                    prev_c = False
                    for (e, fn, waits, inc, k, kind, post) in self.ops:
                        if e != name:
                            continue
                        for (stm, v) in waits:
                            if v > 0 and v > water.get(stm.sem.num, 0):
                                eng.wait_ge(stm.sem, v)
                                water[stm.sem.num] = v
                        if fn is None:
                            ins = (eng.drain() if name in
                                   ("vector", "scalar", "gpsimd")
                                   else eng.nop())
                        else:
                            if kind == "c" and prev_c and name in (
                                    "vector", "scalar", "gpsimd"):
                                eng.drain()
                            ins = fn(eng)
                        prev_c = (kind == "c" and fn is not None)
                        if inc is not None:
                            ins.then_inc(inc.sem, k)
                            if (kind != "d" and name != "gpsimd"
                                    and post > water.get(inc.sem.num, 0)):
                                water[inc.sem.num] = post
                    return None
                return f
            blk.sync(mk("sync"))
            blk.gpsimd(mk("gpsimd"))
            blk.scalar(mk("scalar"))
            blk.vector(mk("vector"))
            blk.tensor(mk("tensor"))


def _bk(ps, b, n=1):
    return ps[:, 512 * b:512 * (b + n)]


def build_program(stop=None):
    nc = bass.Bass("TRN2", target_bir_lowering=False)
    DT = nc.dram_tensor
    SA = nc.alloc_sbuf_tensor

    x_in = DT("x_in", [T, D], F32, kind="ExternalInput")
    wq_d = DT("wq_d", [P, NKD, D], F32, kind="ExternalInput")
    wk_d = DT("wk_d", [P, NKD, D], F32, kind="ExternalInput")
    wv_d = DT("wv_d", [P, NKD, D], F32, kind="ExternalInput")
    wq1_d = DT("wq1_d", [P, NKD, D], F32, kind="ExternalInput")
    woT = DT("woT", [D, D], F32, kind="ExternalInput")
    wkvT = DT("wkvT", [D, 2 * D], F32, kind="ExternalInput")
    wprojT = DT("wprojT", [D, D], F32, kind="ExternalInput")
    h1024 = DT("h1024", [1024, 1024], BF16, kind="ExternalInput")
    bq_in = DT("bq_in", [P, NKD], F32, kind="ExternalInput")
    bk_in = DT("bk_in", [P, NKD], F32, kind="ExternalInput")
    bv_in = DT("bv_in", [P, NKD], F32, kind="ExternalInput")
    bo_row = DT("bo_row", [1, D], F32, kind="ExternalInput")
    bpj_row = DT("bpj_row", [1, D], F32, kind="ExternalInput")
    shup = DT("shup", [D // 8, DI], F32, kind="ExternalInput")
    shdn = DT("shdn", [DI // 8, D], F32, kind="ExternalInput")
    shcf = DT("shcf", [D // 8, H2], F32, kind="ExternalInput")
    shcp = DT("shcp", [H2 // 8, D], F32, kind="ExternalInput")
    y_out = DT("y_out", [T, D], F32, kind="ExternalOutput")

    ws_in = DT("ws_in", [1, 4], F32)
    ws_out = DT("ws_out", [1, 4], F32)
    qup_in = DT("qup_in", [D // 8, DI], BF16)
    qdn_in = DT("qdn_in", [DI // 8, D], BF16)
    qcf_in = DT("qcf_in", [D // 8, H2], BF16)
    qcp_in = DT("qcp_in", [H2 // 8, D], BF16)
    qup = DT("qup", [D, DI], BF16)
    qdn = DT("qdn", [DI, D], BF16)
    qcf = DT("qcf", [D, H2], BF16)
    qcp = DT("qcp", [H2, D], BF16)
    k_in = DT("k_in", [D, T], BF16)
    v_in = DT("v_in", [D, T], BF16)
    n_in = DT("n_in", [1, NKD], BF16)
    k_all = DT("k_all", [4 * D, T], BF16)
    v_all = DT("v_all", [4 * D, T], BF16)
    n_all = DT("n_all", [4, NKD], BF16)
    vk_in = DT("vk_in", [32, 2112], F32)
    vk_out = DT("vk_out", [32, 2112], F32)
    wsum_d = DT("wsum_d", [P, 4], F32)
    xq2_d = DT("xq2_d", [DI, T], BF16)
    khat_d = DT("khat_d", [H2, T], BF16)

    x_sb = SA("x_sb", [P, NT * D], F32)
    big_a = SA("big_a", [P, 16384], BF16)
    big_b = SA("big_b", [P, 16384], BF16)
    wbuf = SA("wbuf", [P, 16384], BF16)
    wt_sb = SA("wt_sb", [P, 4096], BF16)
    lnt = SA("lnt", [P, 2 * D], BF16)
    bc_sb = SA("bc_sb", [P, D], BF16)
    psb = SA("psb", [P, 2 * T], BF16)
    ktile = SA("ktile", [P, 2 * P], BF16)
    vsb = SA("vsb", [P, 288], BF16)
    vaug = SA("vaug", [P, 2 * 2112], BF16)
    st6 = SA("st6", [P, 24], F32)
    agg = SA("agg", [P, 2], F32)
    c_a = SA("c_a", [P, 4], F32)
    qnm = SA("qnm", [1, NKD], F32)
    knm = SA("knm", [1, NKD], F32)
    nrow = SA("nrow", [1, 4 * NKD], BF16)
    n_bf = SA("n_bf", [1, NKD], BF16)
    ksq = SA("ksq", [P, 512], BF16)
    nmx = SA("nmx", [1, 2], F32)
    biasneg = SA("biasneg", [P, NKD], F32)
    bq_sb = SA("bq_sb", [P, NKD], F32)
    bk_sb = SA("bk_sb", [P, NKD], F32)
    bv_sb = SA("bv_sb", [P, NKD], F32)
    wacc = SA("wacc", [P, 4], F32)
    wtmp = SA("wtmp", [P, 4], F32)
    sred = SA("sred", [1, 512], F32)
    wsrow = SA("wsrow", [1, 8], F32)
    negrow = SA("negrow", [1, NKD], F32)
    negrow_bf = SA("negrow_bf", [1, NKD], BF16)
    sc_bc = SA("sc_bc", [P, 4], F32)
    thr_bc = SA("thr_bc", [P, 4], F32)
    nthr_bc = SA("nthr_bc", [P, 4], F32)
    ones_c = SA("ones_c", [P, 1], BF16)
    ones_r = SA("ones_r", [1, P], BF16)
    ones_r32 = SA("ones_r32", [1, P], F32)
    dnv = SA("dnv", [P, 4], F32)
    cfv = SA("cfv", [P, NT], F32)
    cfv2 = SA("cfv2", [P, NT], F32)
    rcd = SA("rcd", [P, 4], F32)
    mmx = SA("mmx", [P, 4], F32)
    vkacc = SA("vkacc", [32, 2112], F32)

    ps = nc.alloc_psum_tensor("ps", [P, 4096], F32)

    def at(base, off_bytes, name, shape, dt):
        a = nc.lookup_mloc(base).addr
        return nc.alloc_sbuf_tensor_at(name, shape, dt, offset=a + off_bytes)

    qtA = at(wbuf, 0, "qtA", [P, 2048], F32)
    qtB = at(wbuf, 8192, "qtB", [P, 2048], F32)
    tq1 = at(psb, 0, "tq1", [P, 2048], BF16)
    tq2 = at(bc_sb, 0, "tq2", [P, 2048], BF16)
    ksl = at(vaug, 0, "ksl", [P, 2 * 512], BF16)
    fmt = at(big_b, 2 * 4096, "fmt", [P, 8192], BF16)
    zq = at(big_b, 2 * 12288, "zq", [P, 4096], BF16)
    qi32 = at(wt_sb, 0, "qi32", [P, 2048], I32)
    y32 = at(vaug, 0, "y32", [P, 2048], F32)
    vkT = at(vaug, 0, "vkT", [P, 2112], BF16)
    ktm2 = at(wt_sb, 0, "ktm2", [P, 4096], BF16)
    rD0 = at(big_b, 0, "rD0", [P, 4096], BF16)
    rD1 = at(big_a, 2 * 12288, "rD1", [P, 4096], BF16)

    def rbufD(tp):
        return rD0 if tp == 0 else rD1

    em = Em(nc)
    o = em.o
    stD = {}

    def st(name):
        stD[name] = St(nc, name)
        return stD[name]

    TP = Pl(st, 5, "tp")
    KL = Pl(st, 4, "kl")
    VL = Pl(st, 4, "vl")
    KS = Pl(st, 4, "ks")

    def tpb(dst, do_, dbs, src, so_, sbs, nblk, waits, grp=8):
        dw, sw = dst.shape[1], src.shape[1]
        for g0 in range(0, nblk, grp):
            g = min(grp, nblk - g0)

            def f(e, g0=g0, g=g):
                ia = bass.AP(src, so_ + sbs * g0,
                             [[sw, P], [sbs, g], [1, 128]])
                oa = bass.AP(dst, do_ + dbs * g0,
                             [[dw, P], [dbs, g], [1, 128]])
                return e.dma_start_transpose(out=oa, in_=ia)
            o("sync", f, waits=waits, inc=TP.nxt(), k=16, kind="d")

    s_x = st("s_x")
    s_bs = st("s_bs")
    s_ini = st("s_ini")
    s_q0 = st("s_q0")
    s_qv = st("s_qv")
    s_c0 = st("s_c0")

    for tt in range(NT):
        o("sync", lambda e, tt=tt: e.dma_start(
            out=x_sb[:, D * tt:D * (tt + 1)],
            in_=x_in[P * tt:P * (tt + 1), :]), inc=s_x, k=16, kind="d")
    XF = s_x.n
    o("sync", lambda e: e.dma_start(out=bq_sb[:, :], in_=bq_in[:, :]),
      inc=s_bs, k=16, kind="d")
    o("sync", lambda e: e.dma_start(out=bk_sb[:, :], in_=bk_in[:, :]),
      inc=s_bs, k=16, kind="d")
    o("sync", lambda e: e.dma_start(out=bv_sb[:, :], in_=bv_in[:, :]),
      inc=s_bs, k=16, kind="d")
    BSF = s_bs.n
    o("gpsimd", lambda e: e.memset(ones_c[:, :], 1.0))
    o("gpsimd", lambda e: e.memset(ones_r[:, :], 1.0))
    o("gpsimd", lambda e: e.memset(vsb[:, 128:129], 1.0))
    o("gpsimd", lambda e: e.memset(vsb[:, 272:273], 1.0))
    INI = o("gpsimd", lambda e: e.memset(ones_r32[:, :], 1.0), inc=s_ini)
    v0 = o("vector", lambda e: e.memset(wacc[:, :], 0.0), inc=s_qv)
    VK0 = o("vector", lambda e: e.memset(vkacc[:, :], 0.0), inc=s_qv)

    qjobs = []
    for r in range(2):
        for cs in range(2):
            qjobs.append((shup, 128 * r, 2048 * cs, 2048, qup_in, 0))
    for r in range(4):
        qjobs.append((shdn, 128 * r, 0, 2048, qdn_in, 1))
    for r in range(2):
        for cs in range(4):
            qjobs.append((shcf, 128 * r, 2048 * cs, 2048, qcf_in, 2))
    for r in range(8):
        qjobs.append((shcp, 128 * r, 0, 2048, qcp_in, 3))
    NQJ = len(qjobs)

    QL = Pl(st, 2, "ql")
    qv_done = []
    for j, (dr, r0, c0, w, _, wi) in enumerate(qjobs):
        buf = qtA if j % 2 == 0 else qtB
        wl = [] if j < 2 else [(s_qv, qv_done[j - 2])]
        stm = QL.nxt()
        ld = (stm, o("sync", lambda e, dr=dr, r0=r0, c0=c0, w=w, buf=buf:
                     e.dma_start(out=buf[:, 0:w],
                                 in_=dr[r0:r0 + P, c0:c0 + w]),
                     waits=wl, inc=stm, k=16, kind="d"))
        o("vector", lambda e, buf=buf, w=w, wi=wi: e.reduce_sum(
            out=wtmp[:, wi:wi + 1], in_=buf[:, 0:w], axis=AX,
            apply_absolute_value=True), waits=[ld])
        qv_done.append(o("vector", lambda e, wi=wi: e.tensor_tensor(
            out=wacc[:, wi:wi + 1], in0=wacc[:, wi:wi + 1],
            in1=wtmp[:, wi:wi + 1], op=AL.add), inc=s_qv))

    c1 = o("gpsimd", lambda e: e.dma_start(out=wsum_d[:, :], in_=wacc[:, :]),
           waits=[(s_qv, qv_done[-1])], inc=s_c0, k=16, kind="d")
    c2 = o("gpsimd", lambda e: e.dma_start(
        out=sred[:, :], in_=bass.AP(wsum_d, 0, [[512, 1], [1, 512]])),
        waits=[(s_c0, c1)], inc=s_c0, k=16, kind="d")
    for wi in range(4):
        o("vector", lambda e, wi=wi: e.reduce_sum(
            out=wsrow[:, wi:wi + 1],
            in_=bass.AP(sred, wi, [[512, 1], [4, 128]]), axis=AX),
          waits=[(s_c0, c2)])
    s_c0v = st("s_c0v")
    s_ar0 = st("s_ar0")
    c3 = o("vector", None, inc=s_c0v)
    c4 = o("gpsimd", lambda e: e.dma_start(out=ws_in[:, :],
                                           in_=wsrow[:, 0:4]),
           waits=[(s_c0v, c3)], inc=s_c0, k=16, kind="d")
    c5 = o("gpsimd", lambda e: e.collective_compute(
        "AllReduce", AL.add, replica_groups=[list(range(8))],
        ins=[ws_in.ap().opt()], outs=[ws_out.ap().opt()]),
        waits=[(s_c0, c4)], inc=s_ar0, kind="d")
    c6 = o("gpsimd", lambda e: e.dma_start(out=wsrow[:, 4:8],
                                           in_=ws_out[:, :]),
           waits=[(s_ar0, c5)], inc=s_c0, k=16, kind="d")
    for wi in range(4):
        o("vector", lambda e, wi=wi: e.tensor_scalar(
            out=wsrow[:, 4 + wi:5 + wi], in0=wsrow[:, 4 + wi:5 + wi],
            scalar1=1.0 / TW_NUMEL[wi], scalar2=1e-5, op0=AL.mult,
            op1=AL.max), waits=[(s_c0, c6)])
    c7 = o("vector", None, inc=s_c0v)
    c8 = o("gpsimd", lambda e: e.dma_start(out=ws_in[:, :],
                                           in_=wsrow[0:1, 4:8]),
           waits=[(s_c0v, c7)], inc=s_c0, k=16, kind="d")
    c9 = o("gpsimd", lambda e: e.dma_start(
        out=sc_bc[:, :], in_=bass.AP(ws_in, 0, [[0, P], [1, 4]])),
        waits=[(s_c0, c8)], inc=s_c0, k=16, kind="d")

    def fscl(e):
        e.tensor_scalar(out=thr_bc[:, :], in0=sc_bc[:, :],
                        scalar1=0.5, scalar2=None, op0=AL.mult)
        e.drain()
        return e.tensor_scalar(out=nthr_bc[:, :], in0=thr_bc[:, :],
                               scalar1=-1.0, scalar2=None, op0=AL.mult)
    SCL = o("vector", fscl, waits=[(s_c0, c9)], inc=s_c0v)

    s_q2 = st("s_q2")
    s_qs = st("s_qs")
    s_qw = st("s_qw")
    s_qst = st("s_qst")
    s_p0 = st("s_p0")
    s_pdn = st("s_pdn")
    s_pcf = st("s_pcf")
    s_pcp = st("s_pcp")

    qst_done = []
    qw_done = []
    for j, (dr, r0, c0, w, qd, wi) in enumerate(qjobs):
        buf = qtA if j % 2 == 0 else qtB
        wl = [(s_c0v, SCL)] if j < 2 else [(s_qw, qw_done[j - 2])]
        stm2 = QL.nxt()
        ld = (stm2, o("sync", lambda e, dr=dr, r0=r0, c0=c0, w=w, buf=buf:
                      e.dma_start(out=buf[:, 0:w],
                                  in_=dr[r0:r0 + P, c0:c0 + w]),
                      waits=wl, inc=stm2, k=16, kind="d"))

        def fq1(e, buf=buf, w=w, wi=wi):
            e.tensor_scalar(out=tq2[:, 0:w], in0=buf[:, 0:w],
                            scalar1=nthr_bc[:, wi:wi + 1], scalar2=None,
                            op0=AL.is_lt)
            e.drain()
            return e.tensor_scalar(out=tq1[:, 0:w], in0=buf[:, 0:w],
                                   scalar1=thr_bc[:, wi:wi + 1],
                                   scalar2=None, op0=AL.is_gt)
        wl_q = [ld]
        if j >= 1:
            wl_q.append((s_qst, qst_done[j - 1]))
        o("vector", fq1, waits=wl_q)
        qw_done.append(o("vector", lambda e, w=w: e.tensor_tensor(
            out=tq1[:, 0:w], in0=tq1[:, 0:w], in1=tq2[:, 0:w],
            op=AL.subtract), inc=s_qw))
        qst_done.append(o("sync", lambda e, qd=qd, r0=r0, c0=c0, w=w:
                          e.dma_start(out=qd[r0:r0 + P, c0:c0 + w],
                                      in_=tq1[:, 0:w]),
                          waits=[(s_qw, qw_done[-1])], inc=s_qst, k=16,
                          kind="d"))
    QST_ALL = qst_done[-1]

    s_lnc = st("s_lnc")

    def ln_chain(tt, slot, pre):
        s_l = s_lnc
        base = s_l.n
        xt = x_sb[:, D * tt:D * (tt + 1)]

        def f1(e):
            for c in range(4):
                e.bn_stats(out=st6[:, 6 * c:6 * (c + 1)],
                           in_=xt[:, 512 * c:512 * (c + 1)])
                e.drain()
            e.bn_aggr(out=agg[:, :], in_=st6[:, :])
            e.drain()
            return e.tensor_scalar(out=c_a[:, 1:2], in0=agg[:, 1:2],
                                   scalar1=1e-5, scalar2=None, op0=AL.add)
        o("vector", f1, waits=pre, inc=s_l)
        o("scalar", lambda e: e.activation(
            out=c_a[:, 1:2], in_=c_a[:, 1:2], func=AF.Sqrt),
          waits=[(s_l, base + 1)], inc=s_l)

        def f2(e):
            e.reciprocal(out=c_a[:, 2:3], in_=c_a[:, 1:2])
            e.drain()
            e.scalar_tensor_tensor(out=c_a[:, 3:4], in0=agg[:, 0:1],
                                   scalar=-1.0, in1=c_a[:, 2:3],
                                   op0=AL.mult, op1=AL.mult)
            e.drain()
            return e.tensor_scalar(
                out=lnt[:, D * slot:D * (slot + 1)], in0=xt,
                scalar1=c_a[:, 2:3], scalar2=c_a[:, 3:4],
                op0=AL.mult, op1=AL.add)
        o("vector", f2, waits=[(s_l, base + 2)], inc=s_l)
        return (s_l, base + 3)

    def wslice_load(dst_half, src, og_col, n_ki, colw, stream, waits,
                    row0=0, cast=True):
        ncols = src.shape[1]

        def f(e):
            ia = bass.AP(src, row0 * ncols + og_col,
                         [[ncols, P], [P * ncols, n_ki], [1, colw]])
            oa = bass.AP(wbuf, 8192 * dst_half,
                         [[16384, P], [colw, n_ki], [1, colw]])
            return e.dma_start(out=oa, in_=ia)
        return o("gpsimd" if cast else "sync", f, waits=waits, inc=stream,
                 k=16, kind="d")

    s_wA = st("s_wA")
    s_peA = st("s_peA")
    s_evA = st("s_evA")
    s_ccA = st("s_ccA")
    s_kamx = st("s_kamx")
    s_sqA = st("s_sqA")
    s_nmm = st("s_nmm")
    s_nmv = st("s_nmv")
    nmv_done = []

    lhsA_done = []
    LHA0 = None
    for tt in range(NT):
        pre = [(s_x, 16 * (tt + 1))]
        if tt >= 2:
            pre.append(lhsA_done[tt - 2])
        s_l, lv = ln_chain(tt, tt % 2, pre)
        tpb(big_a, 128 * tt, 1024, lnt, D * (tt % 2), 128, NKD,
            [(s_l, lv)])
        lhsA_done.append(TP.mark())
        if tt == 3:
            LHA0 = TP.mark()
    LHA = lhsA_done[-1]

    wproj_srcs = [wk_d, wv_d, wq_d]
    wA_done = []
    for pi in range(3):
        for oc in range(NKD):
            j = pi * NKD + oc
            wl = [] if j < 2 else [(s_peA, 2 * (j - 1))]
            wA_done.append(o(
                "gpsimd", lambda e, pi=pi, oc=oc, j=j: e.dma_start(
                    out=wt_sb[:, 2048 * (j % 2):2048 * (j % 2) + 2048],
                    in_=wproj_srcs[pi][:, oc, :]),
                waits=wl, inc=s_wA, k=16, kind="d"))
    peA_done = []
    evA_done = []
    kst_done = []
    nmv_done = []
    KSK = KSV = None
    for pi in range(3):
        bias = (bk_sb, bv_sb, bq_sb)[pi]
        for oc in range(NKD):
            j = pi * NKD + oc
            for g2 in range(2):
                bidx = 2 * (j % 2) + g2

                def fmm(e, j=j, g2=g2, bidx=bidx):
                    last = None
                    for ki in range(NKD):
                        last = e.matmul(
                            _bk(ps, bidx),
                            wt_sb[:, 2048 * (j % 2) + 128 * ki:
                                  2048 * (j % 2) + 128 * (ki + 1)],
                            big_a[:, 1024 * ki + 512 * g2:
                                  1024 * ki + 512 * (g2 + 1)],
                            start=(ki == 0), stop=(ki == NKD - 1))
                    return last
                wl = [LHA0 if g2 == 0 else LHA, (s_wA, wA_done[j]),
                      (s_bs, BSF)]
                if j >= 2:
                    wl.append((s_evA, evA_done[2 * (j - 2) + 1]))
                peA_done.append(o("tensor", fmm, waits=wl, inc=s_peA,
                                  kind="m"))
                if pi == 2:
                    dst = big_b[:, 1024 * oc + 512 * g2:
                                1024 * oc + 512 * (g2 + 1)]
                    evA_done.append(o(
                        "scalar", lambda e, dst=dst, bidx=bidx, oc=oc,
                        bias=bias: e.activation(
                            out=dst, in_=_bk(ps, bidx), func=AF.Identity,
                            bias=bias[:, oc:oc + 1], scale=1.0),
                        waits=[(s_peA, peA_done[-1])], inc=s_evA))
                else:
                    nksl = len(kst_done)
                    half = nksl % 2
                    dst = ksl[:, 512 * half:512 * (half + 1)]
                    wl2 = [(s_peA, peA_done[-1])]
                    if nksl >= 2:
                        wl2.append(kst_done[nksl - 2])
                    evA_done.append(o(
                        "scalar", lambda e, dst=dst, bidx=bidx, oc=oc,
                        bias=bias: e.activation(
                            out=dst, in_=_bk(ps, bidx), func=AF.Identity,
                            bias=bias[:, oc:oc + 1], scale=1.0),
                        waits=wl2, inc=s_evA))
                    dstd = k_in if pi == 0 else v_in
                    kst_done.append((lambda stm: (stm, o(
                        "sync", lambda e, dstd=dstd, oc=oc, g2=g2, dst=dst:
                        e.dma_start(out=dstd[128 * oc:128 * (oc + 1),
                                             512 * g2:512 * (g2 + 1)],
                                    in_=dst),
                        waits=[(s_evA, evA_done[-1])], inc=stm, k=16,
                        kind="d")))(KS.nxt()))
                if pi != 1:
                    nj = len(nmv_done)
                    wl3 = [(s_evA, evA_done[-1])]
                    if nj >= 1:
                        wl3.append((s_nmm, nj))
                    sqv = o("scalar", lambda e, bidx=bidx: e.activation(
                        out=ksq[:, :], in_=_bk(ps, bidx), func=AF.Square),
                        waits=wl3, inc=s_sqA)
                    nb = 4 + (nj % 2)
                    wlm = [(s_sqA, sqv)]
                    if nj >= 2:
                        wlm.append((s_nmv, nmv_done[nj - 2]))
                    mmv = o("tensor", lambda e, nb=nb: e.matmul(
                        ps[0:1, 512 * nb:512 * nb + 512], ones_c[:, :],
                        ksq[:, :], start=True, stop=True),
                        waits=wlm, inc=s_nmm, kind="m")

                    def fnm(e, nb=nb, oc=oc, pi=pi, first=(g2 == 0)):
                        dstn = knm if pi == 0 else qnm
                        e.reduce_max(out=nmx[:, 0:1],
                                     in_=ps[0:1, 512 * nb:512 * nb + 512],
                                     axis=AX)
                        e.drain()
                        if first:
                            return e.tensor_copy(out=dstn[:, oc:oc + 1],
                                                 in_=nmx[:, 0:1])
                        return e.tensor_tensor(out=dstn[:, oc:oc + 1],
                                               in0=dstn[:, oc:oc + 1],
                                               in1=nmx[:, 0:1], op=AL.max)
                    nmv_done.append(o("vector", fnm, waits=[(s_nmm, mmv)],
                                      inc=s_nmv))
        if pi == 0:
            kmx = o("vector", lambda e: e.tensor_scalar(
                out=n_bf[:, :], in0=knm[:, :], scalar1=1.05, scalar2=None,
                op0=AL.mult), waits=[(s_nmv, nmv_done[-1])], inc=s_kamx)
            _nst = (lambda stm: (stm, o(
                "sync", lambda e: e.dma_start(out=n_in[:, :],
                                              in_=n_bf[:, :]),
                waits=[(s_kamx, kmx)], inc=stm, k=16, kind="d")))(KS.nxt())
            KSK = KS.mark()
            grp4 = [[0, 1, 2, 3], [4, 5, 6, 7]]
            ccA1 = o("gpsimd", lambda e: e.collective_compute(
                "AllGather", AL.bypass, replica_groups=grp4,
                ins=[n_in.ap().opt()], outs=[n_all.ap().opt()]),
                waits=KSK, inc=s_ccA, kind="d")
            ccA2 = o("gpsimd", lambda e: e.collective_compute(
                "AllGather", AL.bypass, replica_groups=grp4,
                ins=[k_in.ap().opt()], outs=[k_all.ap().opt()]),
                waits=KSK, inc=s_ccA, kind="d")
        elif pi == 1:
            KSV = KS.mark()
            ccA3 = o("gpsimd", lambda e: e.collective_compute(
                "AllGather", AL.bypass, replica_groups=grp4,
                ins=[v_in.ap().opt()], outs=[v_all.ap().opt()]),
                waits=KSV, inc=s_ccA, kind="d")
            VAUG1 = o("gpsimd", lambda e: e.memset(
                bass.AP(vaug, 32, [[4224, P], [33, 128]]), 1.0),
                waits=KSV, inc=s_ini)

    EVAF = evA_done[-1]

    s_nr = st("s_nr")
    s_peN = st("s_peN")
    s_sc = st("s_sc")
    s_ex = st("s_ex")
    s_at = st("s_at")
    s_evB = st("s_evB")
    s_bo = st("s_bo")

    s_nrv = st("s_nrv")
    ld = o("gpsimd", lambda e: e.dma_start(
        out=nrow[:, :], in_=bass.AP(n_all, 0, [[64, 1], [1, 64]])),
        waits=[(s_ccA, ccA1)], inc=s_nr, k=16, kind="d")

    def fpr(e):
        e.tensor_copy(out=knm[:, :], in_=nrow[0:1, 0:NKD])
        e.drain()
        for r in range(1, 4):
            e.tensor_tensor(out=knm[:, :], in0=knm[:, :],
                            in1=nrow[0:1, NKD * r:NKD * (r + 1)],
                            op=AL.max)
            e.drain()
        return e.drain()
    pr1 = o("vector", fpr, waits=[(s_nr, ld)], inc=s_nrv)

    sq1 = o("scalar", lambda e: e.activation(
        out=qnm[:, :], in_=qnm[:, :], func=AF.Sqrt),
        waits=[(s_nrv, pr1), (s_nmv, nmv_done[-1])], inc=s_nrv)
    sq2 = o("scalar", lambda e: e.activation(
        out=knm[:, :], in_=knm[:, :], func=AF.Sqrt),
        waits=[(s_nrv, sq1)], inc=s_nrv)

    def fng(e):
        e.tensor_tensor(out=negrow[:, :], in0=qnm[:, :], in1=knm[:, :],
                        op=AL.mult)
        e.drain()
        e.tensor_scalar(out=negrow[:, :], in0=negrow[:, :],
                        scalar1=-1.05 * RS128, scalar2=None, op0=AL.mult)
        e.drain()
        return e.tensor_copy(out=negrow_bf[:, :], in_=negrow[:, :])
    ng1 = o("vector", fng, waits=[(s_nrv, sq2), (s_evA, EVAF),
                                  (s_ini, INI), (s_c0v, SCL)], inc=s_nrv)
    pn2 = o("tensor", lambda e: e.matmul(
        ps[:, 3584:3600], ones_r[:, :], negrow_bf[:, :], start=True,
        stop=True), waits=[(s_nrv, ng1)], inc=s_peN, kind="m")
    NRMF = o("vector", lambda e: e.tensor_copy(out=biasneg[:, :],
                                               in_=ps[:, 3584:3600]),
             waits=[(s_peN, pn2)], inc=s_nrv)

    s_bov = st("s_bov")
    b1 = o("gpsimd", lambda e: e.dma_start(
        out=bc_sb[:, :], in_=bass.AP(bo_row, 0, [[0, P], [1, D]])),
        waits=[(s_qst, qst_done[-1]), (s_qw, qw_done[-1])],
        inc=s_bo, k=16, kind="d")
    bo_done = []
    for tt in range(NT):
        bo_done.append(o("vector", lambda e, tt=tt: e.tensor_tensor(
            out=x_sb[:, D * tt:D * (tt + 1)],
            in0=x_sb[:, D * tt:D * (tt + 1)], in1=bc_sb[:, :], op=AL.add),
            waits=[(s_bo, b1), (s_evA, EVAF)], inc=s_bov))

    def at_slot(qc):
        b = 4 + qc // 3
        sl = qc % 3
        return ps[:, 512 * b + 129 * sl:512 * b + 129 * sl + 129]

    sc_done, ex_done, at_done, kld_done, vld_done = [], [], [], [], []
    evB_done = []
    pend_fat = [None]

    def emit_fat():
        if pend_fat[0] is not None:
            f, wl = pend_fat[0]
            at_done.append(o("tensor", f, waits=wl, inc=s_at, kind="m"))
            pend_fat[0] = None
    for h in range(16):
        for kc in range(NKC):
            i = h * NKC + kc
            wl = [(s_ccA, ccA2)]
            if i >= 2:
                wl.append((s_sc, sc_done[i - 2]))
            stm = KL.nxt()
            kld_done.append((stm, o(
                "sync", lambda e, h=h, kc=kc, i=i: e.dma_start(
                    out=ktile[:, 128 * (i % 2):128 * (i % 2) + 128],
                    in_=k_all[2048 * (kc // 8) + 128 * h:
                              2048 * (kc // 8) + 128 * (h + 1),
                              128 * (kc % 8):128 * (kc % 8) + 128]),
                waits=wl, inc=stm, k=16, kind="d")))
            wl = [(s_ccA, ccA3)]
            if i >= 2:
                wl.append((s_at, at_done[i - 2]))
            stm = VL.nxt()
            vld_done.append((stm, o(
                "sync", lambda e, h=h, kc=kc, i=i: e.dma_start_transpose(
                    out=vsb[:, 144 * (i % 2):144 * (i % 2) + 128],
                    in_=v_all[2048 * (kc // 8) + 128 * h:
                              2048 * (kc // 8) + 128 * (h + 1),
                              128 * (kc % 8):128 * (kc % 8) + 128]),
                waits=wl, inc=stm, k=16, kind="d")))

            def fsc(e, h=h, i=i):
                last = None
                for g2 in range(2):
                    last = e.matmul(
                        _bk(ps, 2 * (i % 2) + g2),
                        ktile[:, 128 * (i % 2):128 * (i % 2) + 128],
                        big_b[:, 1024 * h + 512 * g2:
                              1024 * h + 512 * (g2 + 1)],
                        start=True, stop=True)
                return last
            wl = [kld_done[i], (s_evA, EVAF)]
            if i >= 2:
                wl.append((s_ex, ex_done[i - 2]))
            sc_done.append(o("tensor", fsc, waits=wl, inc=s_sc, kind="m"))
            emit_fat()
            wl = [(s_sc, sc_done[i]), (s_nrv, NRMF)]
            if i == 0:
                wl.append((s_qw, qw_done[-1]))
                wl.append((s_qst, QST_ALL))
            if i >= 2:
                wl.append((s_at, at_done[i - 2]))
            ex_done.append(o("scalar", lambda e, i=i, h=h: e.activation(
                out=psb[:, 1024 * (i % 2):1024 * (i % 2) + 1024],
                in_=_bk(ps, 2 * (i % 2), 2), func=AF.Exp,
                bias=biasneg[:, h:h + 1], scale=RS128),
                waits=wl, inc=s_ex))

            def fat(e, kc=kc, i=i):
                last = None
                for qc in range(NT):
                    last = e.matmul(
                        at_slot(qc),
                        psb[:, 1024 * (i % 2) + 128 * qc:
                            1024 * (i % 2) + 128 * (qc + 1)],
                        vsb[:, 144 * (i % 2):144 * (i % 2) + 129],
                        start=(kc == 0 and qc % 3 == 0),
                        stop=(kc == NKC - 1 and (qc % 3 == 2 or qc == 7)))
                return last
            wl = [(s_ex, ex_done[i]), vld_done[i]]
            if kc == 0 and h > 0:
                wl.append((s_evB, evB_done[-1]))
            pend_fat[0] = (fat, wl)

        emit_fat()

        def fev(e, h=h):
            for qc in range(NT):
                e.reciprocal(out=rcd[:, 0:1], in_=at_slot(qc)[:, 128:129])
                e.drain()
                e.tensor_scalar(
                    out=big_a[:, 2048 * qc + 128 * h:
                              2048 * qc + 128 * (h + 1)],
                    in0=at_slot(qc)[:, 0:128], scalar1=rcd[:, 0:1],
                    scalar2=None, op0=AL.mult)
                e.drain()
            return e.drain()
        evB_done.append(o("vector", fev, waits=[(s_at, at_done[-1])],
                          inc=s_evB))
    EVBF = evB_done[-1]
    for (iin, iout, sem) in ((qup_in, qup, s_p0),):
        o("gpsimd", lambda e, iin=iin, iout=iout:
          e.collective_compute(
            "AllGather", AL.bypass, replica_groups=[list(range(8))],
            ins=[iin.ap().opt()], outs=[iout.ap().opt()]),
          waits=[(s_qst, QST_ALL), (s_at, at_done[int(len(at_done) * 0.7)])], inc=sem, kind="d")
    SCF = sc_done[-1]
    EXF = ex_done[-1]

    if stop == "B":
        _finish(nc, em, o, st, x_sb, y_out,
                [(s_evB, EVBF), (s_bov, bo_done[-1])],
                dumps=[("big_a", big_a), ("big_b", big_b),
                       ("biasneg", biasneg), ("qnm", qnm), ("knm", knm),
                       ("bc_sb", bc_sb)])
        em.emit()
        return nc

    s_wC = st("s_wC")
    s_peC = st("s_peC")
    s_evC = st("s_evC")

    for qc in range(NT):
        tpb(big_b, 128 * qc, 1024, big_a, 2048 * qc, 128, NKD,
            [(s_evB, EVBF), (s_sc, SCF)])
    LHC = TP.mark()
    wC_done, peC_done, evC_done = [], [], []
    for og in range(4):
        wl = [(s_qst, qst_done[-1])]
        if og >= 2:
            wl.append((s_peC, peC_done[8 * (og - 1) - 1]))
        wC_done.append(wslice_load(og % 2, woT, 512 * og, NKD, 512, s_wC,
                                   wl))
        for tt in range(NT):
            def fmm(e, og=og, tt=tt):
                last = None
                for ki in range(NKD):
                    last = e.matmul(
                        _bk(ps, tt % 2),
                        big_b[:, 1024 * ki + 128 * tt:
                              1024 * ki + 128 * (tt + 1)],
                        wbuf[:, 8192 * (og % 2) + 512 * ki:
                             8192 * (og % 2) + 512 * (ki + 1)],
                        start=(ki == 0), stop=(ki == NKD - 1))
                return last
            wl = [LHC, (s_wC, wC_done[og]), (s_ex, EXF)]
            n = len(peC_done)
            if n >= 2:
                wl.append((s_evC, evC_done[n - 2]))
            peC_done.append(o("tensor", fmm, waits=wl, inc=s_peC, kind="m"))
            evC_done.append(o(
                "vector", lambda e, og=og, tt=tt: e.scalar_tensor_tensor(
                    out=x_sb[:, D * tt + 512 * og:D * tt + 512 * (og + 1)],
                    in0=_bk(ps, tt % 2), scalar=1.0,
                    in1=x_sb[:, D * tt + 512 * og:
                             D * tt + 512 * (og + 1)],
                    op0=AL.mult, op1=AL.add),
                waits=[(s_peC, peC_done[-1]), (s_bov, bo_done[tt])],
                inc=s_evC))
    EVCF = evC_done[-1]
    PECF = peC_done[-1]

    if stop == "C":
        _finish(nc, em, o, st, x_sb, y_out, [(s_evC, EVCF)])
        em.emit()
        return nc

    for (iin, iout, sem) in ((qdn_in, qdn, s_pdn), (qcf_in, qcf, s_pcf), (qcp_in, qcp, s_pcp)):
        o("gpsimd", lambda e, iin=iin, iout=iout:
          e.collective_compute(
            "AllGather", AL.bypass, replica_groups=[list(range(8))],
            ins=[iin.ap().opt()], outs=[iout.ap().opt()]),
          waits=[(s_qst, QST_ALL)], inc=sem, kind="d")
    s_hld = st("s_hld")
    s_peD = st("s_peD")
    s_aqD = st("s_aqD")
    s_wD = st("s_wD")
    s_peU = st("s_peU")
    s_rD = st("s_rD")
    s_bfD = st("s_bfD")
    s_peH2 = st("s_peH2")
    s_zD = st("s_zD")
    s_aq2 = st("s_aq2")
    s_xqst = st("s_xqst")
    s_wDn = st("s_wDn")
    s_lhDn = st("s_lhDn")
    s_peDn = st("s_peDn")
    s_evD = st("s_evD")

    def fhl(e):
        ia = bass.AP(h1024, 0, [[1024, P], [P * 1024, 8], [1, 1024]])
        oa = bass.AP(big_a, 0, [[16384, P], [1024, 8], [1, 1024]])
        return e.dma_start(out=oa, in_=ia)
    HLD = o("sync", fhl, waits=LHC, inc=s_hld, k=16, kind="d")
    h_sb = big_a

    peD_done = []
    aqD_done = []
    x1T_done = []
    rD_done = []
    bfD_done = []
    zD_done = []
    aq2_done = []
    x2T_done = []
    xqst_done = []
    peU_done = []
    peH2_done = []

    for pr in range(4):
        pair_mark = TP.mark()
        for tp in range(2):
            tt = 2 * pr + tp
            slot = tt % 2
            pre = [(s_evC, EVCF), (s_peC, PECF)]
            if tt >= 2:
                pre += pair_mark
            s_l, lv = ln_chain(tt, slot, pre)

            def fbt(e, slot=slot):
                e.tensor_tensor(
                    out=fmt[:, 4096 * slot:4096 * slot + 1024],
                    in0=lnt[:, D * slot:D * slot + 1024],
                    in1=lnt[:, D * slot + 1024:D * slot + 2048], op=AL.add)
                e.drain()
                return e.tensor_tensor(
                    out=fmt[:, 4096 * slot + 1024:4096 * slot + 2048],
                    in0=lnt[:, D * slot:D * slot + 1024],
                    in1=lnt[:, D * slot + 1024:D * slot + 2048],
                    op=AL.subtract)
            wl = [(s_l, lv)]
            if tt >= 2:
                wl.append((s_peD, peD_done[tt - 2]))
                wl.append((s_xqst, xqst_done[tt - 2]))
            if tt >= 1 and pr >= 1:
                wl.append((s_peU, s_peU.n))
            bt = o("vector", fbt, waits=wl, inc=s_l)
            tpb(fmt, 4096 * slot + 2048, 128, fmt, 4096 * slot, 128, 16,
                [(s_l, bt)])
            lhsD_tile = TP.mark()
            last_pe = None
            for b in range(2):
                for og in range(2):
                    bidx = 4 * (tt % 2) + 2 * b + og

                    def fmm(e, slot=slot, b=b, og=og, bidx=bidx):
                        last = None
                        for ki in range(8):
                            last = e.matmul(
                                _bk(ps, bidx),
                                fmt[:, 4096 * slot + 2048 +
                                    128 * (8 * b + ki):
                                    4096 * slot + 2048 +
                                    128 * (8 * b + ki) + 128],
                                h_sb[:, 1024 * ki + 512 * og:
                                     1024 * ki + 512 * (og + 1)],
                                start=(ki == 0), stop=(ki == 7))
                        return last
                    wl = [lhsD_tile, (s_hld, HLD)]
                    if tt == 0:
                        wl += [(s_ex, EXF), (s_evB, EVBF), (s_nrv, NRMF)]
                    if tt >= 2:
                        wl.append((s_aqD, aqD_done[tt - 2]))
                    last_pe = o("tensor", fmm, waits=wl, inc=s_peD,
                                kind="m")
            peD_done.append(last_pe)
        for tp in range(2):
            tt = 2 * pr + tp

            def faq(e, tt=tt, bidx0=4 * (tt % 2)):
                sp = _bk(ps, bidx0, 4)
                e.reduce_max(out=dnv[:, 0:1], in_=sp, axis=AX,
                             apply_absolute_value=True)
                e.drain()
                e.tensor_scalar(out=dnv[:, 1:2], in0=dnv[:, 0:1],
                                scalar1=EPS_U, scalar2=None, op0=AL.max)
                e.drain()
                e.reciprocal(out=dnv[:, 2:3], in_=dnv[:, 1:2])
                e.drain()
                e.tensor_scalar(out=dnv[:, 2:3], in0=dnv[:, 2:3],
                                scalar1=127.0, scalar2=None, op0=AL.mult)
                e.drain()
                e.scalar_tensor_tensor(out=cfv[:, tt:tt + 1],
                                       in0=dnv[:, 1:2], scalar=C_UP,
                                       in1=sc_bc[:, 0:1], op0=AL.mult,
                                       op1=AL.mult)
                e.drain()
                e.tensor_scalar(out=qi32[:, 0:2048], in0=sp,
                                scalar1=dnv[:, 2:3], scalar2=None,
                                op0=AL.mult)
                e.drain()
                return e.tensor_scalar(
                    out=lnt[:, D * (tt % 2):D * (tt % 2) + 2048],
                    in0=qi32[:, 0:2048], scalar1=0.0, scalar2=None,
                    op0=AL.add)
            wl = [(s_peD, peD_done[tt]), (s_c0v, SCL)]
            if tt >= 2:
                wl.append(x1T_done[tt - 2])
            aqD_done.append(o("vector", faq, waits=wl, inc=s_aqD))
            tpb(big_a, 8192 + 128 * tp, 256, lnt, D * (tt % 2), 128, NKD,
                [(s_aqD, aqD_done[-1]), (s_hld, HLD)])
            x1T_done.append(TP.mark())
        for og in range(8):
            j = 8 * pr + og
            wl = [(s_p0, 1)]
            if j >= 2:
                wl.append((s_peU, peU_done[2 * (j - 2) + 1]))
            wD_j = wslice_load(og % 2, qup, 512 * og, NKD, 512, s_wD,
                               wl, cast=False)
            for tp in range(2):
                tt = 2 * pr + tp

                def fmm(e, og=og, tp=tp):
                    last = None
                    for ki in range(NKD):
                        last = e.matmul(
                            _bk(ps, 2 * (og % 2) + tp),
                            big_a[:, 8192 + 256 * ki + 128 * tp:
                                  8192 + 256 * ki + 128 * (tp + 1)],
                            wbuf[:, 8192 * (og % 2) + 512 * ki:
                                 8192 * (og % 2) + 512 * (ki + 1)],
                            start=(ki == 0), stop=(ki == NKD - 1))
                    return last
                wl = [x1T_done[-1], (s_wD, wD_j)]
                n = len(peU_done)
                if n >= 2:
                    wl.append((s_rD, rD_done[n - 2]))
                if pr >= 1 and og == 0 and tp == 0:
                    wl.append((s_bfD, s_bfD.n))
                peU_done.append(o("tensor", fmm, waits=wl, inc=s_peU,
                                  kind="m"))
                rD_done.append(o(
                    "scalar", lambda e, tp=tp, tt=tt, og=og:
                    e.activation(
                        out=rbufD(tp)[:, 512 * og:512 * (og + 1)],
                        in_=_bk(ps, 2 * (og % 2) + tp), func=AF.Relu,
                        scale=cfv[:, tt:tt + 1]),
                    waits=[(s_peU, peU_done[-1])], inc=s_rD))
        sq_v = {}
        lhsT_t = {}
        peH2_t = {}
        zD_t = {}

        def d3a(tp):
            tt = 2 * pr + tp
            slot = tt % 2
            sq_v[tp] = o("scalar", lambda e, tp=tp: e.activation(
                out=rbufD(tp)[:, :], in_=rbufD(tp)[:, :], func=AF.Square),
                waits=[(s_rD, rD_done[-1])], inc=s_rD)

            def fb2(e, tp=tp, slot=slot):
                rb = rbufD(tp)
                for (do, i0, i1, op2) in ((0, 0, 1024, AL.add),
                                          (1024, 0, 1024, AL.subtract),
                                          (2048, 2048, 3072, AL.add),
                                          (3072, 2048, 3072, AL.subtract)):
                    e.tensor_tensor(
                        out=fmt[:, 4096 * slot + do:
                                4096 * slot + do + 1024],
                        in0=rb[:, i0:i0 + 1024],
                        in1=rb[:, i1:i1 + 1024], op=op2)
                    e.drain()
                for (do, i0, i1, op2) in ((0, 0, 2048, AL.add),
                                          (1024, 1024, 3072, AL.add),
                                          (2048, 0, 2048, AL.subtract),
                                          (3072, 1024, 3072, AL.subtract)):
                    e.tensor_tensor(
                        out=rb[:, do:do + 1024],
                        in0=fmt[:, 4096 * slot + i0:
                                4096 * slot + i0 + 1024],
                        in1=fmt[:, 4096 * slot + i1:
                                4096 * slot + i1 + 1024], op=op2)
                    e.drain()
                return e.nop()
            bfD_done.append(o("vector", fb2,
                              waits=[(s_rD, sq_v[tp]),
                                     (s_peU, peU_done[-1]),
                                     (s_peD, peD_done[tt])], inc=s_bfD))
            tpb(fmt, 4096 * slot, 128, rbufD(tp), 0, 128, 32,
                [(s_bfD, bfD_done[-1])])
            lhsT_t[tp] = TP.mark()

        def d3b(tp):
            slot = (2 * pr + tp) % 2
            for j1 in range(4):
                bidx = 2 * (j1 % 2)

                def fmm(e, slot=slot, j1=j1, bidx=bidx):
                    last = None
                    for ki in range(8):
                        for og in range(2):
                            last = e.matmul(
                                _bk(ps, bidx + og),
                                fmt[:, 4096 * slot +
                                    128 * (8 * j1 + ki):
                                    4096 * slot +
                                    128 * (8 * j1 + ki) + 128],
                                h_sb[:, 1024 * ki + 512 * og:
                                     1024 * ki + 512 * (og + 1)],
                                start=(ki == 0), stop=(ki == 7))
                    return last
                wl = [lhsT_t[tp]]
                n = len(peH2_done)
                if n >= 2:
                    wl.append((s_zD, zD_done[n - 2]))
                peH2_done.append(o("tensor", fmm, waits=wl, inc=s_peH2,
                                   kind="m"))
                wl = [(s_peH2, peH2_done[-1])]
                if j1 == 0 and len(aq2_done) > 0:
                    wl.append((s_aq2, aq2_done[-1]))
                zD_done.append(o("scalar", lambda e, j1=j1, bidx=bidx:
                                 e.activation(
                                   out=zq[:, 1024 * j1:1024 * (j1 + 1)],
                                   in_=_bk(ps, bidx, 2), func=AF.Copy),
                                 waits=wl, inc=s_zD))
            peH2_t[tp] = peH2_done[-1]
            zD_t[tp] = zD_done[-1]

        def d3c(tp):
            tt = 2 * pr + tp
            slot = tt % 2

            def faq2(e, tt=tt, tp=tp):
                e.reduce_max(out=dnv[:, 0:1], in_=zq[:, :], axis=AX,
                             apply_absolute_value=True)
                e.drain()
                e.tensor_scalar(out=dnv[:, 1:2], in0=dnv[:, 0:1],
                                scalar1=EPS_Z, scalar2=None, op0=AL.max)
                e.drain()
                e.reciprocal(out=dnv[:, 2:3], in_=dnv[:, 1:2])
                e.drain()
                e.tensor_scalar(out=dnv[:, 2:3], in0=dnv[:, 2:3],
                                scalar1=127.0, scalar2=None, op0=AL.mult)
                e.drain()
                e.scalar_tensor_tensor(out=cfv2[:, tt:tt + 1],
                                       in0=dnv[:, 1:2], scalar=C_DN,
                                       in1=sc_bc[:, 1:2], op0=AL.mult,
                                       op1=AL.mult)
                e.drain()
                for c in range(2):
                    e.tensor_scalar(out=qi32[:, 0:2048],
                                    in0=zq[:, 2048 * c:2048 * (c + 1)],
                                    scalar1=dnv[:, 2:3], scalar2=None,
                                    op0=AL.mult)
                    e.drain()
                    e.tensor_scalar(
                        out=rbufD(tp)[:, 2048 * c:2048 * (c + 1)],
                        in0=qi32[:, 0:2048], scalar1=0.0, scalar2=None,
                        op0=AL.add)
                    e.drain()
                return e.drain()
            aq2_done.append(o("vector", faq2,
                              waits=[(s_zD, zD_t[tp])], inc=s_aq2))
            tpb(fmt, 4096 * slot, 128, rbufD(tp), 0, 128, 32,
                [(s_aq2, aq2_done[-1]), (s_peH2, peH2_t[tp])])
            x2T_done.append(TP.mark())

            def fst(e, slot=slot, tt=tt):
                ia = bass.AP(fmt, 4096 * slot,
                             [[8192, P], [128, 32], [1, 128]])
                oa = bass.AP(xq2_d, 128 * tt,
                             [[1024, P], [131072, 32], [1, 128]])
                return e.dma_start(out=oa, in_=ia)
            xqst_done.append(o("sync", fst, waits=x2T_done[-1],
                               inc=s_xqst, k=16, kind="d"))

        d3a(0)
        d3a(1)
        d3b(0)
        d3c(0)
        d3b(1)
        d3c(1)
    lhDn_done, wDn_done, peDn_done, evD_done = [], [], [], []
    for kq in range(2):
        def fld(e, kq=kq):
            ia = bass.AP(xq2_d, 2048 * kq * T,
                         [[1024, P], [131072, 16], [1, 1024]])
            oa = bass.AP(big_a, 0, [[16384, P], [1024, 16], [1, 1024]])
            return e.dma_start(out=oa, in_=ia)
        wl = [(s_xqst, xqst_done[-1]), (s_peH2, peH2_done[-1]),
              (s_peU, peU_done[-1]), (s_peD, peD_done[-1])]
        wl += x1T_done[-1]
        if kq == 1:
            wl.append((s_peDn, peDn_done[-1]))
        lhDn_done.append(o("sync", fld, waits=wl, inc=s_lhDn, k=16,
                           kind="d"))
        for og in range(4):
            j = 4 * kq + og
            wl = [(s_peC, PECF), (s_pdn, 1)]
            if j >= 2:
                wl.append((s_peDn, peDn_done[8 * (j - 1) - 1]))
            wDn_done.append(wslice_load(og % 2, qdn, 512 * og, NKD, 512,
                                        s_wDn, wl, row0=2048 * kq,
                                        cast=False))
            for tt in range(NT):
                def fmm(e, og=og, tt=tt):
                    last = None
                    for ki in range(NKD):
                        last = e.matmul(
                            _bk(ps, tt % 2),
                            big_a[:, 1024 * ki + 128 * tt:
                                  1024 * ki + 128 * (tt + 1)],
                            wbuf[:, 8192 * (og % 2) + 512 * ki:
                                 8192 * (og % 2) + 512 * (ki + 1)],
                            start=(ki == 0), stop=(ki == NKD - 1))
                    return last
                wl = [(s_lhDn, lhDn_done[kq]), (s_wDn, wDn_done[j]),
                      (s_aq2, aq2_done[-1])]
                n = len(peDn_done)
                if n >= 2:
                    wl.append((s_evD, evD_done[n - 2]))
                peDn_done.append(o("tensor", fmm, waits=wl, inc=s_peDn,
                                   kind="m"))
                evD_done.append(o(
                    "vector", lambda e, og=og, tt=tt:
                    e.scalar_tensor_tensor(
                        out=x_sb[:, D * tt + 512 * og:
                                 D * tt + 512 * (og + 1)],
                        in0=_bk(ps, tt % 2), scalar=cfv2[:, tt:tt + 1],
                        in1=x_sb[:, D * tt + 512 * og:
                                 D * tt + 512 * (og + 1)],
                        op0=AL.mult, op1=AL.add),
                    waits=[(s_peDn, peDn_done[-1])], inc=s_evD))
    EVDF = evD_done[-1]
    PEDNF = peDn_done[-1]

    if stop == "D":
        _finish(nc, em, o, st, x_sb, y_out, [(s_evD, EVDF)])
        em.emit()
        return nc

    s_wE = st("s_wE")
    s_peE = st("s_peE")
    s_evE = st("s_evE")
    s_vkE = st("s_vkE")
    s_ccE = st("s_ccE")
    s_wQ = st("s_wQ")
    s_peQ = st("s_peQ")
    s_evQ = st("s_evQ")
    s_peO = st("s_peO")
    s_evO = st("s_evO")
    s_wP = st("s_wP")
    s_peP = st("s_peP")
    s_evP = st("s_evP")
    s_bp = st("s_bp")

    lhsE_done = []
    for tt in range(NT):
        pre = [(s_evD, EVDF)]
        if tt >= 2:
            pre.append(lhsE_done[tt - 2])
        s_l, lv = ln_chain(tt, tt % 2, pre)
        tpb(big_a, 128 * tt, 1024, lnt, D * (tt % 2), 128, NKD,
            [(s_l, lv), (s_lhDn, lhDn_done[-1]), (s_peDn, PEDNF)])
        lhsE_done.append(TP.mark())
    LHE = lhsE_done[-1]

    wE_done, peE_done, evE_done, vkE_done, peO_done = [], [], [], [], []
    s_evEv = st("s_evEv")
    for pr in range(4):
        for p2 in range(2):
            for og in range(4):
                j = (pr * 2 + p2) * 4 + og
                wl = [(s_peDn, PEDNF)]
                if j >= 2:
                    wl.append((s_peE, peE_done[2 * (j - 2) + 1]))
                wE_done.append(wslice_load(
                    og % 2, wkvT, 2048 * p2 + 512 * og, NKD, 512, s_wE,
                    wl))
                for tp in range(2):
                    tt = 2 * pr + tp

                    def fmm(e, og=og, tt=tt):
                        last = None
                        for ki in range(NKD):
                            last = e.matmul(
                                _bk(ps, tt % 2),
                                big_a[:, 1024 * ki + 128 * tt:
                                      1024 * ki + 128 * (tt + 1)],
                                wbuf[:, 8192 * (og % 2) + 512 * ki:
                                     8192 * (og % 2) + 512 * (ki + 1)],
                                start=(ki == 0), stop=(ki == NKD - 1))
                        return last
                    wl = [LHE, (s_wE, wE_done[-1]),
                          (s_evD, EVDF)]
                    n = len(peE_done)
                    if n >= 2:
                        wl.append(evE_done[n - 2])
                    if pr >= 1 and p2 == 0 and og == 0:
                        wl.append((s_vkE, vkE_done[-1]))
                    peE_done.append(o("tensor", fmm, waits=wl, inc=s_peE,
                                      kind="m"))
                    if p2 == 0:
                        evE_done.append((s_evE, o(
                            "scalar", lambda e, tp=tp, og=og, tt=tt:
                            e.activation(
                                out=ktm2[:, 2048 * tp + 512 * og:
                                         2048 * tp + 512 * (og + 1)],
                                in_=_bk(ps, tt % 2), func=AF.Relu),
                            waits=[(s_peE, peE_done[-1])], inc=s_evE)))
                    else:
                        def fv(e, tp=tp, og=og, tt=tt):
                            oa = bass.AP(vaug, 2112 * tp + 528 * og,
                                         [[4224, P], [33, 16], [1, 32]])
                            return e.tensor_copy(out=oa,
                                                 in_=_bk(ps, tt % 2))
                        evE_done.append((s_evEv, o(
                            "vector", fv,
                            waits=[(s_peE, peE_done[-1]), (s_ini, VAUG1)],
                            inc=s_evEv)))
        for tp in range(2):
            for rnd in range(5):
                h0 = 15 * rnd
                nh = min(15, 64 - h0)
                bidx = len(peO_done) % 2

                def fvk(e, tp=tp, h0=h0, nh=nh, bidx=bidx):
                    last = None
                    for hl in range(nh):
                        h = h0 + hl
                        last = e.matmul(
                            ps[0:32, 512 * bidx + 33 * hl:
                               512 * bidx + 33 * hl + 33],
                            ktm2[:, 2048 * tp + 32 * h:
                                 2048 * tp + 32 * (h + 1)],
                            vaug[:, 2112 * tp + 33 * h:
                                 2112 * tp + 33 * (h + 1)],
                            start=(hl == 0), stop=(hl == nh - 1))
                    return last
                wl = [evE_done[-1]]
                n = len(peO_done)
                if n >= 2:
                    wl.append((s_vkE, vkE_done[n - 2]))
                peO_done.append(o("tensor", fvk, waits=wl, inc=s_peO,
                                  kind="m"))
                vkE_done.append(o(
                    "vector", lambda e, h0=h0, nh=nh, bidx=bidx:
                    e.tensor_tensor(
                        out=vkacc[0:32, 33 * h0:33 * (h0 + nh)],
                        in0=vkacc[0:32, 33 * h0:33 * (h0 + nh)],
                        in1=ps[0:32, 512 * bidx:512 * bidx + 33 * nh],
                        op=AL.add),
                    waits=[(s_peO, peO_done[-1]), (s_qv, VK0)],
                    inc=s_vkE))
    VKF = vkE_done[-1]

    s_arE = st("s_arE")
    e1 = o("gpsimd", lambda e: e.dma_start(out=vk_in[:, :],
                                           in_=vkacc[:, :]),
           waits=[(s_vkE, VKF)], inc=s_ccE, k=16, kind="d")
    e2 = o("gpsimd", lambda e: e.collective_compute(
        "AllReduce", AL.add, replica_groups=grp4, ins=[vk_in.ap().opt()],
        outs=[vk_out.ap().opt()]), waits=[(s_ccE, e1)], inc=s_arE,
        kind="d")
    ms = o("gpsimd", lambda e: e.memset(vkT[:, :], 0.0),
           waits=[(s_vkE, VKF), (s_arE, e2)], inc=s_arE)
    for j2 in range(4):
        e3 = o("gpsimd", lambda e, j2=j2: e.dma_start(
            out=vkT[32 * j2:32 * (j2 + 1), :].rearrange(
                "p (c e) -> p c e", e=132)[:, :, 33 * j2:33 * (j2 + 1)],
            in_=bass.AP(vk_out, 33 * j2, [[2112, 32], [132, 16], [1, 33]])),
            waits=[(s_arE, ms)], inc=s_ccE, k=16, kind="d")
    CCEF = s_ccE.n

    wQ_done, peQ_done, evQ_done = [], [], []
    for oc in range(NKD):
        wl = [(s_vkE, VKF)]
        if oc >= 2:
            wl.append((s_peQ, peQ_done[2 * (oc - 2) + 1]))
        wQ_done.append(o("gpsimd", lambda e, oc=oc: e.dma_start(
            out=wt_sb[:, 2048 * (oc % 2):2048 * (oc % 2) + 2048],
            in_=wq1_d[:, oc, :]), waits=wl, inc=s_wQ, k=16, kind="d"))
        for g2 in range(2):
            bidx = 4 + 2 * (oc % 2) + g2

            def fmm(e, oc=oc, g2=g2, bidx=bidx):
                last = None
                for ki in range(NKD):
                    last = e.matmul(
                        _bk(ps, bidx),
                        wt_sb[:, 2048 * (oc % 2) + 128 * ki:
                              2048 * (oc % 2) + 128 * (ki + 1)],
                        big_a[:, 1024 * ki + 512 * g2:
                              1024 * ki + 512 * (g2 + 1)],
                        start=(ki == 0), stop=(ki == NKD - 1))
                return last
            wl = [(s_wQ, wQ_done[oc]), (s_aqD, aq2_done[-1] if False
                                        else s_aqD.n)]
            if oc == 0 and g2 == 0:
                wl.append((s_aq2, aq2_done[-1]))
            if oc >= 2:
                wl.append((s_evQ, evQ_done[2 * (oc - 2) + 1]))
            peQ_done.append(o("tensor", fmm, waits=wl, inc=s_peQ,
                              kind="m"))
            evQ_done.append(o("scalar", lambda e, oc=oc, g2=g2, bidx=bidx:
                              e.activation(
                                out=big_b[:, 1024 * oc + 512 * g2:
                                          1024 * oc + 512 * (g2 + 1)],
                                in_=_bk(ps, bidx), func=AF.Relu),
                              waits=[(s_peQ, peQ_done[-1])], inc=s_evQ))
    EVQF = evQ_done[-1]

    peO2_done, evO_done, lhsP_done = [], [], []
    for tt in range(NT):
        for c in range(NKD):
            i2 = tt * NKD + c
            bidx = i2 % 2

            def fo(e, tt=tt, c=c, bidx=bidx):
                return e.matmul(
                    ps[:, 512 * bidx:512 * bidx + 132],
                    big_b[:, 1024 * c + 128 * tt:1024 * c + 128 * (tt + 1)],
                    vkT[:, 132 * c:132 * (c + 1)],
                    start=True, stop=True)
            wl = [(s_evQ, EVQF), (s_ccE, CCEF), (s_vkE, VKF)]
            if i2 >= 2:
                wl.append((s_evO, evO_done[i2 - 2]))
            peO2_done.append(o("tensor", fo, waits=wl, inc=s_peO,
                               kind="m"))

            def fdiv(e, tt=tt, c=c, bidx=bidx):
                e.tensor_scalar(
                    out=rcd[:, 0:4],
                    in0=bass.AP(ps, 512 * bidx + 32,
                                [[4096, P], [33, 4]]),
                    scalar1=1e-15, scalar2=None, op0=AL.add)
                e.drain()
                e.reciprocal(out=rcd[:, 0:4], in_=rcd[:, 0:4])
                e.drain()
                e.tensor_tensor(
                    out=lnt[:, D * (tt % 2) + 128 * c:
                            D * (tt % 2) + 128 * (c + 1)],
                    in0=bass.AP(ps, 512 * bidx,
                                [[4096, P], [33, 4], [1, 32]]),
                    in1=bass.AP(rcd, 0, [[4, P], [1, 4], [0, 32]]),
                    op=AL.mult)
                e.drain()
                return e.drain()
            wl = [(s_peO, peO2_done[-1])]
            if tt >= 2 and c == 0:
                wl.append(lhsP_done[tt - 2])
            evO_done.append(o("vector", fdiv, waits=wl, inc=s_evO))
        tpb(big_a, 128 * tt, 1024, lnt, D * (tt % 2), 128, NKD,
            [(s_evO, evO_done[-1]), (s_peQ, peQ_done[-1])] + LHE)
        lhsP_done.append(TP.mark())
    LHP = lhsP_done[-1]

    s_bpv = st("s_bpv")
    p1 = o("gpsimd", lambda e: e.dma_start(
        out=bc_sb[:, :], in_=bass.AP(bpj_row, 0, [[0, P], [1, D]])),
        waits=[(s_bov, bo_done[-1]), (s_evC, EVCF)], inc=s_bp, k=16,
        kind="d")
    bp_done = []
    for tt in range(NT):
        bp_done.append(o("vector", lambda e, tt=tt: e.tensor_tensor(
            out=x_sb[:, D * tt:D * (tt + 1)],
            in0=x_sb[:, D * tt:D * (tt + 1)], in1=bc_sb[:, :], op=AL.add),
            waits=[(s_bp, p1), (s_evD, EVDF)] + lhsE_done[tt],
            inc=s_bpv))

    wP_done, peP_done, evP_done = [], [], []
    for og in range(4):
        wl = [LHP]
        if og >= 2:
            wl.append((s_peP, peP_done[8 * (og - 1) - 1]))
        wP_done.append(wslice_load(og % 2, wprojT, 512 * og, NKD, 512,
                                   s_wP, wl))
        for tt in range(NT):
            def fmm(e, og=og, tt=tt):
                last = None
                for ki in range(NKD):
                    last = e.matmul(
                        _bk(ps, tt % 2),
                        big_a[:, 1024 * ki + 128 * tt:
                              1024 * ki + 128 * (tt + 1)],
                        wbuf[:, 8192 * (og % 2) + 512 * ki:
                             8192 * (og % 2) + 512 * (ki + 1)],
                        start=(ki == 0), stop=(ki == NKD - 1))
                return last
            wl = [(s_wP, wP_done[og]), (s_peO, peO2_done[-1])] + LHP
            n = len(peP_done)
            if n >= 2:
                wl.append((s_evP, evP_done[n - 2]))
            peP_done.append(o("tensor", fmm, waits=wl, inc=s_peP,
                              kind="m"))
            evP_done.append(o(
                "vector", lambda e, og=og, tt=tt: e.scalar_tensor_tensor(
                    out=x_sb[:, D * tt + 512 * og:
                             D * tt + 512 * (og + 1)],
                    in0=_bk(ps, tt % 2), scalar=1.0,
                    in1=x_sb[:, D * tt + 512 * og:
                             D * tt + 512 * (og + 1)],
                    op0=AL.mult, op1=AL.add),
                waits=[(s_peP, peP_done[-1]), (s_bpv, bp_done[tt])],
                inc=s_evP))
    EVPF = evP_done[-1]
    PEPF = peP_done[-1]

    if stop == "E":
        _finish(nc, em, o, st, x_sb, y_out, [(s_evP, EVPF)])
        em.emit()
        return nc

    s_wF = st("s_wF")
    s_peF = st("s_peF")
    s_rF = st("s_rF")
    s_qF = st("s_qF")
    s_sqF = st("s_sqF")
    s_khst = st("s_khst")
    s_lhG = st("s_lhG")
    s_wG = st("s_wG")
    s_peG = st("s_peG")
    s_evG = st("s_evG")

    rbufF = big_a
    lnF_done, wF_done, peF_done, rF_done = [], [], [], []
    qF_done, khst_done = [], []

    def emit_lnF(prx):
        for tp in range(2):
            tt = 2 * prx + tp
            pre = [(s_evP, EVPF)]
            if tt >= 2:
                pre.append(lnF_done[tt - 2])
            s_l, lv = ln_chain(tt, tt % 2, pre)
            wl = [(s_l, lv), (s_peP, PEPF)]
            if prx >= 1:
                wl.append((s_peF, peF_done[-2] if tp == 0
                           else peF_done[-1]))
            tpb(big_b, 128 * tp, 256, lnt, D * (tt % 2), 128, NKD, wl)
            lnF_done.append(TP.mark())

    emit_lnF(0)
    for pr in range(4):
        for og in range(16):
            j = 16 * pr + og
            wl = [(s_pcf, 1), (s_peP, PEPF)]
            if j >= 2:
                wl.append((s_peF, peF_done[2 * (j - 2) + 1]))
            wF_done.append(wslice_load(og % 2, qcf, 512 * og, NKD,
                                       512, s_wF, wl, cast=False))
            for tp in range(2):
                tt = 2 * pr + tp

                def fmm(e, og=og, tp=tp):
                    last = None
                    for ki in range(NKD):
                        last = e.matmul(
                            _bk(ps, 2 * (og % 2) + tp),
                            big_b[:, 256 * ki + 128 * tp:
                                  256 * ki + 128 * (tp + 1)],
                            wbuf[:, 8192 * (og % 2) + 512 * ki:
                                 8192 * (og % 2) + 512 * (ki + 1)],
                            start=(ki == 0), stop=(ki == NKD - 1))
                    return last
                wl = [lnF_done[2 * pr + tp], (s_wF, wF_done[-1])]
                n = len(peF_done)
                if n >= 2:
                    wl.append((s_rF, rF_done[n - 2]))
                if pr >= 1 and og == 0:
                    wl.append((s_khst, khst_done[-3] if tp == 0
                               else khst_done[-1]))
                peF_done.append(o("tensor", fmm, waits=wl, inc=s_peF,
                                  kind="m"))
                rF_done.append(o(
                    "scalar", lambda e, tp=tp, og=og: e.activation(
                        out=rbufF[:, 8192 * tp + 512 * og:
                                  8192 * tp + 512 * (og + 1)],
                        in_=_bk(ps, 2 * (og % 2) + tp), func=AF.Relu,
                        scale=sc_bc[:, 2:3]),
                    waits=[(s_peF, peF_done[-1]), (s_c0v, SCL)],
                    inc=s_rF))
        if pr < 3:
            emit_lnF(pr + 1)
        for tp in range(2):
            tt = 2 * pr + tp

            def fq(e, tp=tp, tt=tt):
                e.reduce_max(out=dnv[:, 0:1],
                             in_=rbufF[:, 8192 * tp:8192 * (tp + 1)],
                             axis=AX, apply_absolute_value=True)
                e.drain()
                e.tensor_tensor(out=dnv[:, 0:1], in0=dnv[:, 0:1],
                                in1=dnv[:, 0:1], op=AL.mult)
                e.drain()
                e.tensor_scalar(out=dnv[:, 1:2], in0=dnv[:, 0:1],
                                scalar1=1e-5, scalar2=None, op0=AL.max)
                e.drain()
                e.reciprocal(out=dnv[:, 2:3], in_=dnv[:, 1:2])
                e.drain()
                e.tensor_scalar(out=dnv[:, 2:3], in0=dnv[:, 2:3],
                                scalar1=255.0, scalar2=None, op0=AL.mult)
                e.drain()
                e.scalar_tensor_tensor(out=cfv[:, tt:tt + 1],
                                       in0=dnv[:, 1:2],
                                       scalar=1.0 / 255.0,
                                       in1=sc_bc[:, 3:4], op0=AL.mult,
                                       op1=AL.mult)
                e.drain()
                return e.drain()
            qF_done.append(o("vector", fq, waits=[(s_rF, rF_done[-1])],
                             inc=s_qF))
            base_q = qF_done[-1]
            sq_done = []
            for c in range(8):
                hb = 1024 * (c % 2)
                wl = [(s_qF, base_q)] if c < 2 else [(s_qF, qF_done[-1])]
                sq_done.append(o("scalar", lambda e, tp=tp, c=c, hb=hb:
                                 e.activation(
                                   out=y32[:, hb:hb + 1024],
                                   in_=rbufF[:, 8192 * tp + 1024 * c:
                                             8192 * tp + 1024 * (c + 1)],
                                   func=AF.Square),
                                 waits=wl, inc=s_sqF))

                def fr(e, tp=tp, c=c, hb=hb):
                    e.tensor_scalar(out=qi32[:, hb:hb + 1024],
                                    in0=y32[:, hb:hb + 1024],
                                    scalar1=dnv[:, 2:3], scalar2=None,
                                    op0=AL.mult)
                    e.drain()
                    return e.tensor_scalar(
                        out=rbufF[:, 8192 * tp + 1024 * c:
                                  8192 * tp + 1024 * (c + 1)],
                        in0=qi32[:, hb:hb + 1024], scalar1=0.0,
                        scalar2=None, op0=AL.add)
                qF_done.append(o("vector", fr,
                                 waits=[(s_sqF, sq_done[-1])], inc=s_qF))
            for wv in range(2):
                wl = [(s_qF, qF_done[-1] if wv == 1 else qF_done[-5])]
                if len(khst_done) >= 1:
                    wl.append((s_khst, khst_done[-1]))
                tpb(zq, 0, 128, rbufF, 8192 * tp + 4096 * wv, 128, 32, wl)

                def fst(e, tt=tt, wv=wv):
                    ia = bass.AP(zq, 0, [[4096, P], [128, 32], [1, 128]])
                    oa = bass.AP(khat_d, 4096 * wv * T + 128 * tt,
                                 [[1024, P], [131072, 32], [1, 128]])
                    return e.dma_start(out=oa, in_=ia)
                khst_done.append(o("sync", fst, waits=TP.mark(),
                                   inc=s_khst, k=16, kind="d"))
    lhG_done, wG_done, peG_done, evG_done = [], [], [], []
    for kq in range(4):
        def fld(e, kq=kq):
            ia = bass.AP(khat_d, 2048 * kq * T,
                         [[1024, P], [131072, 16], [1, 1024]])
            oa = bass.AP(big_a, 0, [[16384, P], [1024, 16], [1, 1024]])
            return e.dma_start(out=oa, in_=ia)
        wl = [(s_khst, khst_done[-1]), (s_peF, peF_done[-1]),
              (s_qF, qF_done[-1]), (s_rF, rF_done[-1])]
        if kq >= 1:
            wl.append((s_peG, peG_done[-1]))
        lhG_done.append(o("sync", fld, waits=wl, inc=s_lhG, k=16,
                          kind="d"))
        for og in range(4):
            j = 4 * kq + og
            wl = [(s_peP, PEPF), (s_pcp, 1)]
            if j >= 2:
                wl.append((s_peG, peG_done[8 * (j - 1) - 1]))
            wG_done.append(wslice_load(og % 2, qcp, 512 * og, NKD, 512,
                                       s_wG, wl, row0=2048 * kq,
                                       cast=False))
            for tt in range(NT):
                def fmm(e, og=og, tt=tt):
                    last = None
                    for ki in range(NKD):
                        last = e.matmul(
                            _bk(ps, tt % 2),
                            big_a[:, 1024 * ki + 128 * tt:
                                  1024 * ki + 128 * (tt + 1)],
                            wbuf[:, 8192 * (og % 2) + 512 * ki:
                                 8192 * (og % 2) + 512 * (ki + 1)],
                            start=(ki == 0), stop=(ki == NKD - 1))
                    return last
                wl = [(s_lhG, lhG_done[kq]), (s_wG, wG_done[j]),
                      (s_evP, EVPF)]
                n = len(peG_done)
                if n >= 2:
                    wl.append((s_evG, evG_done[n - 2]))
                peG_done.append(o("tensor", fmm, waits=wl, inc=s_peG,
                                  kind="m"))
                evG_done.append(o(
                    "vector", lambda e, og=og, tt=tt:
                    e.scalar_tensor_tensor(
                        out=x_sb[:, D * tt + 512 * og:
                                 D * tt + 512 * (og + 1)],
                        in0=_bk(ps, tt % 2), scalar=cfv[:, tt:tt + 1],
                        in1=x_sb[:, D * tt + 512 * og:
                                 D * tt + 512 * (og + 1)],
                        op0=AL.mult, op1=AL.add),
                    waits=[(s_peG, peG_done[-1])], inc=s_evG))

    _finish(nc, em, o, st, x_sb, y_out, [(s_evG, evG_done[-1])])
    em.emit()
    return nc


def _finish(nc, em, o, st, x_sb, y_out, waits, dumps=()):
    s_out = st("s_out")
    vals = []
    for tt in range(NT):
        vals.append(o("sync", lambda e, tt=tt: e.dma_start(
            out=y_out[P * tt:P * (tt + 1), :],
            in_=x_sb[:, D * tt:D * (tt + 1)]), waits=waits, inc=s_out,
            k=16, kind="d"))
    for (name, sb) in dumps:
        dshape = [sb.shape[0], sb.shape[1]]
        dten = nc.dram_tensor("dbg_" + name, dshape, sb.dtype,
                              kind="ExternalOutput")
        vals.append(o("sync", lambda e, dten=dten, sb=sb: e.dma_start(
            out=dten[:, :], in_=sb[:, :]), waits=waits, inc=s_out,
            k=16, kind="d"))
    o("sync", None, waits=[(s_out, vals[-1])])


def _hadamard(n):
    h = np.array([[1.0]], dtype=np.float32)
    while h.shape[0] < n:
        h = np.block([[h, h], [h, -h]])
    return h


def _dev_wstat(wT):
    K, O = wT.shape
    nki, noc = K // P, O // P
    w = wT.reshape(nki, P, noc, P).transpose(1, 2, 0, 3).reshape(
        P, noc, nki * P)
    return np.ascontiguousarray(w)


_NC_CACHE = {}


def kernel(L0_attn_x, L0_attn_Wq, L0_attn_bq, L0_attn_Wk, L0_attn_bk,
           L0_attn_Wv, L0_attn_bv, L0_attn_Wout, L0_attn_bout,
           L0_ffn_w_up, L0_ffn_w_down, L1_attn_w_qkv, L1_attn_w_proj,
           L1_attn_b_proj, L1_ffn_c_fc, L1_ffn_c_proj):
    f32 = np.float32
    x = np.ascontiguousarray(np.asarray(L0_attn_x, f32).reshape(2 * SB, D))
    wqT = np.asarray(L0_attn_Wq, f32).T
    wkT = np.asarray(L0_attn_Wk, f32).T
    wvT = np.asarray(L0_attn_Wv, f32).T
    woT = np.ascontiguousarray(np.asarray(L0_attn_Wout, f32).T)
    wqkvT = np.asarray(L1_attn_w_qkv, f32).T
    wq1T = wqkvT[:, 0:D]
    wkvT = np.ascontiguousarray(wqkvT[:, D:3 * D])
    wprojT = np.ascontiguousarray(np.asarray(L1_attn_w_proj, f32).T)
    w_upT = np.ascontiguousarray(np.asarray(L0_ffn_w_up, f32).T)
    w_dnT = np.ascontiguousarray(np.asarray(L0_ffn_w_down, f32).T)
    cfc = np.ascontiguousarray(np.asarray(L1_ffn_c_fc, f32))
    cpjT = np.ascontiguousarray(np.asarray(L1_ffn_c_proj, f32).T)

    wq_d = _dev_wstat(wqT)
    wk_d = _dev_wstat(wkT)
    wv_d = _dev_wstat(wvT)
    wq1_d = _dev_wstat(np.ascontiguousarray(wq1T))

    def fmb(b):
        return np.ascontiguousarray(np.asarray(b, f32).reshape(NKD, P).T)

    h1024 = _hadamard(1024).astype(ml_dtypes.bfloat16)

    key = DEBUG_STOP
    if key not in _NC_CACHE:
        _NC_CACHE[key] = build_program(stop=DEBUG_STOP)
    nc = _NC_CACHE[key]

    in_maps = []
    for c in range(8):
        in_maps.append({
            "x_in": np.ascontiguousarray(x[T * c:T * (c + 1)]),
            "wq_d": wq_d, "wk_d": wk_d, "wv_d": wv_d, "wq1_d": wq1_d,
            "woT": woT, "wkvT": wkvT, "wprojT": wprojT, "h1024": h1024,
            "bq_in": fmb(L0_attn_bq), "bk_in": fmb(L0_attn_bk),
            "bv_in": fmb(L0_attn_bv),
            "bo_row": np.ascontiguousarray(
                np.asarray(L0_attn_bout, f32)[None, :]),
            "bpj_row": np.ascontiguousarray(
                np.asarray(L1_attn_b_proj, f32)[None, :]),
            "shup": np.ascontiguousarray(w_upT[256 * c:256 * (c + 1)]),
            "shdn": np.ascontiguousarray(w_dnT[512 * c:512 * (c + 1)]),
            "shcf": np.ascontiguousarray(cfc[256 * c:256 * (c + 1)]),
            "shcp": np.ascontiguousarray(cpjT[1024 * c:1024 * (c + 1)]),
        })
    res = run_bass_kernel_spmd(nc, in_maps, core_ids=list(range(8)))
    out = np.concatenate([res.results[c]["y_out"] for c in range(8)],
                         axis=0)
    return out.reshape(2, SB, D)


# version 14
# speedup vs baseline: 1.0380x; 1.0161x over previous
import numpy as np
import ml_dtypes
import concourse.bass as bass
import concourse.mybir as mybir
from concourse.bass_utils import run_bass_kernel_spmd

F32 = mybir.dt.float32
BF16 = mybir.dt.bfloat16
I32 = mybir.dt.int32
AL = mybir.AluOpType
AF = mybir.ActivationFunctionType
AX = mybir.AxisListType.X

P = 128
T = 1024
NT = 8
D = 2048
NKD = 16
DI = 4096
H2 = 8192
SB = 4096
NKC = 32
RS128 = float(1.0 / np.sqrt(128.0))
C_UP = float(1.0 / (127.0 * np.sqrt(2048.0)))
C_DN = float(1.0 / (127.0 * np.sqrt(4096.0)))
EPS_U = float(1e-5 * np.sqrt(2048.0))
EPS_Z = float(1e-5 * np.sqrt(4096.0))
TW_NUMEL = [D * DI, DI * D, D * H2, H2 * D]

DEBUG_STOP = None


class St:
    def __init__(self, nc, name):
        self.sem = nc.alloc_semaphore(name)
        self.n = 0


class Pl:

    def __init__(self, mk, n, name):
        self.sts = [mk("%s_%d" % (name, i)) for i in range(n)]
        self.i = 0

    def nxt(self):
        s = self.sts[self.i % len(self.sts)]
        self.i += 1
        return s

    def mark(self):
        return [(s, s.n) for s in self.sts]


def _flat(waits):
    out = []
    for w in waits:
        if isinstance(w, list):
            out.extend(w)
        else:
            out.append(w)
    return out


class Em:

    def __init__(self, nc):
        self.nc = nc
        self.ops = []

    def o(self, eng, fn=None, waits=(), inc=None, k=1, kind="c"):
        post = None
        waits = _flat(list(waits))
        if inc is not None:
            if kind == "d":
                waits.append((inc, inc.n))
            inc.n += k
            post = inc.n
        self.ops.append((eng, fn, tuple(waits), inc, k, kind, post))
        return post

    def emit(self):
        nc = self.nc
        with nc.Block() as blk:
            def mk(name):
                def f(eng):
                    water = {}
                    prev_c = False
                    for (e, fn, waits, inc, k, kind, post) in self.ops:
                        if e != name:
                            continue
                        for (stm, v) in waits:
                            if v > 0 and v > water.get(stm.sem.num, 0):
                                eng.wait_ge(stm.sem, v)
                                water[stm.sem.num] = v
                        if fn is None:
                            ins = (eng.drain() if name in
                                   ("vector", "scalar", "gpsimd")
                                   else eng.nop())
                        else:
                            if kind == "c" and prev_c and name in (
                                    "vector", "scalar", "gpsimd"):
                                eng.drain()
                            ins = fn(eng)
                        prev_c = (kind == "c" and fn is not None)
                        if inc is not None:
                            ins.then_inc(inc.sem, k)
                            if (kind != "d" and name != "gpsimd"
                                    and post > water.get(inc.sem.num, 0)):
                                water[inc.sem.num] = post
                    return None
                return f
            blk.sync(mk("sync"))
            blk.gpsimd(mk("gpsimd"))
            blk.scalar(mk("scalar"))
            blk.vector(mk("vector"))
            blk.tensor(mk("tensor"))


def _bk(ps, b, n=1):
    return ps[:, 512 * b:512 * (b + n)]


def build_program(stop=None):
    nc = bass.Bass("TRN2", target_bir_lowering=False)
    DT = nc.dram_tensor
    SA = nc.alloc_sbuf_tensor

    x_in = DT("x_in", [T, D], F32, kind="ExternalInput")
    wq_d = DT("wq_d", [P, NKD, D], F32, kind="ExternalInput")
    wk_d = DT("wk_d", [P, NKD, D], F32, kind="ExternalInput")
    wv_d = DT("wv_d", [P, NKD, D], F32, kind="ExternalInput")
    wq1_d = DT("wq1_d", [P, NKD, D], F32, kind="ExternalInput")
    woT = DT("woT", [D, D], F32, kind="ExternalInput")
    wkvT = DT("wkvT", [D, 2 * D], F32, kind="ExternalInput")
    wprojT = DT("wprojT", [D, D], F32, kind="ExternalInput")
    h1024 = DT("h1024", [1024, 1024], BF16, kind="ExternalInput")
    bq_in = DT("bq_in", [P, NKD], F32, kind="ExternalInput")
    bk_in = DT("bk_in", [P, NKD], F32, kind="ExternalInput")
    bv_in = DT("bv_in", [P, NKD], F32, kind="ExternalInput")
    bo_row = DT("bo_row", [1, D], F32, kind="ExternalInput")
    bpj_row = DT("bpj_row", [1, D], F32, kind="ExternalInput")
    shup = DT("shup", [D // 8, DI], F32, kind="ExternalInput")
    shdn = DT("shdn", [DI // 8, D], F32, kind="ExternalInput")
    shcf = DT("shcf", [D // 8, H2], F32, kind="ExternalInput")
    shcp = DT("shcp", [H2 // 8, D], F32, kind="ExternalInput")
    y_out = DT("y_out", [T, D], F32, kind="ExternalOutput")

    ws_in = DT("ws_in", [1, 4], F32)
    ws_out = DT("ws_out", [1, 4], F32)
    qup_in = DT("qup_in", [D // 8, DI], BF16)
    qdn_in = DT("qdn_in", [DI // 8, D], BF16)
    qcf_in = DT("qcf_in", [D // 8, H2], BF16)
    qcp_in = DT("qcp_in", [H2 // 8, D], BF16)
    qup = DT("qup", [D, DI], BF16)
    qdn = DT("qdn", [DI, D], BF16)
    qcf = DT("qcf", [D, H2], BF16)
    qcp = DT("qcp", [H2, D], BF16)
    k_in = DT("k_in", [D, T], BF16)
    v_in = DT("v_in", [D, T], BF16)
    n_in = DT("n_in", [1, NKD], BF16)
    k_all = DT("k_all", [4 * D, T], BF16)
    v_all = DT("v_all", [4 * D, T], BF16)
    n_all = DT("n_all", [4, NKD], BF16)
    vk_in = DT("vk_in", [32, 2112], F32)
    vk_out = DT("vk_out", [32, 2112], F32)
    wsum_d = DT("wsum_d", [P, 4], F32)
    xq2_d = DT("xq2_d", [DI, T], BF16)
    khat_d = DT("khat_d", [H2, T], BF16)

    x_sb = SA("x_sb", [P, NT * D], F32)
    big_a = SA("big_a", [P, 16384], BF16)
    big_b = SA("big_b", [P, 16384], BF16)
    wbuf = SA("wbuf", [P, 16384], BF16)
    wt_sb = SA("wt_sb", [P, 4096], BF16)
    lnt = SA("lnt", [P, 2 * D], BF16)
    bc_sb = SA("bc_sb", [P, D], BF16)
    psb = SA("psb", [P, 2 * T], BF16)
    ktile = SA("ktile", [P, 2 * P], BF16)
    vsb = SA("vsb", [P, 288], BF16)
    vaug = SA("vaug", [P, 2 * 2112], BF16)
    st6 = SA("st6", [P, 24], F32)
    agg = SA("agg", [P, 2], F32)
    c_a = SA("c_a", [P, 4], F32)
    qnm = SA("qnm", [1, NKD], F32)
    knm = SA("knm", [1, NKD], F32)
    nrow = SA("nrow", [1, 4 * NKD], BF16)
    n_bf = SA("n_bf", [1, NKD], BF16)
    ksq = SA("ksq", [P, 512], BF16)
    nmx = SA("nmx", [1, 2], F32)
    biasneg = SA("biasneg", [P, NKD], F32)
    bq_sb = SA("bq_sb", [P, NKD], F32)
    bk_sb = SA("bk_sb", [P, NKD], F32)
    bv_sb = SA("bv_sb", [P, NKD], F32)
    wacc = SA("wacc", [P, 4], F32)
    wtmp = SA("wtmp", [P, 4], F32)
    sred = SA("sred", [1, 512], F32)
    wsrow = SA("wsrow", [1, 8], F32)
    negrow = SA("negrow", [1, NKD], F32)
    negrow_bf = SA("negrow_bf", [1, NKD], BF16)
    sc_bc = SA("sc_bc", [P, 4], F32)
    thr_bc = SA("thr_bc", [P, 4], F32)
    nthr_bc = SA("nthr_bc", [P, 4], F32)
    ones_c = SA("ones_c", [P, 1], BF16)
    ones_r = SA("ones_r", [1, P], BF16)
    ones_r32 = SA("ones_r32", [1, P], F32)
    dnv = SA("dnv", [P, 4], F32)
    cfv = SA("cfv", [P, NT], F32)
    cfv2 = SA("cfv2", [P, NT], F32)
    rcd = SA("rcd", [P, 4], F32)
    mmx = SA("mmx", [P, 4], F32)
    vkacc = SA("vkacc", [32, 2112], F32)

    ps = nc.alloc_psum_tensor("ps", [P, 4096], F32)

    def at(base, off_bytes, name, shape, dt):
        a = nc.lookup_mloc(base).addr
        return nc.alloc_sbuf_tensor_at(name, shape, dt, offset=a + off_bytes)

    qtA = at(wbuf, 0, "qtA", [P, 2048], F32)
    qtB = at(wbuf, 8192, "qtB", [P, 2048], F32)
    tq1 = at(psb, 0, "tq1", [P, 2048], BF16)
    tq2 = at(bc_sb, 0, "tq2", [P, 2048], BF16)
    ksl = at(vaug, 0, "ksl", [P, 2 * 512], BF16)
    fmt = at(big_b, 2 * 4096, "fmt", [P, 8192], BF16)
    zq = at(big_b, 2 * 12288, "zq", [P, 4096], BF16)
    qi32 = at(wt_sb, 0, "qi32", [P, 2048], I32)
    y32 = at(vaug, 0, "y32", [P, 2048], F32)
    vkT = at(vaug, 0, "vkT", [P, 2112], BF16)
    ktm2 = at(wt_sb, 0, "ktm2", [P, 4096], BF16)
    rD0 = at(big_b, 0, "rD0", [P, 4096], BF16)
    rD1 = at(big_a, 2 * 12288, "rD1", [P, 4096], BF16)

    def rbufD(tp):
        return rD0 if tp == 0 else rD1

    em = Em(nc)
    o = em.o
    stD = {}

    def st(name):
        stD[name] = St(nc, name)
        return stD[name]

    TP = Pl(st, 5, "tp")
    KL = Pl(st, 4, "kl")
    VL = Pl(st, 4, "vl")
    KS = Pl(st, 4, "ks")

    def tpb(dst, do_, dbs, src, so_, sbs, nblk, waits, grp=8):
        dw, sw = dst.shape[1], src.shape[1]
        for g0 in range(0, nblk, grp):
            g = min(grp, nblk - g0)

            def f(e, g0=g0, g=g):
                ia = bass.AP(src, so_ + sbs * g0,
                             [[sw, P], [sbs, g], [1, 128]])
                oa = bass.AP(dst, do_ + dbs * g0,
                             [[dw, P], [dbs, g], [1, 128]])
                return e.dma_start_transpose(out=oa, in_=ia)
            o("sync", f, waits=waits, inc=TP.nxt(), k=16, kind="d")

    s_x = st("s_x")
    s_bs = st("s_bs")
    s_ini = st("s_ini")
    s_q0 = st("s_q0")
    s_qv = st("s_qv")
    s_c0 = st("s_c0")

    for tt in range(NT):
        o("sync", lambda e, tt=tt: e.dma_start(
            out=x_sb[:, D * tt:D * (tt + 1)],
            in_=x_in[P * tt:P * (tt + 1), :]), inc=s_x, k=16, kind="d")
    XF = s_x.n
    o("sync", lambda e: e.dma_start(out=bq_sb[:, :], in_=bq_in[:, :]),
      inc=s_bs, k=16, kind="d")
    o("sync", lambda e: e.dma_start(out=bk_sb[:, :], in_=bk_in[:, :]),
      inc=s_bs, k=16, kind="d")
    o("sync", lambda e: e.dma_start(out=bv_sb[:, :], in_=bv_in[:, :]),
      inc=s_bs, k=16, kind="d")
    BSF = s_bs.n
    o("gpsimd", lambda e: e.memset(ones_c[:, :], 1.0))
    o("gpsimd", lambda e: e.memset(ones_r[:, :], 1.0))
    o("gpsimd", lambda e: e.memset(vsb[:, 128:129], 1.0))
    o("gpsimd", lambda e: e.memset(vsb[:, 272:273], 1.0))
    INI = o("gpsimd", lambda e: e.memset(ones_r32[:, :], 1.0), inc=s_ini)
    v0 = o("vector", lambda e: e.memset(wacc[:, :], 0.0), inc=s_qv)
    VK0 = o("vector", lambda e: e.memset(vkacc[:, :], 0.0), inc=s_qv)

    qjobs = []
    for r in range(2):
        for cs in range(2):
            qjobs.append((shup, 128 * r, 2048 * cs, 2048, qup_in, 0))
    for r in range(4):
        qjobs.append((shdn, 128 * r, 0, 2048, qdn_in, 1))
    for r in range(2):
        for cs in range(4):
            qjobs.append((shcf, 128 * r, 2048 * cs, 2048, qcf_in, 2))
    for r in range(8):
        qjobs.append((shcp, 128 * r, 0, 2048, qcp_in, 3))
    NQJ = len(qjobs)

    QL = Pl(st, 2, "ql")
    qv_done = []
    for j, (dr, r0, c0, w, _, wi) in enumerate(qjobs):
        buf = qtA if j % 2 == 0 else qtB
        wl = [] if j < 2 else [(s_qv, qv_done[j - 2])]
        stm = QL.nxt()
        ld = (stm, o("sync", lambda e, dr=dr, r0=r0, c0=c0, w=w, buf=buf:
                     e.dma_start(out=buf[:, 0:w],
                                 in_=dr[r0:r0 + P, c0:c0 + w]),
                     waits=wl, inc=stm, k=16, kind="d"))
        o("vector", lambda e, buf=buf, w=w, wi=wi: e.reduce_sum(
            out=wtmp[:, wi:wi + 1], in_=buf[:, 0:w], axis=AX,
            apply_absolute_value=True), waits=[ld])
        qv_done.append(o("vector", lambda e, wi=wi: e.tensor_tensor(
            out=wacc[:, wi:wi + 1], in0=wacc[:, wi:wi + 1],
            in1=wtmp[:, wi:wi + 1], op=AL.add), inc=s_qv))

    c1 = o("gpsimd", lambda e: e.dma_start(out=wsum_d[:, :], in_=wacc[:, :]),
           waits=[(s_qv, qv_done[-1])], inc=s_c0, k=16, kind="d")
    c2 = o("gpsimd", lambda e: e.dma_start(
        out=sred[:, :], in_=bass.AP(wsum_d, 0, [[512, 1], [1, 512]])),
        waits=[(s_c0, c1)], inc=s_c0, k=16, kind="d")
    for wi in range(4):
        o("vector", lambda e, wi=wi: e.reduce_sum(
            out=wsrow[:, wi:wi + 1],
            in_=bass.AP(sred, wi, [[512, 1], [4, 128]]), axis=AX),
          waits=[(s_c0, c2)])
    s_c0v = st("s_c0v")
    s_ar0 = st("s_ar0")
    c3 = o("vector", None, inc=s_c0v)
    c4 = o("gpsimd", lambda e: e.dma_start(out=ws_in[:, :],
                                           in_=wsrow[:, 0:4]),
           waits=[(s_c0v, c3)], inc=s_c0, k=16, kind="d")
    c5 = o("gpsimd", lambda e: e.collective_compute(
        "AllReduce", AL.add, replica_groups=[list(range(8))],
        ins=[ws_in.ap().opt()], outs=[ws_out.ap().opt()]),
        waits=[(s_c0, c4)], inc=s_ar0, kind="d")
    c6 = o("gpsimd", lambda e: e.dma_start(out=wsrow[:, 4:8],
                                           in_=ws_out[:, :]),
           waits=[(s_ar0, c5)], inc=s_c0, k=16, kind="d")
    for wi in range(4):
        o("vector", lambda e, wi=wi: e.tensor_scalar(
            out=wsrow[:, 4 + wi:5 + wi], in0=wsrow[:, 4 + wi:5 + wi],
            scalar1=1.0 / TW_NUMEL[wi], scalar2=1e-5, op0=AL.mult,
            op1=AL.max), waits=[(s_c0, c6)])
    c7 = o("vector", None, inc=s_c0v)
    c8 = o("gpsimd", lambda e: e.dma_start(out=ws_in[:, :],
                                           in_=wsrow[0:1, 4:8]),
           waits=[(s_c0v, c7)], inc=s_c0, k=16, kind="d")
    c9 = o("gpsimd", lambda e: e.dma_start(
        out=sc_bc[:, :], in_=bass.AP(ws_in, 0, [[0, P], [1, 4]])),
        waits=[(s_c0, c8)], inc=s_c0, k=16, kind="d")

    def fscl(e):
        e.tensor_scalar(out=thr_bc[:, :], in0=sc_bc[:, :],
                        scalar1=0.5, scalar2=None, op0=AL.mult)
        e.drain()
        return e.tensor_scalar(out=nthr_bc[:, :], in0=thr_bc[:, :],
                               scalar1=-1.0, scalar2=None, op0=AL.mult)
    SCL = o("vector", fscl, waits=[(s_c0, c9)], inc=s_c0v)

    s_q2 = st("s_q2")
    s_qs = st("s_qs")
    s_qw = st("s_qw")
    s_qst = st("s_qst")
    s_p0 = st("s_p0")
    s_pdn = st("s_pdn")
    s_pcf = st("s_pcf")
    s_pcp = st("s_pcp")

    qst_done = []
    qw_done = []
    for j, (dr, r0, c0, w, qd, wi) in enumerate(qjobs):
        buf = qtA if j % 2 == 0 else qtB
        wl = [(s_c0v, SCL)] if j < 2 else [(s_qw, qw_done[j - 2])]
        stm2 = QL.nxt()
        ld = (stm2, o("sync", lambda e, dr=dr, r0=r0, c0=c0, w=w, buf=buf:
                      e.dma_start(out=buf[:, 0:w],
                                  in_=dr[r0:r0 + P, c0:c0 + w]),
                      waits=wl, inc=stm2, k=16, kind="d"))

        def fq1(e, buf=buf, w=w, wi=wi):
            e.tensor_scalar(out=tq2[:, 0:w], in0=buf[:, 0:w],
                            scalar1=nthr_bc[:, wi:wi + 1], scalar2=None,
                            op0=AL.is_lt)
            e.drain()
            return e.tensor_scalar(out=tq1[:, 0:w], in0=buf[:, 0:w],
                                   scalar1=thr_bc[:, wi:wi + 1],
                                   scalar2=None, op0=AL.is_gt)
        wl_q = [ld]
        if j >= 1:
            wl_q.append((s_qst, qst_done[j - 1]))
        o("vector", fq1, waits=wl_q)
        qw_done.append(o("vector", lambda e, w=w: e.tensor_tensor(
            out=tq1[:, 0:w], in0=tq1[:, 0:w], in1=tq2[:, 0:w],
            op=AL.subtract), inc=s_qw))
        qst_done.append(o("sync", lambda e, qd=qd, r0=r0, c0=c0, w=w:
                          e.dma_start(out=qd[r0:r0 + P, c0:c0 + w],
                                      in_=tq1[:, 0:w]),
                          waits=[(s_qw, qw_done[-1])], inc=s_qst, k=16,
                          kind="d"))
    QST_ALL = qst_done[-1]

    s_lnc = st("s_lnc")

    def ln_chain(tt, slot, pre):
        s_l = s_lnc
        base = s_l.n
        xt = x_sb[:, D * tt:D * (tt + 1)]

        def f1(e):
            for c in range(4):
                e.bn_stats(out=st6[:, 6 * c:6 * (c + 1)],
                           in_=xt[:, 512 * c:512 * (c + 1)])
                e.drain()
            e.bn_aggr(out=agg[:, :], in_=st6[:, :])
            e.drain()
            return e.tensor_scalar(out=c_a[:, 1:2], in0=agg[:, 1:2],
                                   scalar1=1e-5, scalar2=None, op0=AL.add)
        o("vector", f1, waits=pre, inc=s_l)
        o("scalar", lambda e: e.activation(
            out=c_a[:, 1:2], in_=c_a[:, 1:2], func=AF.Sqrt),
          waits=[(s_l, base + 1)], inc=s_l)

        def f2(e):
            e.reciprocal(out=c_a[:, 2:3], in_=c_a[:, 1:2])
            e.drain()
            e.scalar_tensor_tensor(out=c_a[:, 3:4], in0=agg[:, 0:1],
                                   scalar=-1.0, in1=c_a[:, 2:3],
                                   op0=AL.mult, op1=AL.mult)
            e.drain()
            return e.tensor_scalar(
                out=lnt[:, D * slot:D * (slot + 1)], in0=xt,
                scalar1=c_a[:, 2:3], scalar2=c_a[:, 3:4],
                op0=AL.mult, op1=AL.add)
        o("vector", f2, waits=[(s_l, base + 2)], inc=s_l)
        return (s_l, base + 3)

    def wslice_load(dst_half, src, og_col, n_ki, colw, stream, waits,
                    row0=0, cast=True):
        ncols = src.shape[1]

        def f(e):
            ia = bass.AP(src, row0 * ncols + og_col,
                         [[ncols, P], [P * ncols, n_ki], [1, colw]])
            oa = bass.AP(wbuf, 8192 * dst_half,
                         [[16384, P], [colw, n_ki], [1, colw]])
            return e.dma_start(out=oa, in_=ia)
        return o("gpsimd" if cast else "sync", f, waits=waits, inc=stream,
                 k=16, kind="d")

    s_wA = st("s_wA")
    s_peA = st("s_peA")
    s_evA = st("s_evA")
    s_ccA = st("s_ccA")
    s_kamx = st("s_kamx")
    s_sqA = st("s_sqA")
    s_nmm = st("s_nmm")
    s_nmv = st("s_nmv")
    nmv_done = []

    lhsA_done = []
    LHA0 = None
    for tt in range(NT):
        pre = [(s_x, 16 * (tt + 1))]
        if tt >= 2:
            pre.append(lhsA_done[tt - 2])
        s_l, lv = ln_chain(tt, tt % 2, pre)
        tpb(big_a, 128 * tt, 1024, lnt, D * (tt % 2), 128, NKD,
            [(s_l, lv)])
        lhsA_done.append(TP.mark())
        if tt == 3:
            LHA0 = TP.mark()
    LHA = lhsA_done[-1]

    wproj_srcs = [wk_d, wv_d, wq_d]
    wA_done = []
    for pi in range(3):
        for oc in range(NKD):
            j = pi * NKD + oc
            wl = [] if j < 2 else [(s_peA, 2 * (j - 1))]
            wA_done.append(o(
                "gpsimd", lambda e, pi=pi, oc=oc, j=j: e.dma_start(
                    out=wt_sb[:, 2048 * (j % 2):2048 * (j % 2) + 2048],
                    in_=wproj_srcs[pi][:, oc, :]),
                waits=wl, inc=s_wA, k=16, kind="d"))
    peA_done = []
    evA_done = []
    kst_done = []
    nmv_done = []
    KSK = KSV = None
    for pi in range(3):
        bias = (bk_sb, bv_sb, bq_sb)[pi]
        for oc in range(NKD):
            j = pi * NKD + oc
            for g2 in range(2):
                bidx = 2 * (j % 2) + g2

                def fmm(e, j=j, g2=g2, bidx=bidx):
                    last = None
                    for ki in range(NKD):
                        last = e.matmul(
                            _bk(ps, bidx),
                            wt_sb[:, 2048 * (j % 2) + 128 * ki:
                                  2048 * (j % 2) + 128 * (ki + 1)],
                            big_a[:, 1024 * ki + 512 * g2:
                                  1024 * ki + 512 * (g2 + 1)],
                            start=(ki == 0), stop=(ki == NKD - 1))
                    return last
                wl = [LHA0 if g2 == 0 else LHA, (s_wA, wA_done[j]),
                      (s_bs, BSF)]
                if j >= 2:
                    wl.append((s_evA, evA_done[2 * (j - 2) + 1]))
                peA_done.append(o("tensor", fmm, waits=wl, inc=s_peA,
                                  kind="m"))
                if pi == 2:
                    dst = big_b[:, 1024 * oc + 512 * g2:
                                1024 * oc + 512 * (g2 + 1)]
                    evA_done.append(o(
                        "scalar", lambda e, dst=dst, bidx=bidx, oc=oc,
                        bias=bias: e.activation(
                            out=dst, in_=_bk(ps, bidx), func=AF.Identity,
                            bias=bias[:, oc:oc + 1], scale=1.0),
                        waits=[(s_peA, peA_done[-1])], inc=s_evA))
                else:
                    nksl = len(kst_done)
                    half = nksl % 2
                    dst = ksl[:, 512 * half:512 * (half + 1)]
                    wl2 = [(s_peA, peA_done[-1])]
                    if nksl >= 2:
                        wl2.append(kst_done[nksl - 2])
                    evA_done.append(o(
                        "scalar", lambda e, dst=dst, bidx=bidx, oc=oc,
                        bias=bias: e.activation(
                            out=dst, in_=_bk(ps, bidx), func=AF.Identity,
                            bias=bias[:, oc:oc + 1], scale=1.0),
                        waits=wl2, inc=s_evA))
                    dstd = k_in if pi == 0 else v_in
                    kst_done.append((lambda stm: (stm, o(
                        "sync", lambda e, dstd=dstd, oc=oc, g2=g2, dst=dst:
                        e.dma_start(out=dstd[128 * oc:128 * (oc + 1),
                                             512 * g2:512 * (g2 + 1)],
                                    in_=dst),
                        waits=[(s_evA, evA_done[-1])], inc=stm, k=16,
                        kind="d")))(KS.nxt()))
                if pi != 1:
                    nj = len(nmv_done)
                    wl3 = [(s_evA, evA_done[-1])]
                    if nj >= 1:
                        wl3.append((s_nmm, nj))
                    sqv = o("scalar", lambda e, bidx=bidx: e.activation(
                        out=ksq[:, :], in_=_bk(ps, bidx), func=AF.Square),
                        waits=wl3, inc=s_sqA)
                    nb = 4 + (nj % 2)
                    wlm = [(s_sqA, sqv)]
                    if nj >= 2:
                        wlm.append((s_nmv, nmv_done[nj - 2]))
                    mmv = o("tensor", lambda e, nb=nb: e.matmul(
                        ps[0:1, 512 * nb:512 * nb + 512], ones_c[:, :],
                        ksq[:, :], start=True, stop=True),
                        waits=wlm, inc=s_nmm, kind="m")

                    def fnm(e, nb=nb, oc=oc, pi=pi, first=(g2 == 0)):
                        dstn = knm if pi == 0 else qnm
                        e.reduce_max(out=nmx[:, 0:1],
                                     in_=ps[0:1, 512 * nb:512 * nb + 512],
                                     axis=AX)
                        e.drain()
                        if first:
                            return e.tensor_copy(out=dstn[:, oc:oc + 1],
                                                 in_=nmx[:, 0:1])
                        return e.tensor_tensor(out=dstn[:, oc:oc + 1],
                                               in0=dstn[:, oc:oc + 1],
                                               in1=nmx[:, 0:1], op=AL.max)
                    nmv_done.append(o("vector", fnm, waits=[(s_nmm, mmv)],
                                      inc=s_nmv))
        if pi == 0:
            kmx = o("vector", lambda e: e.tensor_scalar(
                out=n_bf[:, :], in0=knm[:, :], scalar1=1.05, scalar2=None,
                op0=AL.mult), waits=[(s_nmv, nmv_done[-1])], inc=s_kamx)
            _nst = (lambda stm: (stm, o(
                "sync", lambda e: e.dma_start(out=n_in[:, :],
                                              in_=n_bf[:, :]),
                waits=[(s_kamx, kmx)], inc=stm, k=16, kind="d")))(KS.nxt())
            KSK = KS.mark()
            grp4 = [[0, 1, 2, 3], [4, 5, 6, 7]]
            ccA1 = o("gpsimd", lambda e: e.collective_compute(
                "AllGather", AL.bypass, replica_groups=grp4,
                ins=[n_in.ap().opt()], outs=[n_all.ap().opt()]),
                waits=KSK, inc=s_ccA, kind="d")
            ccA2 = o("gpsimd", lambda e: e.collective_compute(
                "AllGather", AL.bypass, replica_groups=grp4,
                ins=[k_in.ap().opt()], outs=[k_all.ap().opt()]),
                waits=KSK, inc=s_ccA, kind="d")
        elif pi == 1:
            KSV = KS.mark()
            ccA3 = o("gpsimd", lambda e: e.collective_compute(
                "AllGather", AL.bypass, replica_groups=grp4,
                ins=[v_in.ap().opt()], outs=[v_all.ap().opt()]),
                waits=KSV, inc=s_ccA, kind="d")
            VAUG1 = o("gpsimd", lambda e: e.memset(
                bass.AP(vaug, 32, [[4224, P], [33, 128]]), 1.0),
                waits=KSV, inc=s_ini)

    EVAF = evA_done[-1]

    s_nr = st("s_nr")
    s_peN = st("s_peN")
    s_sc = st("s_sc")
    s_ex = st("s_ex")
    s_at = st("s_at")
    s_evB = st("s_evB")
    s_bo = st("s_bo")

    s_nrv = st("s_nrv")
    ld = o("gpsimd", lambda e: e.dma_start(
        out=nrow[:, :], in_=bass.AP(n_all, 0, [[64, 1], [1, 64]])),
        waits=[(s_ccA, ccA1)], inc=s_nr, k=16, kind="d")

    def fpr(e):
        e.tensor_copy(out=knm[:, :], in_=nrow[0:1, 0:NKD])
        e.drain()
        for r in range(1, 4):
            e.tensor_tensor(out=knm[:, :], in0=knm[:, :],
                            in1=nrow[0:1, NKD * r:NKD * (r + 1)],
                            op=AL.max)
            e.drain()
        return e.drain()
    pr1 = o("vector", fpr, waits=[(s_nr, ld)], inc=s_nrv)

    sq1 = o("scalar", lambda e: e.activation(
        out=qnm[:, :], in_=qnm[:, :], func=AF.Sqrt),
        waits=[(s_nrv, pr1), (s_nmv, nmv_done[-1])], inc=s_nrv)
    sq2 = o("scalar", lambda e: e.activation(
        out=knm[:, :], in_=knm[:, :], func=AF.Sqrt),
        waits=[(s_nrv, sq1)], inc=s_nrv)

    def fng(e):
        e.tensor_tensor(out=negrow[:, :], in0=qnm[:, :], in1=knm[:, :],
                        op=AL.mult)
        e.drain()
        e.tensor_scalar(out=negrow[:, :], in0=negrow[:, :],
                        scalar1=-1.05 * RS128, scalar2=None, op0=AL.mult)
        e.drain()
        return e.tensor_copy(out=negrow_bf[:, :], in_=negrow[:, :])
    ng1 = o("vector", fng, waits=[(s_nrv, sq2), (s_evA, EVAF),
                                  (s_ini, INI), (s_c0v, SCL)], inc=s_nrv)
    pn2 = o("tensor", lambda e: e.matmul(
        ps[:, 3584:3600], ones_r[:, :], negrow_bf[:, :], start=True,
        stop=True), waits=[(s_nrv, ng1)], inc=s_peN, kind="m")
    NRMF = o("vector", lambda e: e.tensor_copy(out=biasneg[:, :],
                                               in_=ps[:, 3584:3600]),
             waits=[(s_peN, pn2)], inc=s_nrv)

    s_bov = st("s_bov")
    b1 = o("gpsimd", lambda e: e.dma_start(
        out=bc_sb[:, :], in_=bass.AP(bo_row, 0, [[0, P], [1, D]])),
        waits=[(s_qst, qst_done[-1]), (s_qw, qw_done[-1])],
        inc=s_bo, k=16, kind="d")
    bo_done = []
    for tt in range(NT):
        bo_done.append(o("vector", lambda e, tt=tt: e.tensor_tensor(
            out=x_sb[:, D * tt:D * (tt + 1)],
            in0=x_sb[:, D * tt:D * (tt + 1)], in1=bc_sb[:, :], op=AL.add),
            waits=[(s_bo, b1), (s_evA, EVAF)], inc=s_bov))

    def at_slot(qc):
        b = 4 + qc // 3
        sl = qc % 3
        return ps[:, 512 * b + 129 * sl:512 * b + 129 * sl + 129]

    sc_done, ex_done, at_done, kld_done, vld_done = [], [], [], [], []
    evB_done = []
    pend_fat = [None]

    def emit_fat():
        if pend_fat[0] is not None:
            f, wl = pend_fat[0]
            at_done.append(o("tensor", f, waits=wl, inc=s_at, kind="m"))
            pend_fat[0] = None
    for h in range(16):
        for kc in range(NKC):
            i = h * NKC + kc
            wl = [(s_ccA, ccA2)]
            if i >= 2:
                wl.append((s_sc, sc_done[i - 2]))
            stm = KL.nxt()
            kld_done.append((stm, o(
                "sync", lambda e, h=h, kc=kc, i=i: e.dma_start(
                    out=ktile[:, 128 * (i % 2):128 * (i % 2) + 128],
                    in_=k_all[2048 * (kc // 8) + 128 * h:
                              2048 * (kc // 8) + 128 * (h + 1),
                              128 * (kc % 8):128 * (kc % 8) + 128]),
                waits=wl, inc=stm, k=16, kind="d")))
            wl = [(s_ccA, ccA3)]
            if i >= 2:
                wl.append((s_at, at_done[i - 2]))
            stm = VL.nxt()
            vld_done.append((stm, o(
                "sync", lambda e, h=h, kc=kc, i=i: e.dma_start_transpose(
                    out=vsb[:, 144 * (i % 2):144 * (i % 2) + 128],
                    in_=v_all[2048 * (kc // 8) + 128 * h:
                              2048 * (kc // 8) + 128 * (h + 1),
                              128 * (kc % 8):128 * (kc % 8) + 128]),
                waits=wl, inc=stm, k=16, kind="d")))

            def fsc(e, h=h, i=i):
                last = None
                for g2 in range(2):
                    last = e.matmul(
                        _bk(ps, 2 * (i % 2) + g2),
                        ktile[:, 128 * (i % 2):128 * (i % 2) + 128],
                        big_b[:, 1024 * h + 512 * g2:
                              1024 * h + 512 * (g2 + 1)],
                        start=True, stop=True)
                return last
            wl = [kld_done[i], (s_evA, EVAF)]
            if i >= 2:
                wl.append((s_ex, ex_done[i - 2]))
            sc_done.append(o("tensor", fsc, waits=wl, inc=s_sc, kind="m"))
            emit_fat()
            wl = [(s_sc, sc_done[i]), (s_nrv, NRMF)]
            if i == 0:
                wl.append((s_qw, qw_done[-1]))
                wl.append((s_qst, QST_ALL))
            if i >= 2:
                wl.append((s_at, at_done[i - 2]))
            ex_done.append(o("scalar", lambda e, i=i, h=h: e.activation(
                out=psb[:, 1024 * (i % 2):1024 * (i % 2) + 1024],
                in_=_bk(ps, 2 * (i % 2), 2), func=AF.Exp,
                bias=biasneg[:, h:h + 1], scale=RS128),
                waits=wl, inc=s_ex))

            def fat(e, kc=kc, i=i):
                last = None
                for qc in range(NT):
                    last = e.matmul(
                        at_slot(qc),
                        psb[:, 1024 * (i % 2) + 128 * qc:
                            1024 * (i % 2) + 128 * (qc + 1)],
                        vsb[:, 144 * (i % 2):144 * (i % 2) + 129],
                        start=(kc == 0 and qc % 3 == 0),
                        stop=(kc == NKC - 1 and (qc % 3 == 2 or qc == 7)))
                return last
            wl = [(s_ex, ex_done[i]), vld_done[i]]
            if kc == 0 and h > 0:
                wl.append((s_evB, evB_done[-1]))
            pend_fat[0] = (fat, wl)

        emit_fat()

        def fev(e, h=h):
            for qc in range(NT):
                e.reciprocal(out=rcd[:, 0:1], in_=at_slot(qc)[:, 128:129])
                e.drain()
                e.tensor_scalar(
                    out=big_a[:, 2048 * qc + 128 * h:
                              2048 * qc + 128 * (h + 1)],
                    in0=at_slot(qc)[:, 0:128], scalar1=rcd[:, 0:1],
                    scalar2=None, op0=AL.mult)
                e.drain()
            return e.drain()
        evB_done.append(o("vector", fev, waits=[(s_at, at_done[-1])],
                          inc=s_evB))
    EVBF = evB_done[-1]
    for (iin, iout, sem) in ((qup_in, qup, s_p0),):
        o("gpsimd", lambda e, iin=iin, iout=iout:
          e.collective_compute(
            "AllGather", AL.bypass, replica_groups=[list(range(8))],
            ins=[iin.ap().opt()], outs=[iout.ap().opt()]),
          waits=[(s_qst, QST_ALL), (s_at, at_done[int(len(at_done) * 0.7)])], inc=sem, kind="d")
    SCF = sc_done[-1]
    EXF = ex_done[-1]

    if stop == "B":
        _finish(nc, em, o, st, x_sb, y_out,
                [(s_evB, EVBF), (s_bov, bo_done[-1])],
                dumps=[("big_a", big_a), ("big_b", big_b),
                       ("biasneg", biasneg), ("qnm", qnm), ("knm", knm),
                       ("bc_sb", bc_sb)])
        em.emit()
        return nc

    s_wC = st("s_wC")
    s_peC = st("s_peC")
    s_evC = st("s_evC")

    for qc in range(NT):
        tpb(big_b, 128 * qc, 1024, big_a, 2048 * qc, 128, NKD,
            [(s_evB, EVBF), (s_sc, SCF)])
    LHC = TP.mark()
    wC_done, peC_done, evC_done = [], [], []
    for og in range(4):
        wl = [(s_qst, qst_done[-1])]
        if og >= 2:
            wl.append((s_peC, peC_done[8 * (og - 1) - 1]))
        wC_done.append(wslice_load(og % 2, woT, 512 * og, NKD, 512, s_wC,
                                   wl))
        for tt in range(NT):
            def fmm(e, og=og, tt=tt):
                last = None
                for ki in range(NKD):
                    last = e.matmul(
                        _bk(ps, tt % 2),
                        big_b[:, 1024 * ki + 128 * tt:
                              1024 * ki + 128 * (tt + 1)],
                        wbuf[:, 8192 * (og % 2) + 512 * ki:
                             8192 * (og % 2) + 512 * (ki + 1)],
                        start=(ki == 0), stop=(ki == NKD - 1))
                return last
            wl = [LHC, (s_wC, wC_done[og]), (s_ex, EXF)]
            n = len(peC_done)
            if n >= 2:
                wl.append((s_evC, evC_done[n - 2]))
            peC_done.append(o("tensor", fmm, waits=wl, inc=s_peC, kind="m"))
            evC_done.append(o(
                "vector", lambda e, og=og, tt=tt: e.scalar_tensor_tensor(
                    out=x_sb[:, D * tt + 512 * og:D * tt + 512 * (og + 1)],
                    in0=_bk(ps, tt % 2), scalar=1.0,
                    in1=x_sb[:, D * tt + 512 * og:
                             D * tt + 512 * (og + 1)],
                    op0=AL.mult, op1=AL.add),
                waits=[(s_peC, peC_done[-1]), (s_bov, bo_done[tt])],
                inc=s_evC))
    EVCF = evC_done[-1]
    PECF = peC_done[-1]

    if stop == "C":
        _finish(nc, em, o, st, x_sb, y_out, [(s_evC, EVCF)])
        em.emit()
        return nc

    for (iin, iout, sem) in ((qdn_in, qdn, s_pdn), (qcf_in, qcf, s_pcf), (qcp_in, qcp, s_pcp)):
        o("gpsimd", lambda e, iin=iin, iout=iout:
          e.collective_compute(
            "AllGather", AL.bypass, replica_groups=[list(range(8))],
            ins=[iin.ap().opt()], outs=[iout.ap().opt()]),
          waits=[(s_qst, QST_ALL)], inc=sem, kind="d")
    s_hld = st("s_hld")
    s_peD = st("s_peD")
    s_aqD = st("s_aqD")
    s_wD = st("s_wD")
    s_peU = st("s_peU")
    s_rD = st("s_rD")
    s_bfD = st("s_bfD")
    s_peH2 = st("s_peH2")
    s_zD = st("s_zD")
    s_aq2 = st("s_aq2")
    s_xqst = st("s_xqst")
    s_wDn = st("s_wDn")
    s_lhDn = st("s_lhDn")
    s_peDn = st("s_peDn")
    s_evD = st("s_evD")

    def fhl(e):
        ia = bass.AP(h1024, 0, [[1024, P], [P * 1024, 8], [1, 1024]])
        oa = bass.AP(big_a, 0, [[16384, P], [1024, 8], [1, 1024]])
        return e.dma_start(out=oa, in_=ia)
    HLD = o("sync", fhl, waits=LHC, inc=s_hld, k=16, kind="d")
    h_sb = big_a

    peD_done = []
    aqD_done = []
    x1T_done = []
    rD_done = []
    bfD_done = []
    zD_done = []
    aq2_done = []
    x2T_done = []
    xqst_done = []
    peU_done = []
    peH2_done = []

    for pr in range(4):
        pair_mark = TP.mark()
        for tp in range(2):
            tt = 2 * pr + tp
            slot = tt % 2
            pre = [(s_evC, EVCF), (s_peC, PECF)]
            if tt >= 2:
                pre += pair_mark
            s_l, lv = ln_chain(tt, slot, pre)

            def fbt(e, slot=slot):
                e.tensor_tensor(
                    out=fmt[:, 4096 * slot:4096 * slot + 1024],
                    in0=lnt[:, D * slot:D * slot + 1024],
                    in1=lnt[:, D * slot + 1024:D * slot + 2048], op=AL.add)
                e.drain()
                return e.tensor_tensor(
                    out=fmt[:, 4096 * slot + 1024:4096 * slot + 2048],
                    in0=lnt[:, D * slot:D * slot + 1024],
                    in1=lnt[:, D * slot + 1024:D * slot + 2048],
                    op=AL.subtract)
            wl = [(s_l, lv)]
            if tt >= 2:
                wl.append((s_peD, peD_done[tt - 2]))
                wl.append((s_xqst, xqst_done[tt - 2]))
            if tt >= 1 and pr >= 1:
                wl.append((s_peU, s_peU.n))
            bt = o("vector", fbt, waits=wl, inc=s_l)
            tpb(fmt, 4096 * slot + 2048, 128, fmt, 4096 * slot, 128, 16,
                [(s_l, bt)])
            lhsD_tile = TP.mark()
            last_pe = None
            for b in range(2):
                for og in range(2):
                    bidx = 4 * (tt % 2) + 2 * b + og

                    def fmm(e, slot=slot, b=b, og=og, bidx=bidx):
                        last = None
                        for ki in range(8):
                            last = e.matmul(
                                _bk(ps, bidx),
                                fmt[:, 4096 * slot + 2048 +
                                    128 * (8 * b + ki):
                                    4096 * slot + 2048 +
                                    128 * (8 * b + ki) + 128],
                                h_sb[:, 1024 * ki + 512 * og:
                                     1024 * ki + 512 * (og + 1)],
                                start=(ki == 0), stop=(ki == 7))
                        return last
                    wl = [lhsD_tile, (s_hld, HLD)]
                    if tt == 0:
                        wl += [(s_ex, EXF), (s_evB, EVBF), (s_nrv, NRMF)]
                    if tt >= 2:
                        wl.append((s_aqD, aqD_done[tt - 2]))
                    last_pe = o("tensor", fmm, waits=wl, inc=s_peD,
                                kind="m")
            peD_done.append(last_pe)
        for tp in range(2):
            tt = 2 * pr + tp

            def faq(e, tt=tt, bidx0=4 * (tt % 2)):
                sp = _bk(ps, bidx0, 4)
                e.reduce_max(out=dnv[:, 0:1], in_=sp, axis=AX,
                             apply_absolute_value=True)
                e.drain()
                e.tensor_scalar(out=dnv[:, 1:2], in0=dnv[:, 0:1],
                                scalar1=EPS_U, scalar2=None, op0=AL.max)
                e.drain()
                e.reciprocal(out=dnv[:, 2:3], in_=dnv[:, 1:2])
                e.drain()
                e.tensor_scalar(out=dnv[:, 2:3], in0=dnv[:, 2:3],
                                scalar1=127.0, scalar2=None, op0=AL.mult)
                e.drain()
                e.scalar_tensor_tensor(out=cfv[:, tt:tt + 1],
                                       in0=dnv[:, 1:2], scalar=C_UP,
                                       in1=sc_bc[:, 0:1], op0=AL.mult,
                                       op1=AL.mult)
                e.drain()
                e.tensor_scalar(out=qi32[:, 0:2048], in0=sp,
                                scalar1=dnv[:, 2:3], scalar2=None,
                                op0=AL.mult)
                e.drain()
                return e.tensor_scalar(
                    out=lnt[:, D * (tt % 2):D * (tt % 2) + 2048],
                    in0=qi32[:, 0:2048], scalar1=0.0, scalar2=None,
                    op0=AL.add)
            wl = [(s_peD, peD_done[tt]), (s_c0v, SCL)]
            if tt >= 2:
                wl.append(x1T_done[tt - 2])
            aqD_done.append(o("vector", faq, waits=wl, inc=s_aqD))
            tpb(big_a, 8192 + 128 * tp, 256, lnt, D * (tt % 2), 128, NKD,
                [(s_aqD, aqD_done[-1]), (s_hld, HLD)])
            x1T_done.append(TP.mark())
        for og in range(8):
            j = 8 * pr + og
            wl = [(s_p0, 1)]
            if j >= 2:
                wl.append((s_peU, peU_done[2 * (j - 2) + 1]))
            wD_j = wslice_load(og % 2, qup, 512 * og, NKD, 512, s_wD,
                               wl, cast=False)
            for tp in range(2):
                tt = 2 * pr + tp

                def fmm(e, og=og, tp=tp):
                    last = None
                    for ki in range(NKD):
                        last = e.matmul(
                            _bk(ps, 2 * (og % 2) + tp),
                            big_a[:, 8192 + 256 * ki + 128 * tp:
                                  8192 + 256 * ki + 128 * (tp + 1)],
                            wbuf[:, 8192 * (og % 2) + 512 * ki:
                                 8192 * (og % 2) + 512 * (ki + 1)],
                            start=(ki == 0), stop=(ki == NKD - 1))
                    return last
                wl = [x1T_done[-1], (s_wD, wD_j)]
                n = len(peU_done)
                if n >= 2:
                    wl.append((s_rD, rD_done[n - 2]))
                if pr >= 1 and og == 0 and tp == 0:
                    wl.append((s_bfD, s_bfD.n))
                peU_done.append(o("tensor", fmm, waits=wl, inc=s_peU,
                                  kind="m"))
                rD_done.append(o(
                    "scalar", lambda e, tp=tp, tt=tt, og=og:
                    e.activation(
                        out=rbufD(tp)[:, 512 * og:512 * (og + 1)],
                        in_=_bk(ps, 2 * (og % 2) + tp), func=AF.Relu,
                        scale=cfv[:, tt:tt + 1]),
                    waits=[(s_peU, peU_done[-1])], inc=s_rD))
        sq_v = {}
        lhsT_t = {}
        peH2_t = {}
        zD_t = {}

        def d3a(tp):
            tt = 2 * pr + tp
            slot = tt % 2
            sq_v[tp] = o("scalar", lambda e, tp=tp: e.activation(
                out=rbufD(tp)[:, :], in_=rbufD(tp)[:, :], func=AF.Square),
                waits=[(s_rD, rD_done[-1])], inc=s_rD)

            def fb2(e, tp=tp, slot=slot):
                rb = rbufD(tp)
                for (do, i0, i1, op2) in ((0, 0, 1024, AL.add),
                                          (1024, 0, 1024, AL.subtract),
                                          (2048, 2048, 3072, AL.add),
                                          (3072, 2048, 3072, AL.subtract)):
                    e.tensor_tensor(
                        out=fmt[:, 4096 * slot + do:
                                4096 * slot + do + 1024],
                        in0=rb[:, i0:i0 + 1024],
                        in1=rb[:, i1:i1 + 1024], op=op2)
                    e.drain()
                for (do, i0, i1, op2) in ((0, 0, 2048, AL.add),
                                          (1024, 1024, 3072, AL.add),
                                          (2048, 0, 2048, AL.subtract),
                                          (3072, 1024, 3072, AL.subtract)):
                    e.tensor_tensor(
                        out=rb[:, do:do + 1024],
                        in0=fmt[:, 4096 * slot + i0:
                                4096 * slot + i0 + 1024],
                        in1=fmt[:, 4096 * slot + i1:
                                4096 * slot + i1 + 1024], op=op2)
                    e.drain()
                return e.nop()
            bfD_done.append(o("vector", fb2,
                              waits=[(s_rD, sq_v[tp]),
                                     (s_peU, peU_done[-1]),
                                     (s_peD, peD_done[tt])], inc=s_bfD))
            tpb(fmt, 4096 * slot, 128, rbufD(tp), 0, 128, 32,
                [(s_bfD, bfD_done[-1])])
            lhsT_t[tp] = TP.mark()

        def d3b(tp):
            slot = (2 * pr + tp) % 2
            for j1 in range(4):
                bidx = 2 * (j1 % 2)

                def fmm(e, slot=slot, j1=j1, bidx=bidx):
                    last = None
                    for ki in range(8):
                        for og in range(2):
                            last = e.matmul(
                                _bk(ps, bidx + og),
                                fmt[:, 4096 * slot +
                                    128 * (8 * j1 + ki):
                                    4096 * slot +
                                    128 * (8 * j1 + ki) + 128],
                                h_sb[:, 1024 * ki + 512 * og:
                                     1024 * ki + 512 * (og + 1)],
                                start=(ki == 0), stop=(ki == 7))
                    return last
                wl = [lhsT_t[tp]]
                n = len(peH2_done)
                if n >= 2:
                    wl.append((s_zD, zD_done[n - 2]))
                peH2_done.append(o("tensor", fmm, waits=wl, inc=s_peH2,
                                   kind="m"))
                wl = [(s_peH2, peH2_done[-1])]
                if j1 == 0 and len(aq2_done) > 0:
                    wl.append((s_aq2, aq2_done[-1]))
                zD_done.append(o("scalar", lambda e, j1=j1, bidx=bidx:
                                 e.activation(
                                   out=zq[:, 1024 * j1:1024 * (j1 + 1)],
                                   in_=_bk(ps, bidx, 2), func=AF.Copy),
                                 waits=wl, inc=s_zD))
            peH2_t[tp] = peH2_done[-1]
            zD_t[tp] = zD_done[-1]

        def d3c(tp):
            tt = 2 * pr + tp
            slot = tt % 2

            def faq2(e, tt=tt, tp=tp):
                e.reduce_max(out=dnv[:, 0:1], in_=zq[:, :], axis=AX,
                             apply_absolute_value=True)
                e.drain()
                e.tensor_scalar(out=dnv[:, 1:2], in0=dnv[:, 0:1],
                                scalar1=EPS_Z, scalar2=None, op0=AL.max)
                e.drain()
                e.reciprocal(out=dnv[:, 2:3], in_=dnv[:, 1:2])
                e.drain()
                e.tensor_scalar(out=dnv[:, 2:3], in0=dnv[:, 2:3],
                                scalar1=127.0, scalar2=None, op0=AL.mult)
                e.drain()
                e.scalar_tensor_tensor(out=cfv2[:, tt:tt + 1],
                                       in0=dnv[:, 1:2], scalar=C_DN,
                                       in1=sc_bc[:, 1:2], op0=AL.mult,
                                       op1=AL.mult)
                e.drain()
                for c in range(2):
                    e.tensor_scalar(out=qi32[:, 0:2048],
                                    in0=zq[:, 2048 * c:2048 * (c + 1)],
                                    scalar1=dnv[:, 2:3], scalar2=None,
                                    op0=AL.mult)
                    e.drain()
                    e.tensor_scalar(
                        out=rbufD(tp)[:, 2048 * c:2048 * (c + 1)],
                        in0=qi32[:, 0:2048], scalar1=0.0, scalar2=None,
                        op0=AL.add)
                    e.drain()
                return e.drain()
            aq2_done.append(o("vector", faq2,
                              waits=[(s_zD, zD_t[tp])], inc=s_aq2))
            tpb(fmt, 4096 * slot, 128, rbufD(tp), 0, 128, 32,
                [(s_aq2, aq2_done[-1]), (s_peH2, peH2_t[tp])])
            x2T_done.append(TP.mark())

            def fst(e, slot=slot, tt=tt):
                ia = bass.AP(fmt, 4096 * slot,
                             [[8192, P], [128, 32], [1, 128]])
                oa = bass.AP(xq2_d, 128 * tt,
                             [[1024, P], [131072, 32], [1, 128]])
                return e.dma_start(out=oa, in_=ia)
            xqst_done.append(o("sync", fst, waits=x2T_done[-1],
                               inc=s_xqst, k=16, kind="d"))

        d3a(0)
        d3a(1)
        d3b(0)
        d3c(0)
        d3b(1)
        d3c(1)
    lhDn_done, wDn_done, peDn_done, evD_done = [], [], [], []
    for kq in range(2):
        def fld(e, kq=kq):
            buf = big_a if kq % 2 == 0 else big_b
            ia = bass.AP(xq2_d, 2048 * kq * T,
                         [[1024, P], [131072, 16], [1, 1024]])
            oa = bass.AP(buf, 0, [[16384, P], [1024, 16], [1, 1024]])
            return e.dma_start(out=oa, in_=ia)
        wl = [(s_xqst, xqst_done[-1]), (s_peH2, peH2_done[-1]),
              (s_peU, peU_done[-1]), (s_peD, peD_done[-1])]
        wl += x1T_done[-1]
        lhDn_done.append(o("sync", fld, waits=wl, inc=s_lhDn, k=16,
                           kind="d"))
        for og in range(4):
            j = 4 * kq + og
            wl = [(s_peC, PECF), (s_pdn, 1)]
            if j >= 2:
                wl.append((s_peDn, peDn_done[8 * (j - 1) - 1]))
            wDn_done.append(wslice_load(og % 2, qdn, 512 * og, NKD, 512,
                                        s_wDn, wl, row0=2048 * kq,
                                        cast=False))
            for tt in range(NT):
                def fmm(e, og=og, tt=tt,
                        buf=(big_a if kq % 2 == 0 else big_b)):
                    last = None
                    for ki in range(NKD):
                        last = e.matmul(
                            _bk(ps, tt % 2),
                            buf[:, 1024 * ki + 128 * tt:
                                1024 * ki + 128 * (tt + 1)],
                            wbuf[:, 8192 * (og % 2) + 512 * ki:
                                 8192 * (og % 2) + 512 * (ki + 1)],
                            start=(ki == 0), stop=(ki == NKD - 1))
                    return last
                wl = [(s_lhDn, lhDn_done[kq]), (s_wDn, wDn_done[j]),
                      (s_aq2, aq2_done[-1])]
                n = len(peDn_done)
                if n >= 2:
                    wl.append((s_evD, evD_done[n - 2]))
                peDn_done.append(o("tensor", fmm, waits=wl, inc=s_peDn,
                                   kind="m"))
                evD_done.append(o(
                    "vector", lambda e, og=og, tt=tt:
                    e.scalar_tensor_tensor(
                        out=x_sb[:, D * tt + 512 * og:
                                 D * tt + 512 * (og + 1)],
                        in0=_bk(ps, tt % 2), scalar=cfv2[:, tt:tt + 1],
                        in1=x_sb[:, D * tt + 512 * og:
                                 D * tt + 512 * (og + 1)],
                        op0=AL.mult, op1=AL.add),
                    waits=[(s_peDn, peDn_done[-1])], inc=s_evD))
    EVDF = evD_done[-1]
    PEDNF = peDn_done[-1]

    if stop == "D":
        _finish(nc, em, o, st, x_sb, y_out, [(s_evD, EVDF)])
        em.emit()
        return nc

    s_wE = st("s_wE")
    s_peE = st("s_peE")
    s_evE = st("s_evE")
    s_vkE = st("s_vkE")
    s_ccE = st("s_ccE")
    s_wQ = st("s_wQ")
    s_peQ = st("s_peQ")
    s_evQ = st("s_evQ")
    s_peO = st("s_peO")
    s_evO = st("s_evO")
    s_wP = st("s_wP")
    s_peP = st("s_peP")
    s_evP = st("s_evP")
    s_bp = st("s_bp")

    lhsE_done = []
    for tt in range(NT):
        pre = [(s_evD, EVDF)]
        if tt >= 2:
            pre.append(lhsE_done[tt - 2])
        s_l, lv = ln_chain(tt, tt % 2, pre)
        tpb(big_a, 128 * tt, 1024, lnt, D * (tt % 2), 128, NKD,
            [(s_l, lv), (s_lhDn, lhDn_done[-1]), (s_peDn, PEDNF)])
        lhsE_done.append(TP.mark())
    LHE = lhsE_done[-1]

    wE_done, peE_done, evE_done, vkE_done, peO_done = [], [], [], [], []
    s_evEv = st("s_evEv")
    for pr in range(4):
        for p2 in range(2):
            for og in range(4):
                j = (pr * 2 + p2) * 4 + og
                wl = [(s_peDn, PEDNF)]
                if j >= 2:
                    wl.append((s_peE, peE_done[2 * (j - 2) + 1]))
                wE_done.append(wslice_load(
                    og % 2, wkvT, 2048 * p2 + 512 * og, NKD, 512, s_wE,
                    wl))
                for tp in range(2):
                    tt = 2 * pr + tp

                    def fmm(e, og=og, tt=tt):
                        last = None
                        for ki in range(NKD):
                            last = e.matmul(
                                _bk(ps, tt % 2),
                                big_a[:, 1024 * ki + 128 * tt:
                                      1024 * ki + 128 * (tt + 1)],
                                wbuf[:, 8192 * (og % 2) + 512 * ki:
                                     8192 * (og % 2) + 512 * (ki + 1)],
                                start=(ki == 0), stop=(ki == NKD - 1))
                        return last
                    wl = [LHE, (s_wE, wE_done[-1]),
                          (s_evD, EVDF)]
                    n = len(peE_done)
                    if n >= 2:
                        wl.append(evE_done[n - 2])
                    if pr >= 1 and p2 == 0 and og == 0:
                        wl.append((s_vkE, vkE_done[-1]))
                    peE_done.append(o("tensor", fmm, waits=wl, inc=s_peE,
                                      kind="m"))
                    if p2 == 0:
                        evE_done.append((s_evE, o(
                            "scalar", lambda e, tp=tp, og=og, tt=tt:
                            e.activation(
                                out=ktm2[:, 2048 * tp + 512 * og:
                                         2048 * tp + 512 * (og + 1)],
                                in_=_bk(ps, tt % 2), func=AF.Relu),
                            waits=[(s_peE, peE_done[-1])], inc=s_evE)))
                    else:
                        def fv(e, tp=tp, og=og, tt=tt):
                            oa = bass.AP(vaug, 2112 * tp + 528 * og,
                                         [[4224, P], [33, 16], [1, 32]])
                            return e.tensor_copy(out=oa,
                                                 in_=_bk(ps, tt % 2))
                        evE_done.append((s_evEv, o(
                            "vector", fv,
                            waits=[(s_peE, peE_done[-1]), (s_ini, VAUG1)],
                            inc=s_evEv)))
        for tp in range(2):
            for rnd in range(5):
                h0 = 15 * rnd
                nh = min(15, 64 - h0)
                bidx = len(peO_done) % 2

                def fvk(e, tp=tp, h0=h0, nh=nh, bidx=bidx):
                    last = None
                    for hl in range(nh):
                        h = h0 + hl
                        last = e.matmul(
                            ps[0:32, 512 * bidx + 33 * hl:
                               512 * bidx + 33 * hl + 33],
                            ktm2[:, 2048 * tp + 32 * h:
                                 2048 * tp + 32 * (h + 1)],
                            vaug[:, 2112 * tp + 33 * h:
                                 2112 * tp + 33 * (h + 1)],
                            start=(hl == 0), stop=(hl == nh - 1))
                    return last
                wl = [evE_done[-1]]
                n = len(peO_done)
                if n >= 2:
                    wl.append((s_vkE, vkE_done[n - 2]))
                peO_done.append(o("tensor", fvk, waits=wl, inc=s_peO,
                                  kind="m"))
                vkE_done.append(o(
                    "vector", lambda e, h0=h0, nh=nh, bidx=bidx:
                    e.tensor_tensor(
                        out=vkacc[0:32, 33 * h0:33 * (h0 + nh)],
                        in0=vkacc[0:32, 33 * h0:33 * (h0 + nh)],
                        in1=ps[0:32, 512 * bidx:512 * bidx + 33 * nh],
                        op=AL.add),
                    waits=[(s_peO, peO_done[-1]), (s_qv, VK0)],
                    inc=s_vkE))
    VKF = vkE_done[-1]

    s_arE = st("s_arE")
    e1 = o("gpsimd", lambda e: e.dma_start(out=vk_in[:, :],
                                           in_=vkacc[:, :]),
           waits=[(s_vkE, VKF)], inc=s_ccE, k=16, kind="d")
    e2 = o("gpsimd", lambda e: e.collective_compute(
        "AllReduce", AL.add, replica_groups=grp4, ins=[vk_in.ap().opt()],
        outs=[vk_out.ap().opt()]), waits=[(s_ccE, e1)], inc=s_arE,
        kind="d")
    ms = o("gpsimd", lambda e: e.memset(vkT[:, :], 0.0),
           waits=[(s_vkE, VKF), (s_arE, e2)], inc=s_arE)
    for j2 in range(4):
        e3 = o("gpsimd", lambda e, j2=j2: e.dma_start(
            out=vkT[32 * j2:32 * (j2 + 1), :].rearrange(
                "p (c e) -> p c e", e=132)[:, :, 33 * j2:33 * (j2 + 1)],
            in_=bass.AP(vk_out, 33 * j2, [[2112, 32], [132, 16], [1, 33]])),
            waits=[(s_arE, ms)], inc=s_ccE, k=16, kind="d")
    CCEF = s_ccE.n

    wQ_done, peQ_done, evQ_done = [], [], []
    for oc in range(NKD):
        wl = [(s_vkE, VKF)]
        if oc >= 2:
            wl.append((s_peQ, peQ_done[2 * (oc - 2) + 1]))
        wQ_done.append(o("gpsimd", lambda e, oc=oc: e.dma_start(
            out=wt_sb[:, 2048 * (oc % 2):2048 * (oc % 2) + 2048],
            in_=wq1_d[:, oc, :]), waits=wl, inc=s_wQ, k=16, kind="d"))
        for g2 in range(2):
            bidx = 4 + 2 * (oc % 2) + g2

            def fmm(e, oc=oc, g2=g2, bidx=bidx):
                last = None
                for ki in range(NKD):
                    last = e.matmul(
                        _bk(ps, bidx),
                        wt_sb[:, 2048 * (oc % 2) + 128 * ki:
                              2048 * (oc % 2) + 128 * (ki + 1)],
                        big_a[:, 1024 * ki + 512 * g2:
                              1024 * ki + 512 * (g2 + 1)],
                        start=(ki == 0), stop=(ki == NKD - 1))
                return last
            wl = [(s_wQ, wQ_done[oc]), (s_aqD, aq2_done[-1] if False
                                        else s_aqD.n)]
            if oc == 0 and g2 == 0:
                wl.append((s_aq2, aq2_done[-1]))
            if oc >= 2:
                wl.append((s_evQ, evQ_done[2 * (oc - 2) + 1]))
            peQ_done.append(o("tensor", fmm, waits=wl, inc=s_peQ,
                              kind="m"))
            evQ_done.append(o("scalar", lambda e, oc=oc, g2=g2, bidx=bidx:
                              e.activation(
                                out=big_b[:, 1024 * oc + 512 * g2:
                                          1024 * oc + 512 * (g2 + 1)],
                                in_=_bk(ps, bidx), func=AF.Relu),
                              waits=[(s_peQ, peQ_done[-1])], inc=s_evQ))
    EVQF = evQ_done[-1]

    peO2_done, evO_done, lhsP_done = [], [], []
    for tt in range(NT):
        for c in range(NKD):
            i2 = tt * NKD + c
            bidx = i2 % 2

            def fo(e, tt=tt, c=c, bidx=bidx):
                return e.matmul(
                    ps[:, 512 * bidx:512 * bidx + 132],
                    big_b[:, 1024 * c + 128 * tt:1024 * c + 128 * (tt + 1)],
                    vkT[:, 132 * c:132 * (c + 1)],
                    start=True, stop=True)
            wl = [(s_evQ, EVQF), (s_ccE, CCEF), (s_vkE, VKF)]
            if i2 >= 2:
                wl.append((s_evO, evO_done[i2 - 2]))
            peO2_done.append(o("tensor", fo, waits=wl, inc=s_peO,
                               kind="m"))

            def fdiv(e, tt=tt, c=c, bidx=bidx):
                e.tensor_scalar(
                    out=rcd[:, 0:4],
                    in0=bass.AP(ps, 512 * bidx + 32,
                                [[4096, P], [33, 4]]),
                    scalar1=1e-15, scalar2=None, op0=AL.add)
                e.drain()
                e.reciprocal(out=rcd[:, 0:4], in_=rcd[:, 0:4])
                e.drain()
                e.tensor_tensor(
                    out=lnt[:, D * (tt % 2) + 128 * c:
                            D * (tt % 2) + 128 * (c + 1)],
                    in0=bass.AP(ps, 512 * bidx,
                                [[4096, P], [33, 4], [1, 32]]),
                    in1=bass.AP(rcd, 0, [[4, P], [1, 4], [0, 32]]),
                    op=AL.mult)
                e.drain()
                return e.drain()
            wl = [(s_peO, peO2_done[-1])]
            if tt >= 2 and c == 0:
                wl.append(lhsP_done[tt - 2])
            evO_done.append(o("vector", fdiv, waits=wl, inc=s_evO))
        tpb(big_a, 128 * tt, 1024, lnt, D * (tt % 2), 128, NKD,
            [(s_evO, evO_done[-1]), (s_peQ, peQ_done[-1])] + LHE)
        lhsP_done.append(TP.mark())
    LHP = lhsP_done[-1]

    s_bpv = st("s_bpv")
    p1 = o("gpsimd", lambda e: e.dma_start(
        out=bc_sb[:, :], in_=bass.AP(bpj_row, 0, [[0, P], [1, D]])),
        waits=[(s_bov, bo_done[-1]), (s_evC, EVCF)], inc=s_bp, k=16,
        kind="d")
    bp_done = []
    for tt in range(NT):
        bp_done.append(o("vector", lambda e, tt=tt: e.tensor_tensor(
            out=x_sb[:, D * tt:D * (tt + 1)],
            in0=x_sb[:, D * tt:D * (tt + 1)], in1=bc_sb[:, :], op=AL.add),
            waits=[(s_bp, p1), (s_evD, EVDF)] + lhsE_done[tt],
            inc=s_bpv))

    wP_done, peP_done, evP_done = [], [], []
    for og in range(4):
        wl = [LHP]
        if og >= 2:
            wl.append((s_peP, peP_done[8 * (og - 1) - 1]))
        wP_done.append(wslice_load(og % 2, wprojT, 512 * og, NKD, 512,
                                   s_wP, wl))
        for tt in range(NT):
            def fmm(e, og=og, tt=tt):
                last = None
                for ki in range(NKD):
                    last = e.matmul(
                        _bk(ps, tt % 2),
                        big_a[:, 1024 * ki + 128 * tt:
                              1024 * ki + 128 * (tt + 1)],
                        wbuf[:, 8192 * (og % 2) + 512 * ki:
                             8192 * (og % 2) + 512 * (ki + 1)],
                        start=(ki == 0), stop=(ki == NKD - 1))
                return last
            wl = [(s_wP, wP_done[og]), (s_peO, peO2_done[-1])] + LHP
            n = len(peP_done)
            if n >= 2:
                wl.append((s_evP, evP_done[n - 2]))
            peP_done.append(o("tensor", fmm, waits=wl, inc=s_peP,
                              kind="m"))
            evP_done.append(o(
                "vector", lambda e, og=og, tt=tt: e.scalar_tensor_tensor(
                    out=x_sb[:, D * tt + 512 * og:
                             D * tt + 512 * (og + 1)],
                    in0=_bk(ps, tt % 2), scalar=1.0,
                    in1=x_sb[:, D * tt + 512 * og:
                             D * tt + 512 * (og + 1)],
                    op0=AL.mult, op1=AL.add),
                waits=[(s_peP, peP_done[-1]), (s_bpv, bp_done[tt])],
                inc=s_evP))
    EVPF = evP_done[-1]
    PEPF = peP_done[-1]

    if stop == "E":
        _finish(nc, em, o, st, x_sb, y_out, [(s_evP, EVPF)])
        em.emit()
        return nc

    s_wF = st("s_wF")
    s_peF = st("s_peF")
    s_rF = st("s_rF")
    s_qF = st("s_qF")
    s_sqF = st("s_sqF")
    s_khst = st("s_khst")
    s_lhG = st("s_lhG")
    s_wG = st("s_wG")
    s_peG = st("s_peG")
    s_evG = st("s_evG")

    rbufF = big_a
    lnF_done, wF_done, peF_done, rF_done = [], [], [], []
    qF_done, khst_done = [], []

    def emit_lnF(prx):
        for tp in range(2):
            tt = 2 * prx + tp
            pre = [(s_evP, EVPF)]
            if tt >= 2:
                pre.append(lnF_done[tt - 2])
            s_l, lv = ln_chain(tt, tt % 2, pre)
            wl = [(s_l, lv), (s_peP, PEPF)]
            if prx >= 1:
                wl.append((s_peF, peF_done[-2] if tp == 0
                           else peF_done[-1]))
            tpb(big_b, 128 * tp, 256, lnt, D * (tt % 2), 128, NKD, wl)
            lnF_done.append(TP.mark())

    emit_lnF(0)
    for pr in range(4):
        for og in range(16):
            j = 16 * pr + og
            wl = [(s_pcf, 1), (s_peP, PEPF)]
            if j >= 2:
                wl.append((s_peF, peF_done[2 * (j - 2) + 1]))
            wF_done.append(wslice_load(og % 2, qcf, 512 * og, NKD,
                                       512, s_wF, wl, cast=False))
            for tp in range(2):
                tt = 2 * pr + tp

                def fmm(e, og=og, tp=tp):
                    last = None
                    for ki in range(NKD):
                        last = e.matmul(
                            _bk(ps, 2 * (og % 2) + tp),
                            big_b[:, 256 * ki + 128 * tp:
                                  256 * ki + 128 * (tp + 1)],
                            wbuf[:, 8192 * (og % 2) + 512 * ki:
                                 8192 * (og % 2) + 512 * (ki + 1)],
                            start=(ki == 0), stop=(ki == NKD - 1))
                    return last
                wl = [lnF_done[2 * pr + tp], (s_wF, wF_done[-1])]
                n = len(peF_done)
                if n >= 2:
                    wl.append((s_rF, rF_done[n - 2]))
                if pr >= 1 and og == 0:
                    wl.append((s_khst, khst_done[-3] if tp == 0
                               else khst_done[-1]))
                peF_done.append(o("tensor", fmm, waits=wl, inc=s_peF,
                                  kind="m"))
                rF_done.append(o(
                    "scalar", lambda e, tp=tp, og=og: e.activation(
                        out=rbufF[:, 8192 * tp + 512 * og:
                                  8192 * tp + 512 * (og + 1)],
                        in_=_bk(ps, 2 * (og % 2) + tp), func=AF.Relu,
                        scale=sc_bc[:, 2:3]),
                    waits=[(s_peF, peF_done[-1]), (s_c0v, SCL)],
                    inc=s_rF))
        if pr < 3:
            emit_lnF(pr + 1)
        for tp in range(2):
            tt = 2 * pr + tp

            def fq(e, tp=tp, tt=tt):
                e.reduce_max(out=dnv[:, 0:1],
                             in_=rbufF[:, 8192 * tp:8192 * (tp + 1)],
                             axis=AX, apply_absolute_value=True)
                e.drain()
                e.tensor_tensor(out=dnv[:, 0:1], in0=dnv[:, 0:1],
                                in1=dnv[:, 0:1], op=AL.mult)
                e.drain()
                e.tensor_scalar(out=dnv[:, 1:2], in0=dnv[:, 0:1],
                                scalar1=1e-5, scalar2=None, op0=AL.max)
                e.drain()
                e.reciprocal(out=dnv[:, 2:3], in_=dnv[:, 1:2])
                e.drain()
                e.tensor_scalar(out=dnv[:, 2:3], in0=dnv[:, 2:3],
                                scalar1=255.0, scalar2=None, op0=AL.mult)
                e.drain()
                e.scalar_tensor_tensor(out=cfv[:, tt:tt + 1],
                                       in0=dnv[:, 1:2],
                                       scalar=1.0 / 255.0,
                                       in1=sc_bc[:, 3:4], op0=AL.mult,
                                       op1=AL.mult)
                e.drain()
                return e.drain()
            qF_done.append(o("vector", fq, waits=[(s_rF, rF_done[-1])],
                             inc=s_qF))
            base_q = qF_done[-1]
            sq_done = []
            for c in range(8):
                hb = 1024 * (c % 2)
                wl = [(s_qF, base_q)] if c < 2 else [(s_qF, qF_done[-1])]
                sq_done.append(o("scalar", lambda e, tp=tp, c=c, hb=hb:
                                 e.activation(
                                   out=y32[:, hb:hb + 1024],
                                   in_=rbufF[:, 8192 * tp + 1024 * c:
                                             8192 * tp + 1024 * (c + 1)],
                                   func=AF.Square),
                                 waits=wl, inc=s_sqF))

                def fr(e, tp=tp, c=c, hb=hb):
                    e.tensor_scalar(out=qi32[:, hb:hb + 1024],
                                    in0=y32[:, hb:hb + 1024],
                                    scalar1=dnv[:, 2:3], scalar2=None,
                                    op0=AL.mult)
                    e.drain()
                    return e.tensor_scalar(
                        out=rbufF[:, 8192 * tp + 1024 * c:
                                  8192 * tp + 1024 * (c + 1)],
                        in0=qi32[:, hb:hb + 1024], scalar1=0.0,
                        scalar2=None, op0=AL.add)
                qF_done.append(o("vector", fr,
                                 waits=[(s_sqF, sq_done[-1])], inc=s_qF))
            for wv in range(2):
                wl = [(s_qF, qF_done[-1] if wv == 1 else qF_done[-5])]
                if len(khst_done) >= 1:
                    wl.append((s_khst, khst_done[-1]))
                tpb(zq, 0, 128, rbufF, 8192 * tp + 4096 * wv, 128, 32, wl)

                def fst(e, tt=tt, wv=wv):
                    ia = bass.AP(zq, 0, [[4096, P], [128, 32], [1, 128]])
                    oa = bass.AP(khat_d, 4096 * wv * T + 128 * tt,
                                 [[1024, P], [131072, 32], [1, 128]])
                    return e.dma_start(out=oa, in_=ia)
                khst_done.append(o("sync", fst, waits=TP.mark(),
                                   inc=s_khst, k=16, kind="d"))
    lhG_done, wG_done, peG_done, evG_done = [], [], [], []
    for kq in range(4):
        def fld(e, kq=kq):
            buf = big_a if kq % 2 == 0 else big_b
            ia = bass.AP(khat_d, 2048 * kq * T,
                         [[1024, P], [131072, 16], [1, 1024]])
            oa = bass.AP(buf, 0, [[16384, P], [1024, 16], [1, 1024]])
            return e.dma_start(out=oa, in_=ia)
        wl = [(s_khst, khst_done[-1]), (s_peF, peF_done[-1]),
              (s_qF, qF_done[-1]), (s_rF, rF_done[-1])]
        if kq >= 2:
            wl.append((s_peG, peG_done[32 * (kq - 2) + 31]))
        lhG_done.append(o("sync", fld, waits=wl, inc=s_lhG, k=16,
                          kind="d"))
        for og in range(4):
            j = 4 * kq + og
            wl = [(s_peP, PEPF), (s_pcp, 1)]
            if j >= 2:
                wl.append((s_peG, peG_done[8 * (j - 1) - 1]))
            wG_done.append(wslice_load(og % 2, qcp, 512 * og, NKD, 512,
                                       s_wG, wl, row0=2048 * kq,
                                       cast=False))
            for tt in range(NT):
                def fmm(e, og=og, tt=tt,
                        buf=(big_a if kq % 2 == 0 else big_b)):
                    last = None
                    for ki in range(NKD):
                        last = e.matmul(
                            _bk(ps, tt % 2),
                            buf[:, 1024 * ki + 128 * tt:
                                1024 * ki + 128 * (tt + 1)],
                            wbuf[:, 8192 * (og % 2) + 512 * ki:
                                 8192 * (og % 2) + 512 * (ki + 1)],
                            start=(ki == 0), stop=(ki == NKD - 1))
                    return last
                wl = [(s_lhG, lhG_done[kq]), (s_wG, wG_done[j]),
                      (s_evP, EVPF)]
                n = len(peG_done)
                if n >= 2:
                    wl.append((s_evG, evG_done[n - 2]))
                peG_done.append(o("tensor", fmm, waits=wl, inc=s_peG,
                                  kind="m"))
                evG_done.append(o(
                    "vector", lambda e, og=og, tt=tt:
                    e.scalar_tensor_tensor(
                        out=x_sb[:, D * tt + 512 * og:
                                 D * tt + 512 * (og + 1)],
                        in0=_bk(ps, tt % 2), scalar=cfv[:, tt:tt + 1],
                        in1=x_sb[:, D * tt + 512 * og:
                                 D * tt + 512 * (og + 1)],
                        op0=AL.mult, op1=AL.add),
                    waits=[(s_peG, peG_done[-1])], inc=s_evG))

    _finish(nc, em, o, st, x_sb, y_out, [(s_evG, evG_done[-1])])
    em.emit()
    return nc


def _finish(nc, em, o, st, x_sb, y_out, waits, dumps=()):
    s_out = st("s_out")
    vals = []
    for tt in range(NT):
        vals.append(o("sync", lambda e, tt=tt: e.dma_start(
            out=y_out[P * tt:P * (tt + 1), :],
            in_=x_sb[:, D * tt:D * (tt + 1)]), waits=waits, inc=s_out,
            k=16, kind="d"))
    for (name, sb) in dumps:
        dshape = [sb.shape[0], sb.shape[1]]
        dten = nc.dram_tensor("dbg_" + name, dshape, sb.dtype,
                              kind="ExternalOutput")
        vals.append(o("sync", lambda e, dten=dten, sb=sb: e.dma_start(
            out=dten[:, :], in_=sb[:, :]), waits=waits, inc=s_out,
            k=16, kind="d"))
    o("sync", None, waits=[(s_out, vals[-1])])


def _hadamard(n):
    h = np.array([[1.0]], dtype=np.float32)
    while h.shape[0] < n:
        h = np.block([[h, h], [h, -h]])
    return h


def _dev_wstat(wT):
    K, O = wT.shape
    nki, noc = K // P, O // P
    w = wT.reshape(nki, P, noc, P).transpose(1, 2, 0, 3).reshape(
        P, noc, nki * P)
    return np.ascontiguousarray(w)


_NC_CACHE = {}


def kernel(L0_attn_x, L0_attn_Wq, L0_attn_bq, L0_attn_Wk, L0_attn_bk,
           L0_attn_Wv, L0_attn_bv, L0_attn_Wout, L0_attn_bout,
           L0_ffn_w_up, L0_ffn_w_down, L1_attn_w_qkv, L1_attn_w_proj,
           L1_attn_b_proj, L1_ffn_c_fc, L1_ffn_c_proj):
    f32 = np.float32
    x = np.ascontiguousarray(np.asarray(L0_attn_x, f32).reshape(2 * SB, D))
    wqT = np.asarray(L0_attn_Wq, f32).T
    wkT = np.asarray(L0_attn_Wk, f32).T
    wvT = np.asarray(L0_attn_Wv, f32).T
    woT = np.ascontiguousarray(np.asarray(L0_attn_Wout, f32).T)
    wqkvT = np.asarray(L1_attn_w_qkv, f32).T
    wq1T = wqkvT[:, 0:D]
    wkvT = np.ascontiguousarray(wqkvT[:, D:3 * D])
    wprojT = np.ascontiguousarray(np.asarray(L1_attn_w_proj, f32).T)
    w_upT = np.ascontiguousarray(np.asarray(L0_ffn_w_up, f32).T)
    w_dnT = np.ascontiguousarray(np.asarray(L0_ffn_w_down, f32).T)
    cfc = np.ascontiguousarray(np.asarray(L1_ffn_c_fc, f32))
    cpjT = np.ascontiguousarray(np.asarray(L1_ffn_c_proj, f32).T)

    wq_d = _dev_wstat(wqT)
    wk_d = _dev_wstat(wkT)
    wv_d = _dev_wstat(wvT)
    wq1_d = _dev_wstat(np.ascontiguousarray(wq1T))

    def fmb(b):
        return np.ascontiguousarray(np.asarray(b, f32).reshape(NKD, P).T)

    h1024 = _hadamard(1024).astype(ml_dtypes.bfloat16)

    key = DEBUG_STOP
    if key not in _NC_CACHE:
        _NC_CACHE[key] = build_program(stop=DEBUG_STOP)
    nc = _NC_CACHE[key]

    in_maps = []
    for c in range(8):
        in_maps.append({
            "x_in": np.ascontiguousarray(x[T * c:T * (c + 1)]),
            "wq_d": wq_d, "wk_d": wk_d, "wv_d": wv_d, "wq1_d": wq1_d,
            "woT": woT, "wkvT": wkvT, "wprojT": wprojT, "h1024": h1024,
            "bq_in": fmb(L0_attn_bq), "bk_in": fmb(L0_attn_bk),
            "bv_in": fmb(L0_attn_bv),
            "bo_row": np.ascontiguousarray(
                np.asarray(L0_attn_bout, f32)[None, :]),
            "bpj_row": np.ascontiguousarray(
                np.asarray(L1_attn_b_proj, f32)[None, :]),
            "shup": np.ascontiguousarray(w_upT[256 * c:256 * (c + 1)]),
            "shdn": np.ascontiguousarray(w_dnT[512 * c:512 * (c + 1)]),
            "shcf": np.ascontiguousarray(cfc[256 * c:256 * (c + 1)]),
            "shcp": np.ascontiguousarray(cpjT[1024 * c:1024 * (c + 1)]),
        })
    res = run_bass_kernel_spmd(nc, in_maps, core_ids=list(range(8)))
    out = np.concatenate([res.results[c]["y_out"] for c in range(8)],
                         axis=0)
    return out.reshape(2, SB, D)
